# version 06
# baseline (speedup 1.0000x reference)
import numpy as np
import ml_dtypes
from contextlib import ExitStack

import concourse.bass as bass
import concourse.mybir as mybir
from concourse.bass_utils import run_bass_kernel_spmd

F32 = mybir.dt.float32
BF16 = mybir.dt.bfloat16
AF = mybir.ActivationFunctionType
ALU = mybir.AluOpType
AX = mybir.AxisListType

RSQ = float(1.0 / np.sqrt(np.float32(128.0)))
LN_EPS = 1e-5

FULL_GEO = dict(D=2048, T=2048, I=1408, E=8, H=16)


class Sem:
    def __init__(self, ctx, nc, name):
        self.h = ctx.enter_context(nc.semaphore(name))
        self.n = 0

    def base(self):
        return self.n

    def add(self, amt=1):
        self.n += amt
        return self.n


def build_nc(g):
    D, T, I, E, H = g["D"], g["T"], g["I"], g["E"], g["H"]
    ISH = 2 * I
    DT = D // 128
    NT = T // 128
    IT = I // 128
    TC = max(1, T // 512)
    TCW = min(T, 512)
    NEXP = E + 2
    NTG = (NT + 7) // 8
    DC = max(1, D // 512)
    DCW = min(D, 512)
    assert D % 128 == 0 and T % 128 == 0 and I % 128 == 0 and T % TCW == 0 and D % DCW == 0

    nc = bass.Bass("TRN2", target_bir_lowering=False)

    x_in = nc.dram_tensor("x_in", [NT, 128, D], F32, kind="ExternalInput")
    wq = nc.dram_tensor("wq", [H, D, 128], F32, kind="ExternalInput")
    wk = nc.dram_tensor("wk", [H, D, 128], F32, kind="ExternalInput")
    wv = nc.dram_tensor("wv", [H, D, 128], F32, kind="ExternalInput")
    wo = nc.dram_tensor("wo", [H, 128, D], F32, kind="ExternalInput")
    bqT = nc.dram_tensor("bqT", [128, H], F32, kind="ExternalInput")
    bkT = nc.dram_tensor("bkT", [128, H], F32, kind="ExternalInput")
    bvT = nc.dram_tensor("bvT", [128, H], F32, kind="ExternalInput")
    bo = nc.dram_tensor("bo", [D], F32, kind="ExternalInput")
    gate_wT = nc.dram_tensor("gate_wT", [D, E], F32, kind="ExternalInput")
    weg_T = nc.dram_tensor("weg_T", [E, D, I], F32, kind="ExternalInput")
    weu_T = nc.dram_tensor("weu_T", [E, D, I], F32, kind="ExternalInput")
    wed_T = nc.dram_tensor("wed_T", [E, I, D], F32, kind="ExternalInput")
    wsg_T = nc.dram_tensor("wsg_T", [D, ISH], F32, kind="ExternalInput")
    wsu_T = nc.dram_tensor("wsu_T", [D, ISH], F32, kind="ExternalInput")
    wsd_T = nc.dram_tensor("wsd_T", [ISH, D], F32, kind="ExternalInput")
    idm = nc.dram_tensor("idm", [128, 128], BF16, kind="ExternalInput")
    amask = nc.dram_tensor("amask", [128, 256], F32, kind="ExternalInput")
    y_out = nc.dram_tensor("y_out", [NT, 128, D], F32, kind="ExternalOutput")
    DBG = g.get("DBG", 0)
    dbg_outs = [nc.dram_tensor("dbg%d" % i, [NT, 128, D], F32, kind="ExternalOutput")
                for i in range(DBG)]
    if DBG:
        dbg_lnT = nc.dram_tensor("dbg_lnT", [128, NT * D], BF16, kind="ExternalOutput")
        dbg_c = nc.dram_tensor("dbg_c", [128, NT * E], F32, kind="ExternalOutput")
        dbg_h = nc.dram_tensor("dbg_h", [128, IT * T], BF16, kind="ExternalOutput")
        dbg_st = nc.dram_tensor("dbg_st", [128, 32], F32, kind="ExternalOutput")

    def emit_sbuf_dump(dst, src_sb, cols):
        with nc.Block() as blk:
            @blk.sync
            def _(s):
                s.dma_start(bass.AP(dst, 0, [[cols, 128], [1, cols]]),
                            src_sb[:, 0:cols]).then_inc(dS.h, 16)
                s.wait_ge(dS.h, dS.add(16))

    HG = max(1, H // 2)
    NCC = (H + HG - 1) // HG
    KSZ = HG * 128 * T
    kv_send = [nc.dram_tensor("kv_send%d" % i, [2 * KSZ // 512, 512], BF16)
               for i in range(NCC)]
    kv_gath = [nc.dram_tensor("kv_gath%d" % i, [4 * KSZ // 512, 512], BF16)
               for i in range(NCC)]
    q_hbm = nc.dram_tensor("q_hbm", [H, 128, T], BF16)
    z_hbm = nc.dram_tensor("z_hbm", [H, 128, T], BF16)

    ctx = ExitStack()
    nc._kernel_ctx = ctx

    def sbuf(name, shape, dt, scope=None):
        return (scope or ctx).enter_context(nc.sbuf_tensor(name, shape, dt))

    def psum(name, shape, dt, scope=None):
        return (scope or ctx).enter_context(nc.psum_tensor(name, shape, dt))

    x_res = sbuf("x_res", [128, NT * D], BF16)
    id_sb = sbuf("id_sb", [128, 128], BF16)
    st = sbuf("st", [128, 32], F32)
    c_all = sbuf("c_all", [128, NT * E], F32)
    bq_sb = sbuf("bq_sb", [128, H], F32)
    bk_sb = sbuf("bk_sb", [128, H], F32)
    bv_sb = sbuf("bv_sb", [128, H], F32)
    bqs_sb = sbuf("bqs_sb", [128, H], F32)
    gw_sb = sbuf("gw_sb", [128, DT * E], BF16)
    mask_sb = sbuf("mask_sb", [128, 256], F32)

    sems = ExitStack()
    ctx.enter_context(sems)
    dW = Sem(ctx, nc, "dW")
    dS = Sem(ctx, nc, "dS")
    pe = Sem(ctx, nc, "pe")
    sc = Sem(ctx, nc, "sc")
    ve = Sem(ctx, nc, "ve")
    ccS = Sem(ctx, nc, "ccS")

    COL = dict(sum=0, sq=1, mean=2, e2=3, var=4, rstd=5, mr=6,
               negmax=7, m1=8, m2=9, den=10, rinv=11)

    def stc(name):
        c = COL[name]
        return st[:, c:c + 1]

    from concourse.hw_specs import get_activation_tables
    _tabs = list(get_activation_tables(nc.m.arch).keys())

    def preload_set(s, name):
        s.add_instruction(mybir.InstLoadActFuncSet(
            name=nc.get_next_instruction_name(),
            act_func_set_id=_tabs.index(name), ins=[], outs=[]))

    def chunk_widths(total, w):
        out = []
        o = 0
        while o < total:
            out.append(min(w, total - o))
            o += w
        return out

    with nc.Block() as blk:
        @blk.sync
        def _(s):
            s.dma_start(id_sb[:, :], idm[:, :]).then_inc(dS.h, 16); dS.add(16)
            s.dma_start(mask_sb[:, :], amask[:, :]).then_inc(dS.h, 16); dS.add(16)
            s.dma_start(bq_sb[:, :], bqT[:, :]).then_inc(dS.h, 16); dS.add(16)
            s.dma_start(bk_sb[:, :], bkT[:, :]).then_inc(dS.h, 16); dS.add(16)
            s.dma_start(bv_sb[:, :], bvT[:, :]).then_inc(dS.h, 16); dS.add(16)
            s.wait_ge(dS.h, dS.n)

        @blk.gpsimd
        def _(gp):
            gp.dma_start(
                bass.AP(x_res, 0, [[NT * D, 128], [D, NT], [1, D]]),
                bass.AP(x_in, 0, [[D, 128], [128 * D, NT], [1, D]]),
            ).then_inc(dW.h, 16); dW.add(16)
            gp.dma_start(
                bass.AP(gw_sb, 0, [[DT * E, 128], [E, DT], [1, E]]),
                bass.AP(gate_wT, 0, [[E, 128], [128 * E, DT], [1, E]]),
            ).then_inc(dW.h, 16); dW.add(16)
            gp.wait_ge(dW.h, dW.n)

        @blk.vector
        def _(v):
            v.wait_ge(dS.h, dS.n)
            v.tensor_scalar_mul(bqs_sb[:, :], bq_sb[:, :], RSQ)

    def emit_ln(ln_T, tag):
        nbat = (DT + 7) // 8
        b_sc, b_ve, b_pe = sc.base(), ve.base(), pe.base()
        with ExitStack() as lctx:
            dump = sbuf("lndump_" + tag, [128, D], BF16, lctx)
            lntok = sbuf("lntok_" + tag, [128, D], BF16, lctx)
            psb = [psum("lnp%d_%s" % (i, tag), [128, 1024], BF16, lctx) for i in range(2)]
            with nc.Block() as blk:
                @blk.scalar
                def _(s):
                    preload_set(s, "natural_log_exp_and_others")
                    for t in range(NT):
                        if t > 0:
                            s.wait_ge(ve.h, b_ve + 3 * (t - 1) + 1)
                        s.activation(dump[:, :], x_res[:, t * D:(t + 1) * D],
                                     AF.Square, accum_out=stc("sq")).then_inc(sc.h, 1)
                        s.wait_ge(ve.h, b_ve + 3 * t + 1)
                        s.activation(stc("rstd"), stc("var"), AF.Ln)
                        s.drain()
                        s.activation(stc("rstd"), stc("rstd"), AF.Exp,
                                     scale=-0.5).then_inc(sc.h, 1)
                sc.add(2 * NT)

                @blk.vector
                def _(v):
                    for t in range(NT):
                        v.reduce_sum(stc("sum"), x_res[:, t * D:(t + 1) * D], axis=AX.X)
                        v.wait_ge(sc.h, b_sc + 2 * t + 1)
                        v.drain()
                        v.tensor_scalar_mul(stc("mean"), stc("sum"), 1.0 / D)
                        v.tensor_scalar_mul(stc("e2"), stc("sq"), 1.0 / D)
                        v.drain()
                        v.tensor_mul(stc("var"), stc("mean"), stc("mean"))
                        v.drain()
                        v.tensor_sub(stc("var"), stc("e2"), stc("var"))
                        v.drain()
                        v.tensor_scalar_add(stc("var"), stc("var"),
                                            LN_EPS).then_inc(ve.h, 1)
                        v.wait_ge(sc.h, b_sc + 2 * t + 2)
                        v.tensor_mul(stc("mr"), stc("mean"), stc("rstd"))
                        v.drain()
                        if t > 0:
                            v.wait_ge(pe.h, b_pe + nbat * t)
                        v.tensor_scalar(lntok[:, :], x_res[:, t * D:(t + 1) * D],
                                        stc("rstd"), stc("mr"),
                                        op0=ALU.mult, op1=ALU.subtract).then_inc(ve.h, 1)
                        lastc = None
                        for b in range(nbat):
                            lo, hi = b * 8, min(DT, b * 8 + 8)
                            v.wait_ge(pe.h, b_pe + nbat * t + b + 1)
                            lastc = v.tensor_copy(
                                bass.AP(ln_T, lo * T + t * 128,
                                        [[NT * D, 128], [T, hi - lo], [1, 128]]),
                                psb[b % 2][:, 0:(hi - lo) * 128])
                        lastc.then_inc(ve.h, 1)
                ve.add(3 * NT)

                @blk.tensor
                def _(te):
                    for t in range(NT):
                        te.wait_ge(ve.h, b_ve + 3 * t + 2)
                        if t > 0:
                            te.wait_ge(ve.h, b_ve + 3 * (t - 1) + 3)
                        for b in range(nbat):
                            lo, hi = b * 8, min(DT, b * 8 + 8)
                            last = None
                            for d in range(lo, hi):
                                last = te.transpose(
                                    psb[b % 2][:, (d - lo) * 128:(d - lo + 1) * 128],
                                    lntok[:, d * 128:(d + 1) * 128], id_sb[:, :])
                            last.then_inc(pe.h, 1)
                pe.add(nbat * NT)
        assert sc.n == b_sc + 2 * NT and ve.n == b_ve + 3 * NT and pe.n == b_pe + nbat * NT

    def emit_routing(ln_T, tag):
        b_pe, b_sc, b_ve = pe.base(), sc.base(), ve.base()
        with ExitStack() as rctx:
            rp = psum("rp_" + tag, [128, 512], F32, rctx)
            lg = sbuf("lg_" + tag, [128, E], F32, rctx)
            ex = sbuf("ex_" + tag, [128, E], F32, rctx)
            zz = sbuf("zz_" + tag, [128, 1], F32, rctx)
            selb = sbuf("sel_" + tag, [128, E], F32, rctx)
            pmb = sbuf("pm_" + tag, [128, E], F32, rctx)
            with nc.Block() as blk:
                @blk.tensor
                def _(te):
                    for t in range(NT):
                        if t > 0:
                            te.wait_ge(sc.h, b_sc + 2 * (t - 1) + 1)
                        last = None
                        for d in range(DT):
                            last = te.matmul(rp[:, 0:E],
                                             bass.AP(ln_T, d * T + t * 128, [[NT * D, 128], [1, 128]]),
                                             gw_sb[:, d * E:(d + 1) * E],
                                             start=(d == 0), stop=(d == DT - 1))
                        last.then_inc(pe.h, 1)
                pe.add(NT)

                @blk.scalar
                def _(s):
                    for t in range(NT):
                        s.wait_ge(pe.h, b_pe + t + 1)
                        if t > 0:
                            s.wait_ge(ve.h, b_ve + 2 * (t - 1) + 2)
                        s.activation(lg[:, :], rp[:, 0:E], AF.Copy).then_inc(sc.h, 1)
                        s.wait_ge(ve.h, b_ve + 2 * t + 1)
                        s.activation(ex[:, :], lg[:, :], AF.Exp, bias=stc("negmax"),
                                     accum_out=zz[:, :]).then_inc(sc.h, 1)
                sc.add(2 * NT)

                @blk.vector
                def _(v):
                    for t in range(NT):
                        v.wait_ge(sc.h, b_sc + 2 * t + 1)
                        v.reduce_max(stc("negmax"), lg[:, :], axis=AX.X,
                                     negate=True).then_inc(ve.h, 1)
                        v.wait_ge(sc.h, b_sc + 2 * t + 2)
                        v.reduce_max(stc("m1"), ex[:, :], axis=AX.X)
                        v.drain()
                        v.tensor_single_scalar(selb[:, :], ex[:, :], stc("m1"), op=ALU.is_ge)
                        v.drain()
                        v.scalar_tensor_tensor(pmb[:, :], selb[:, :], -1.0e9, ex[:, :],
                                               op0=ALU.mult, op1=ALU.add)
                        v.drain()
                        v.reduce_max(stc("m2"), pmb[:, :], axis=AX.X)
                        v.drain()
                        v.tensor_single_scalar(selb[:, :], ex[:, :], stc("m2"), op=ALU.is_ge)
                        v.tensor_tensor(stc("den"), stc("m1"), stc("m2"), op=ALU.add)
                        v.drain()
                        v.scalar_tensor_tensor(stc("den"), zz[:, :], 1.0e-20, stc("den"),
                                               op0=ALU.mult, op1=ALU.add)
                        v.drain()
                        v.reciprocal(stc("rinv"), stc("den"))
                        v.tensor_mul(c_all[:, t * E:(t + 1) * E], ex[:, :], selb[:, :])
                        v.drain()
                        v.tensor_scalar(c_all[:, t * E:(t + 1) * E], c_all[:, t * E:(t + 1) * E],
                                        stc("rinv"), 0.5, op0=ALU.mult,
                                        op1=ALU.mult).then_inc(ve.h, 1)
                ve.add(2 * NT)
        assert pe.n == b_pe + NT and sc.n == b_sc + 2 * NT and ve.n == b_ve + 2 * NT

    hb_last = [None]

    def emit_ffn(ln_T, tag):
        KT = DT
        mid_k = KT // 2
        mid_i = IT // 2
        tg_sizes = [min(8, NT - 8 * tgi) for tgi in range(NTG)]
        experts = []
        for e in range(E):
            experts.append(dict(kind="exp", e=e, gsrc=weg_T, usrc=weu_T, dsrc=wed_T,
                                goff=e * D * I, doff=e * I * D, istart=0))
        for half in range(2):
            experts.append(dict(kind="sh", e=None, gsrc=wsg_T, usrc=wsu_T, dsrc=wsd_T,
                                goff=half * I, doff=half * I * D, istart=0, half=half))

        b_pe, b_sc, b_ve, b_dW = pe.base(), sc.base(), ve.base(), dW.base()
        PE_GU = 2 * IT
        PE_DN = 2 * NTG * DC
        PE_EX = PE_GU + PE_DN
        DW_GU = 4 * IT * 16
        DW_DN = 2 * NTG * DC * 16
        DW_EX = DW_GU + DW_DN

        def pe_gu(x, i, half):
            return b_pe + x * PE_EX + 2 * i + half + 1

        def pe_dn(x, tgdc, half):
            return b_pe + x * PE_EX + PE_GU + 2 * tgdc + half + 1

        def dw_gu(x, i, q):
            return b_dW + x * DW_EX + (4 * i + q) * 16

        def dw_dn(x, tgdc, nloads):
            return b_dW + x * DW_EX + DW_GU + (2 * tgdc + nloads) * 16

        def sc_silu(x, i):
            return b_sc + x * IT + i

        def ve_h(x, i):
            return b_ve + x * (IT + NTG * DC) + i

        def ve_x(x, tgdc):
            return b_ve + x * (IT + NTG * DC) + IT + tgdc

        with ExitStack() as fctx:
            wgb = sbuf("wg_" + tag, [128, KT * 128], BF16, fctx)
            wub = sbuf("wu_" + tag, [128, KT * 128], BF16, fctx)
            wdb = sbuf("wd_" + tag, [128, IT * DCW], BF16, fctx)
            sgb = sbuf("sg_" + tag, [128, TC * TCW], BF16, fctx)
            hb = sbuf("h_" + tag, [128, IT * T], BF16, fctx)
            hb_last[0] = hb
            ps = [psum("fp%d_%s" % (i, tag), [128, 512], F32, fctx) for i in range(8)]

            with nc.Block() as blk:
                @blk.gpsimd
                def _(gp):
                    for x, ex in enumerate(experts):
                        for i in range(IT):
                            for q in range(4):
                                src = ex["gsrc"] if q < 2 else ex["usrc"]
                                hlf = q % 2
                                klo = hlf * mid_k
                                khi = mid_k if hlf == 0 else KT - mid_k
                                dst = wgb if q < 2 else wub
                                if i > 0:
                                    gp.wait_ge(pe.h, pe_gu(x, i - 1, hlf))
                                elif x > 0:
                                    gp.wait_ge(pe.h, pe_dn(x - 1, NTG * DC - 1, hlf))
                                gp.dma_start(
                                    bass.AP(dst, klo * 128, [[KT * 128, 128], [128, khi], [1, 128]]),
                                    bass.AP(src, ex["goff"] + i * 128 + klo * 128 * (I if ex["kind"] == "exp" else ISH),
                                            [[(I if ex["kind"] == "exp" else ISH), 128],
                                             [128 * (I if ex["kind"] == "exp" else ISH), khi],
                                             [1, 128]]),
                                ).then_inc(dW.h, 16)
                        for tgdc in range(NTG * DC):
                            dc = tgdc % DC
                            for hlf in range(2):
                                ilo = hlf * mid_i
                                ihi = mid_i if hlf == 0 else IT - mid_i
                                if tgdc > 0:
                                    gp.wait_ge(pe.h, pe_dn(x, tgdc - 1, hlf))
                                else:
                                    gp.wait_ge(pe.h, pe_gu(x, IT - 1, hlf))
                                gp.dma_start(
                                    bass.AP(wdb, ilo * DCW, [[IT * DCW, 128], [DCW, ihi], [1, DCW]]),
                                    bass.AP(ex["dsrc"], ex["doff"] + (ilo * 128) * D + dc * DCW,
                                            [[D, 128], [128 * D, ihi], [1, DCW]]),
                                ).then_inc(dW.h, 16)
                    dW.add(len(experts) * DW_EX)

                @blk.tensor
                def _(te):
                    for x, ex in enumerate(experts):
                        for i in range(IT):
                            if i > 0:
                                te.wait_ge(sc.h, sc_silu(x, i - 1) + 1)
                                te.wait_ge(ve.h, ve_h(x, i - 1) + 1)
                            elif x > 0:
                                te.wait_ge(sc.h, sc_silu(x - 1, IT - 1) + 1)
                                te.wait_ge(ve.h, ve_x(x - 1, NTG * DC - 1) + 1)
                            for k in range(KT):
                                if k == 0:
                                    te.wait_ge(dW.h, dw_gu(x, i, 4))
                                lastg = None
                                for tc in range(TC):
                                    lastg = te.matmul(ps[tc][:, 0:TCW],
                                                      wgb[:, k * 128:(k + 1) * 128],
                                                      bass.AP(ln_T, k * T + tc * TCW, [[NT * D, 128], [1, TCW]]),
                                                      start=(k == 0), stop=(k == KT - 1))
                                lastu = None
                                for tc in range(TC):
                                    lastu = te.matmul(ps[TC + tc][:, 0:TCW],
                                                      wub[:, k * 128:(k + 1) * 128],
                                                      bass.AP(ln_T, k * T + tc * TCW, [[NT * D, 128], [1, TCW]]),
                                                      start=(k == 0), stop=(k == KT - 1))
                                if k == mid_k - 1 or k == KT - 1:
                                    lastu.then_inc(pe.h, 1)
                        for tgi in range(NTG):
                            ntt = tg_sizes[tgi]
                            for dc in range(DC):
                                tgdc = tgi * DC + dc
                                if tgdc > 0:
                                    te.wait_ge(ve.h, ve_x(x, tgdc - 1) + 1)
                                else:
                                    te.wait_ge(ve.h, ve_h(x, IT - 1) + 1)
                                for i in range(IT):
                                    if i == 0:
                                        te.wait_ge(dW.h, dw_dn(x, tgdc, 2))
                                    last = None
                                    for tt in range(ntt):
                                        tok = (tgi * 8 + tt) * 128
                                        last = te.matmul(
                                            ps[tt][:, 0:DCW],
                                            bass.AP(hb, i * T + tok, [[IT * T, 128], [1, 128]]),
                                            wdb[:, i * DCW:(i + 1) * DCW],
                                            start=(i == 0), stop=(i == IT - 1))
                                    if i == mid_i - 1 or i == IT - 1:
                                        last.then_inc(pe.h, 1)
                    pe.add(len(experts) * PE_EX)

                @blk.scalar
                def _(s):
                    preload_set(s, "silu_and_others")
                    for x, ex in enumerate(experts):
                        for i in range(IT):
                            s.wait_ge(pe.h, pe_gu(x, i, 1))
                            if i > 0 or x > 0:
                                pi = (x, i - 1) if i > 0 else (x - 1, IT - 1)
                                s.wait_ge(ve.h, ve_h(pi[0], pi[1]) + 1)
                            lasts = None
                            for tc in range(TC):
                                lasts = s.activation(sgb[:, tc * TCW:(tc + 1) * TCW],
                                                     ps[tc][:, 0:TCW], AF.Silu)
                            lasts.then_inc(sc.h, 1)
                    sc.add(len(experts) * IT)

                @blk.vector
                def _(v):
                    for x, ex in enumerate(experts):
                        for i in range(IT):
                            v.wait_ge(sc.h, sc_silu(x, i) + 1)
                            lastm = None
                            for tc in range(TC):
                                lastm = v.tensor_tensor(hb[:, i * T + tc * TCW: i * T + (tc + 1) * TCW],
                                                        sgb[:, tc * TCW:(tc + 1) * TCW],
                                                        ps[TC + tc][:, 0:TCW], op=ALU.mult)
                            lastm.then_inc(ve.h, 1)
                        for tgi in range(NTG):
                            ntt = tg_sizes[tgi]
                            for dc in range(DC):
                                tgdc = tgi * DC + dc
                                v.wait_ge(pe.h, pe_dn(x, tgdc, 1))
                                lastx = None
                                for tt in range(ntt):
                                    ti = tgi * 8 + tt
                                    if ex["kind"] == "exp":
                                        cap = c_all[:, ti * E + ex["e"]: ti * E + ex["e"] + 1]
                                    else:
                                        cap = 0.5
                                    lastx = v.scalar_tensor_tensor(
                                        x_res[:, ti * D + dc * DCW: ti * D + (dc + 1) * DCW],
                                        ps[tt][:, 0:DCW], cap,
                                        x_res[:, ti * D + dc * DCW: ti * D + (dc + 1) * DCW],
                                        op0=ALU.mult, op1=ALU.add)
                                if tgdc == NTG * DC - 1:
                                    v.drain().then_inc(ve.h, 1)
                                else:
                                    lastx.then_inc(ve.h, 1)
                    ve.add(len(experts) * (IT + NTG * DC))
        assert pe.n == b_pe + len(experts) * PE_EX
        assert dW.n == b_dW + len(experts) * DW_EX
        assert sc.n == b_sc + len(experts) * IT
        assert ve.n == b_ve + len(experts) * (IT + NTG * DC)

    def emit_qkv(ln_T, tag):
        b_pe, b_ve, b_dW, b_dS = pe.base(), ve.base(), dW.base(), dS.base()
        nvb = (NT + 7) // 8
        PE_H = 3 + nvb
        VE_H = 3 + 1
        W = [wq, wk, wv]
        BIAS = [bqs_sb, bk_sb, bv_sb]
        SCALE = [RSQ, 1.0, 1.0]

        def pe_j(h, j):
            return b_pe + h * PE_H + j + 1

        def pe_vt(h, b):
            return b_pe + h * PE_H + 3 + b + 1

        def ve_j(h, j):
            return b_ve + VE_H * h + j + 1

        def ve_vt(h):
            return b_ve + VE_H * h + 4

        def dw_j(h, j):
            return b_dW + (3 * h + j + 1) * 16

        with ExitStack() as qctx:
            wjb = [sbuf("wj%d_%s" % (j, tag), [128, 2 * DT * 128], BF16, qctx) for j in range(3)]
            js_sb = [sbuf("js%d_%s" % (j, tag), [128, T], BF16, qctx) for j in range(3)]
            vts = sbuf("vts_" + tag, [128, NT * 128], BF16, qctx)
            psq = [psum("qp%d_%s" % (i, tag), [128, 512], F32, qctx) for i in range(4)]
            psb = [psum("qb%d_%s" % (i, tag), [128, 1024], BF16, qctx) for i in range(2)]

            with nc.Block() as blk:
                @blk.gpsimd
                def _(gp):
                    for h in range(H):
                        for j in range(3):
                            buf = wjb[j]
                            par = h % 2
                            if h >= 1:
                                gp.wait_ge(pe.h, pe_j(h - 1, 0))
                            gp.dma_start(
                                bass.AP(buf, par * DT * 128, [[2 * DT * 128, 128], [128, DT], [1, 128]]),
                                bass.AP(W[j], h * D * 128, [[128, 128], [128 * 128, DT], [1, 128]]),
                            ).then_inc(dW.h, 16)
                    dW.add(3 * H * 16)

                @blk.tensor
                def _(te):
                    for h in range(H):
                        te.wait_ge(dW.h, b_dW + 3 * (h + 1) * 16)
                        for j in range(3):
                            par = h % 2
                            if j > 0:
                                te.wait_ge(ve.h, ve_j(h, j - 1))
                            elif h > 0:
                                te.wait_ge(ve.h, ve_j(h - 1, 2))
                            last = None
                            for k in range(DT):
                                for tc in range(TC):
                                    last = te.matmul(
                                        psq[tc][:, 0:TCW],
                                        wjb[j][:, (par * DT + k) * 128:(par * DT + k + 1) * 128],
                                        bass.AP(ln_T, k * T + tc * TCW, [[NT * D, 128], [1, TCW]]),
                                        start=(k == 0), stop=(k == DT - 1))
                            last.then_inc(pe.h, 1)
                        te.wait_ge(ve.h, ve_j(h, 2))
                        if h > 0:
                            te.wait_ge(ve.h, ve_vt(h - 1))
                        for b in range(nvb):
                            lo, hi = b * 8, min(NT, b * 8 + 8)
                            last = None
                            for tti in range(lo, hi):
                                last = te.transpose(psb[b % 2][:, (tti - lo) * 128:(tti - lo + 1) * 128],
                                                    js_sb[2][:, tti * 128:(tti + 1) * 128],
                                                    id_sb[:, :])
                            last.then_inc(pe.h, 1)
                    pe.add(H * PE_H)

                @blk.vector
                def _(v):
                    for h in range(H):
                        if h > 0:
                            v.wait_ge(dS.h, b_dS + 3 * h * 16)
                        for j in range(3):
                            v.wait_ge(pe.h, pe_j(h, j))
                            lastq = None
                            for tc in range(TC):
                                if SCALE[j] == 1.0:
                                    lastq = v.tensor_scalar_add(js_sb[j][:, tc * TCW:(tc + 1) * TCW],
                                                                psq[tc][:, 0:TCW], BIAS[j][:, h:h + 1])
                                else:
                                    lastq = v.tensor_scalar(js_sb[j][:, tc * TCW:(tc + 1) * TCW],
                                                            psq[tc][:, 0:TCW], SCALE[j],
                                                            BIAS[j][:, h:h + 1],
                                                            op0=ALU.mult, op1=ALU.add)
                            lastq.then_inc(ve.h, 1)
                        lastv = None
                        for b in range(nvb):
                            lo, hi = b * 8, min(NT, b * 8 + 8)
                            v.wait_ge(pe.h, pe_vt(h, b))
                            lastv = v.tensor_copy(vts[:, lo * 128:lo * 128 + (hi - lo) * 128],
                                                  psb[b % 2][:, 0:(hi - lo) * 128])
                        lastv.then_inc(ve.h, 1)
                    ve.add(H * VE_H)

                @blk.sync
                def _(s):
                    for h in range(H):
                        for j in range(3):
                            s.wait_ge(ve.h, ve_j(h, j))
                            if j == 0:
                                s.dma_start(bass.AP(q_hbm, h * 128 * T, [[T, 128], [1, T]]),
                                            js_sb[0][:, :]).then_inc(dS.h, 16)
                            elif j == 1:
                                s.dma_start(bass.AP(kv_send[h // HG], (h % HG) * 128 * T,
                                                    [[T, 128], [1, T]]),
                                            js_sb[1][:, :]).then_inc(dS.h, 16)
                            else:
                                s.wait_ge(ve.h, ve_vt(h))
                                s.dma_start(
                                    bass.AP(kv_send[h // HG], KSZ + (h % HG) * T * 128,
                                            [[128, 128], [128 * 128, NT], [1, 128]]),
                                    bass.AP(vts, 0, [[NT * 128, 128], [128, NT], [1, 128]]),
                                ).then_inc(dS.h, 16)
                    s.wait_ge(dS.h, dS.n + 3 * H * 16)
                    dS.add(3 * H * 16)
        assert pe.n == b_pe + H * PE_H and ve.n == b_ve + H * VE_H

        with nc.Block() as blk:
            @blk.gpsimd
            def _(gp):
                for i in range(NCC):
                    gp.collective_compute(
                        "AllGather", ALU.bypass,
                        ins=[kv_send[i].ap().opt()],
                        outs=[kv_gath[i].ap().opt()],
                        replica_groups=g.get("RG") or [[0, 1], [2, 3], [4, 5], [6, 7]],
                    ).then_inc(ccS.h)
                    ccS.add(1)
                    if i < NCC - 1:
                        gp.wait_ge(ccS.h, ccS.n)

    def emit_attn(tag):
        SEQ_TOT = 2 * T
        assert NT % 2 == 0
        b_pe, b_sc, b_ve, b_dS = pe.base(), sc.base(), ve.base(), dS.base()
        cc_base = ccS.n - NCC
        pos = []
        for i in range(NT):
            ext = 256 * (i + 1)
            cw = chunk_widths(ext, 512)
            ntr = ext // 128
            nb = (ntr + 7) // 8
            pos.append((ext, cw, ntr, nb))

        def prev3(i):
            return NT - 1 - ((NT - 1 - i) % 3)

        pe_i, sc_i, ve_i = {}, {}, {}
        npe = nsc = nve = 0
        for h in range(H):
            for r in range(NT + 2):
                if r < NT:
                    for ci in range(len(pos[r][1])):
                        npe += 1; pe_i[(h, "s", r, ci)] = npe
                j = r - 2
                if 0 <= j < NT:
                    for b in range(pos[j][3]):
                        npe += 1; pe_i[(h, "t", j, b)] = npe
                    npe += 1; pe_i[(h, "av", j)] = npe
            for i in range(NT):
                for ci in range(len(pos[i][1])):
                    nsc += 1; sc_i[(h, "ev", i, ci)] = nsc
                if i >= 2:
                    nsc += 1; sc_i[(h, "z", i - 2)] = nsc
                nsc += 1; sc_i[(h, "exp", i)] = nsc
            nsc += 1; sc_i[(h, "z", NT - 2)] = nsc
            nsc += 1; sc_i[(h, "z", NT - 1)] = nsc
            for i in range(NT):
                if i >= 2:
                    for b in range(pos[i - 2][3]):
                        nve += 1; ve_i[(h, "te", i - 2, b)] = nve
                nve += 1; ve_i[(h, "nm", i)] = nve
                nve += 1; ve_i[(h, "sca", i)] = nve
            for j in (NT - 2, NT - 1):
                for b in range(pos[j][3]):
                    nve += 1; ve_i[(h, "te", j, b)] = nve

        def P(k):
            return b_pe + pe_i[k]

        def S(k):
            return b_sc + sc_i[k]

        def V(k):
            return b_ve + ve_i[k]

        def DSH(h):
            return b_dS + (h + 1) * 6 * 16

        with ExitStack() as actx:
            k_sb = sbuf("k_" + tag, [128, SEQ_TOT], BF16, actx)
            vts = sbuf("vt_" + tag, [128, SEQ_TOT], BF16, actx)
            q_sb = sbuf("q_" + tag, [128, T], BF16, actx)
            rows = [sbuf("rows%d_%s" % (i, tag), [128, SEQ_TOT], F32, actx) for i in range(2)]
            att = [sbuf("att%d_%s" % (i, tag), [128, SEQ_TOT], BF16, actx) for i in range(3)]
            attT = sbuf("attT_" + tag, [128, SEQ_TOT], BF16, actx)
            zall = sbuf("zall_" + tag, [128, T], BF16, actx)
            pss = [psum("ap%d_%s" % (i, tag), [128, 512], F32, actx) for i in range(4)]
            psz = psum("az_" + tag, [128, 512], F32, actx)
            psb = [psum("ab%d_%s" % (i, tag), [128, 1024], BF16, actx) for i in range(2)]
            NMC = [7, 8]
            DENC = [10, 12]
            RIC = [11, 13]

            def stp(cols, i):
                c = cols[i % 2]
                return st[:, c:c + 1]

            def emit_te_evac(v, h, j):
                for b in range(pos[j][3]):
                    lo, hi = b * 8, min(pos[j][2], b * 8 + 8)
                    v.wait_ge(pe.h, P((h, "t", j, b)))
                    v.tensor_copy(attT[:, lo * 128:lo * 128 + (hi - lo) * 128],
                                  psb[b % 2][:, 0:(hi - lo) * 128]).then_inc(ve.h, 1)

            with nc.Block() as blk:
                @blk.sync
                def _(s):
                    for h in range(H):
                        if h == 0:
                            s.wait_ge(ccS.h, cc_base + 1)
                        elif h == HG:
                            s.wait_ge(ccS.h, cc_base + NCC)
                        if h > 0:
                            s.wait_ge(sc.h, S((h - 1, "z", NT - 1)))
                        gb = kv_gath[h // HG]
                        hh = h % HG
                        for r in range(2):
                            s.dma_start(
                                bass.AP(k_sb, r * 128, [[SEQ_TOT, 128], [256, NT], [1, 128]]),
                                bass.AP(gb, r * 2 * KSZ + hh * 128 * T,
                                        [[T, 128], [128, NT], [1, 128]]),
                            ).then_inc(dS.h, 16)
                            s.dma_start(
                                bass.AP(vts, r * 128, [[SEQ_TOT, 128], [256, NT], [1, 128]]),
                                bass.AP(gb, r * 2 * KSZ + KSZ + hh * T * 128,
                                        [[128, 128], [128 * 128, NT], [1, 128]]),
                            ).then_inc(dS.h, 16)
                        s.dma_start(q_sb[:, :],
                                    bass.AP(q_hbm, h * 128 * T, [[T, 128], [1, T]])
                                    ).then_inc(dS.h, 16)
                        s.wait_ge(sc.h, S((h, "z", NT - 1)))
                        s.dma_start(bass.AP(z_hbm, h * 128 * T, [[T, 128], [1, T]]),
                                    zall[:, :]).then_inc(dS.h, 16)
                        s.wait_ge(dS.h, DSH(h))
                    dS.add(H * 6 * 16)

                @blk.tensor
                def _(te):
                    for h in range(H):
                        te.wait_ge(dS.h, DSH(h) - 16)
                        for r in range(NT + 2):
                            if r < NT:
                                ext, cw, ntr, nb = pos[r]
                                base_bank = (r % 2) * 2
                                off = 0
                                for ci, w in enumerate(cw):
                                    if ci >= 2:
                                        te.wait_ge(sc.h, S((h, "ev", r, ci - 2)))
                                    elif r >= 2:
                                        te.wait_ge(sc.h, S((h, "ev", r - 2, len(pos[r - 2][1]) - 1)))
                                    elif h > 0:
                                        te.wait_ge(sc.h, S((h - 1, "ev", NT - 2 + r, len(pos[NT - 2 + r][1]) - 1)))
                                    te.matmul(pss[base_bank + ci % 2][:, 0:w],
                                              q_sb[:, r * 128:(r + 1) * 128],
                                              k_sb[:, off:off + w],
                                              start=True, stop=True).then_inc(pe.h, 1)
                                    off += w
                            j = r - 2
                            if 0 <= j < NT:
                                ext, cw, ntr, nb = pos[j]
                                te.wait_ge(ve.h, V((h, "sca", j)))
                                for b in range(nb):
                                    lo, hi = b * 8, min(ntr, b * 8 + 8)
                                    if b >= 2:
                                        te.wait_ge(ve.h, V((h, "te", j, b - 2)))
                                    elif j > 0:
                                        te.wait_ge(ve.h, V((h, "te", j - 1, pos[j - 1][3] - 1)))
                                    elif h > 0:
                                        te.wait_ge(ve.h, V((h - 1, "te", NT - 1, pos[NT - 1][3] - 1)))
                                    last = None
                                    for kt in range(lo, hi):
                                        last = te.transpose(
                                            psb[b % 2][:, (kt - lo) * 128:(kt - lo + 1) * 128],
                                            att[j % 3][:, kt * 128:(kt + 1) * 128], id_sb[:, :])
                                    last.then_inc(pe.h, 1)
                                te.wait_ge(ve.h, V((h, "te", j, nb - 1)))
                                if j > 0:
                                    te.wait_ge(sc.h, S((h, "z", j - 1)))
                                elif h > 0:
                                    te.wait_ge(sc.h, S((h - 1, "z", NT - 1)))
                                last = None
                                for kt in range(ntr):
                                    last = te.matmul(psz[:, 0:128],
                                                     vts[:, kt * 128:(kt + 1) * 128],
                                                     attT[:, kt * 128:(kt + 1) * 128],
                                                     start=(kt == 0), stop=(kt == ntr - 1))
                                last.then_inc(pe.h, 1)
                    pe.add(npe)

                @blk.scalar
                def _(s):
                    preload_set(s, "natural_log_exp_and_others")
                    for h in range(H):
                        for i in range(NT):
                            ext, cw, ntr, nb = pos[i]
                            base_bank = (i % 2) * 2
                            off = 0
                            for ci, w in enumerate(cw):
                                s.wait_ge(pe.h, P((h, "s", i, ci)))
                                if ci == 0 and i >= 2:
                                    s.wait_ge(ve.h, V((h, "nm", i - 2)))
                                s.activation(rows[i % 2][:, off:off + w],
                                             pss[base_bank + ci % 2][:, 0:w],
                                             AF.Copy).then_inc(sc.h, 1)
                                off += w
                            if i >= 2:
                                s.wait_ge(pe.h, P((h, "av", i - 2)))
                                s.activation(zall[:, (i - 2) * 128:(i - 1) * 128],
                                             psz[:, 0:128], AF.Copy).then_inc(sc.h, 1)
                            elif i == 0 and h > 0:
                                s.wait_ge(dS.h, DSH(h - 1))
                            s.wait_ge(ve.h, V((h, "nm", i)))
                            if i >= 3:
                                s.wait_ge(pe.h, P((h, "t", i - 3, pos[i - 3][3] - 1)))
                            elif h > 0:
                                pj = prev3(i)
                                s.wait_ge(pe.h, P((h - 1, "t", pj, pos[pj][3] - 1)))
                            s.drain()
                            s.activation(att[i % 3][:, 0:ext], rows[i % 2][:, 0:ext], AF.Exp,
                                         bias=stp(NMC, i),
                                         accum_out=stp(DENC, i)).then_inc(sc.h, 1)
                        for j in (NT - 2, NT - 1):
                            s.wait_ge(pe.h, P((h, "av", j)))
                            s.activation(zall[:, j * 128:(j + 1) * 128],
                                         psz[:, 0:128], AF.Copy).then_inc(sc.h, 1)
                    sc.add(nsc)

                @blk.vector
                def _(v):
                    for h in range(H):
                        for i in range(NT):
                            ext, cw, ntr, nb = pos[i]
                            if i >= 2:
                                emit_te_evac(v, h, i - 2)
                            v.wait_ge(sc.h, S((h, "ev", i, len(cw) - 1)))
                            v.tensor_add(rows[i % 2][:, ext - 256:ext],
                                         rows[i % 2][:, ext - 256:ext], mask_sb[:, :])
                            v.drain()
                            v.reduce_max(stp(NMC, i), rows[i % 2][:, 0:ext], axis=AX.X,
                                         negate=True).then_inc(ve.h, 1)
                            v.wait_ge(sc.h, S((h, "exp", i)))
                            v.reciprocal(stp(RIC, i), stp(DENC, i))
                            v.drain()
                            v.tensor_scalar_mul(att[i % 3][:, 0:ext], att[i % 3][:, 0:ext],
                                                stp(RIC, i)).then_inc(ve.h, 1)
                        for j in (NT - 2, NT - 1):
                            emit_te_evac(v, h, j)
                    ve.add(nve)
        assert pe.n == b_pe + npe and sc.n == b_sc + nsc and ve.n == b_ve + nve

    def emit_oproj(tag):
        b_pe, b_sc, b_ve, b_dS, b_dW = pe.base(), sc.base(), ve.base(), dS.base(), dW.base()
        with ExitStack() as octx:
            wob = sbuf("wo_" + tag, [128, H * D], BF16, octx)
            zt = sbuf("zt_" + tag, [128, H * 128], BF16, octx)
            bob = sbuf("bo_" + tag, [128, D], F32, octx)
            tmp = sbuf("otmp_" + tag, [128, 512], F32, octx)
            pso = [psum("op%d_%s" % (i, tag), [128, 512], F32, octx) for i in range(DC if DC <= 4 else 4)]
            nps = len(pso)

            with nc.Block() as blk:
                @blk.gpsimd
                def _(gp):
                    gp.dma_start(
                        bass.AP(wob, 0, [[H * D, 128], [D, H], [1, D]]),
                        bass.AP(wo, 0, [[D, 128], [128 * D, H], [1, D]]),
                    ).then_inc(dW.h, 16)
                    dW.add(16)

                @blk.sync
                def _(s):
                    s.dma_start(bob[:, :], bass.AP(bo, 0, [[0, 128], [1, D]])).then_inc(dS.h, 16)
                    s.wait_ge(dS.h, b_dS + 16)
                    for t in range(NT):
                        if t > 0:
                            s.wait_ge(pe.h, b_pe + t * DC)
                        s.dma_start(
                            bass.AP(zt, 0, [[H * 128, 128], [128, H], [1, 128]]),
                            bass.AP(z_hbm, t * 128, [[T, 128], [128 * T, H], [1, 128]]),
                        ).then_inc(dS.h, 16)
                        s.wait_ge(dS.h, b_dS + (t + 2) * 16)
                    dS.add((NT + 1) * 16)

                @blk.tensor
                def _(te):
                    te.wait_ge(dW.h, b_dW + 16)
                    for t in range(NT):
                        te.wait_ge(dS.h, b_dS + (t + 2) * 16)
                        for dc in range(DC):
                            if t * DC + dc >= nps:
                                te.wait_ge(ve.h, b_ve + t * DC + dc - nps + 1)
                            last = None
                            for h in range(H):
                                last = te.matmul(pso[dc % nps][:, 0:DCW],
                                                 zt[:, h * 128:(h + 1) * 128],
                                                 wob[:, h * D + dc * DCW: h * D + (dc + 1) * DCW],
                                                 start=(h == 0), stop=(h == H - 1))
                            last.then_inc(pe.h, 1)
                    pe.add(NT * DC)

                @blk.vector
                def _(v):
                    v.wait_ge(dS.h, b_dS + 16)
                    for t in range(NT):
                        for dc in range(DC):
                            v.wait_ge(pe.h, b_pe + t * DC + dc + 1)
                            v.tensor_add(tmp[:, 0:DCW], pso[dc % nps][:, 0:DCW],
                                         bob[:, dc * DCW:(dc + 1) * DCW])
                            v.drain()
                            v.tensor_add(x_res[:, t * D + dc * DCW: t * D + (dc + 1) * DCW],
                                         x_res[:, t * D + dc * DCW: t * D + (dc + 1) * DCW],
                                         tmp[:, 0:DCW])
                            v.drain().then_inc(ve.h, 1)
                    ve.add(NT * DC)

    def emit_out():
        b_sc, b_dS = sc.base(), dS.base()
        with ExitStack() as octx:
            of = [sbuf("of%d" % i, [128, D], F32, octx) for i in range(2)]
            with nc.Block() as blk:
                @blk.scalar
                def _(s):
                    for t in range(NT):
                        if t >= 2:
                            s.wait_ge(dS.h, b_dS + (t - 1) * 16)
                        s.activation(of[t % 2][:, :], x_res[:, t * D:(t + 1) * D],
                                     AF.Copy).then_inc(sc.h, 1)
                    sc.add(NT)

                @blk.sync
                def _(s):
                    for t in range(NT):
                        s.wait_ge(sc.h, b_sc + t + 1)
                        s.dma_start(bass.AP(y_out, t * 128 * D, [[D, 128], [1, D]]),
                                    of[t % 2][:, :]).then_inc(dS.h, 16)
                        s.wait_ge(dS.h, b_dS + (t + 1) * 16)
                    dS.add(NT * 16)

    def emit_dump(dst):
        b_sc, b_dS = sc.base(), dS.base()
        with ExitStack() as octx:
            of = [sbuf("dof%d_%s" % (i, dst.name), [128, D], F32, octx) for i in range(2)]
            with nc.Block() as blk:
                @blk.scalar
                def _(s):
                    for t in range(NT):
                        if t >= 2:
                            s.wait_ge(dS.h, b_dS + (t - 1) * 16)
                        s.activation(of[t % 2][:, :], x_res[:, t * D:(t + 1) * D],
                                     AF.Copy).then_inc(sc.h, 1)
                    sc.add(NT)

                @blk.sync
                def _(s):
                    for t in range(NT):
                        s.wait_ge(sc.h, b_sc + t + 1)
                        s.dma_start(bass.AP(dst, t * 128 * D, [[D, 128], [1, D]]),
                                    of[t % 2][:, :]).then_inc(dS.h, 16)
                        s.wait_ge(dS.h, b_dS + (t + 1) * 16)
                    dS.add(NT * 16)

    STOP = g.get("STOP")
    with ExitStack() as inv1:
        ln_T = sbuf("lnT1", [128, NT * D], BF16, inv1)
        emit_ln(ln_T, "l1")
        if DBG:
            emit_sbuf_dump(dbg_lnT, ln_T, NT * D)
            emit_sbuf_dump(dbg_st, st, 32)
        if STOP == "ln1":
            return nc
        emit_routing(ln_T, "r1")
        if DBG:
            emit_sbuf_dump(dbg_c, c_all, NT * E)
        if STOP == "route1":
            return nc
        emit_ffn(ln_T, "f1")
        if DBG:
            emit_sbuf_dump(dbg_h, hb_last[0], IT * T)
    if DBG >= 1:
        emit_dump(dbg_outs[0])
    if STOP == "ffn1":
        return nc
    with ExitStack() as att:
        ln_T = sbuf("lnT2", [128, NT * D], BF16, att)
        emit_ln(ln_T, "l2")
        emit_qkv(ln_T, "qk")
    emit_attn("at")
    emit_oproj("op")
    if DBG >= 2:
        emit_dump(dbg_outs[1])
    with ExitStack() as inv2:
        ln_T = sbuf("lnT3", [128, NT * D], BF16, inv2)
        emit_ln(ln_T, "l3")
        emit_routing(ln_T, "r3")
        emit_ffn(ln_T, "f3")
    emit_out()
    return nc


def make_inputs(core, full, g):
    D, T, I, E, H = g["D"], g["T"], g["I"], g["E"], g["H"]
    NT = T // 128
    b, r = core // 2, core % 2
    xb = np.ascontiguousarray(
        full["L0_attn_x"][b].reshape(2 * NT, 128, D)[r::2])
    mask = np.where(np.arange(256)[None, :] <= 128 * r + np.arange(128)[:, None],
                    np.float32(0), np.float32(-1e9 if False else -100000.0)).astype(np.float32)
    im = dict(
        x_in=xb,
        wq=full["L0_attn_W_Q"], wk=full["L0_attn_W_K"], wv=full["L0_attn_W_V"],
        wo=full["L0_attn_W_O"],
        bqT=np.ascontiguousarray(full["L0_attn_b_Q"].T),
        bkT=np.ascontiguousarray(full["L0_attn_b_K"].T),
        bvT=np.ascontiguousarray(full["L0_attn_b_V"].T),
        bo=full["L0_attn_b_O"],
        gate_wT=np.ascontiguousarray(full["L0_ffn_gate_w"].T),
        weg_T=np.ascontiguousarray(full["L0_ffn_expert_gate"].transpose(0, 2, 1)),
        weu_T=np.ascontiguousarray(full["L0_ffn_expert_up"].transpose(0, 2, 1)),
        wed_T=np.ascontiguousarray(full["L0_ffn_expert_down"].transpose(0, 2, 1)),
        wsg_T=np.ascontiguousarray(full["L0_ffn_shared_gate"].T),
        wsu_T=np.ascontiguousarray(full["L0_ffn_shared_up"].T),
        wsd_T=np.ascontiguousarray(full["L0_ffn_shared_down"].T),
        idm=np.eye(128, dtype=ml_dtypes.bfloat16),
        amask=mask,
    )
    return im


def kernel(**inputs):
    g = FULL_GEO
    D, T = g["D"], g["T"]
    NT = T // 128
    nc = build_nc(g)
    shared = None
    in_maps = []
    for c in range(8):
        im = make_inputs(c, inputs, g)
        in_maps.append(im)
    res = run_bass_kernel_spmd(nc, in_maps, core_ids=list(range(8)))
    out = np.zeros((4, 4096, D), np.float32)
    for c in range(8):
        b, r = c // 2, c % 2
        y = res.results[c]["y_out"]
        out[b].reshape(2 * NT, 128, D)[r::2] = y
    return out


# version 07
# speedup vs baseline: 1.0339x; 1.0339x over previous
import numpy as np
import ml_dtypes
from contextlib import ExitStack

import concourse.bass as bass
import concourse.mybir as mybir
from concourse.bass_utils import run_bass_kernel_spmd

F32 = mybir.dt.float32
BF16 = mybir.dt.bfloat16
AF = mybir.ActivationFunctionType
ALU = mybir.AluOpType
AX = mybir.AxisListType

RSQ = float(1.0 / np.sqrt(np.float32(128.0)))
LN_EPS = 1e-5

FULL_GEO = dict(D=2048, T=2048, I=1408, E=8, H=16)


class Sem:
    def __init__(self, ctx, nc, name):
        self.h = ctx.enter_context(nc.semaphore(name))
        self.n = 0

    def base(self):
        return self.n

    def add(self, amt=1):
        self.n += amt
        return self.n


def build_nc(g):
    D, T, I, E, H = g["D"], g["T"], g["I"], g["E"], g["H"]
    ISH = 2 * I
    DT = D // 128
    NT = T // 128
    IT = I // 128
    TC = max(1, T // 512)
    TCW = min(T, 512)
    NEXP = E + 2
    NTG = (NT + 7) // 8
    DC = max(1, D // 512)
    DCW = min(D, 512)
    assert D % 128 == 0 and T % 128 == 0 and I % 128 == 0 and T % TCW == 0 and D % DCW == 0

    nc = bass.Bass("TRN2", target_bir_lowering=False)

    x_in = nc.dram_tensor("x_in", [NT, 128, D], F32, kind="ExternalInput")
    wq = nc.dram_tensor("wq", [H, 128, D], F32, kind="ExternalInput")
    wk = nc.dram_tensor("wk", [H, 128, D], F32, kind="ExternalInput")
    wv = nc.dram_tensor("wv", [H, 128, D], F32, kind="ExternalInput")
    wo = nc.dram_tensor("wo", [H, 128, D], F32, kind="ExternalInput")
    bqT = nc.dram_tensor("bqT", [128, H], F32, kind="ExternalInput")
    bkT = nc.dram_tensor("bkT", [128, H], F32, kind="ExternalInput")
    bvT = nc.dram_tensor("bvT", [128, H], F32, kind="ExternalInput")
    bo = nc.dram_tensor("bo", [D], F32, kind="ExternalInput")
    gate_wT = nc.dram_tensor("gate_wT", [D, E], F32, kind="ExternalInput")
    wg5 = nc.dram_tensor("wg5", [E + 2, IT, 128, DT * 128], F32, kind="ExternalInput")
    wu5 = nc.dram_tensor("wu5", [E + 2, IT, 128, DT * 128], F32, kind="ExternalInput")
    wed_T = nc.dram_tensor("wed_T", [E, I, D], F32, kind="ExternalInput")
    wsd_T = nc.dram_tensor("wsd_T", [ISH, D], F32, kind="ExternalInput")
    idm = nc.dram_tensor("idm", [128, 128], BF16, kind="ExternalInput")
    amask = nc.dram_tensor("amask", [128, 256], F32, kind="ExternalInput")
    y_out = nc.dram_tensor("y_out", [NT, 128, D], F32, kind="ExternalOutput")
    DBG = g.get("DBG", 0)
    dbg_outs = [nc.dram_tensor("dbg%d" % i, [NT, 128, D], F32, kind="ExternalOutput")
                for i in range(DBG)]
    if DBG:
        dbg_lnT = nc.dram_tensor("dbg_lnT", [128, NT * D], BF16, kind="ExternalOutput")
        dbg_c = nc.dram_tensor("dbg_c", [128, NT * E], F32, kind="ExternalOutput")
        dbg_h = nc.dram_tensor("dbg_h", [128, IT * T], BF16, kind="ExternalOutput")
        dbg_st = nc.dram_tensor("dbg_st", [128, 32], F32, kind="ExternalOutput")

    def emit_sbuf_dump(dst, src_sb, cols):
        with nc.Block() as blk:
            @blk.sync
            def _(s):
                s.dma_start(bass.AP(dst, 0, [[cols, 128], [1, cols]]),
                            src_sb[:, 0:cols]).then_inc(dS.h, 16)
                s.wait_ge(dS.h, dS.add(16))

    HG = max(1, H // 2)
    NCC = (H + HG - 1) // HG
    KSZ = HG * 128 * T
    kv_send = [nc.dram_tensor("kv_send%d" % i, [2 * KSZ // 512, 512], BF16)
               for i in range(NCC)]
    kv_gath = [nc.dram_tensor("kv_gath%d" % i, [4 * KSZ // 512, 512], BF16)
               for i in range(NCC)]
    q_hbm = nc.dram_tensor("q_hbm", [H, 128, T], BF16)
    z_hbm = nc.dram_tensor("z_hbm", [H, 128, T], BF16)

    ctx = ExitStack()
    nc._kernel_ctx = ctx

    def sbuf(name, shape, dt, scope=None):
        return (scope or ctx).enter_context(nc.sbuf_tensor(name, shape, dt))

    def psum(name, shape, dt, scope=None):
        return (scope or ctx).enter_context(nc.psum_tensor(name, shape, dt))

    x_res = sbuf("x_res", [128, NT * D], BF16)
    id_sb = sbuf("id_sb", [128, 128], BF16)
    st = sbuf("st", [128, 32], F32)
    c_all = sbuf("c_all", [128, NT * E], F32)
    bq_sb = sbuf("bq_sb", [128, H], F32)
    bk_sb = sbuf("bk_sb", [128, H], F32)
    bv_sb = sbuf("bv_sb", [128, H], F32)
    bqs_sb = sbuf("bqs_sb", [128, H], F32)
    gw_sb = sbuf("gw_sb", [128, DT * E], BF16)
    mask_sb = sbuf("mask_sb", [128, 256], F32)

    sems = ExitStack()
    ctx.enter_context(sems)
    dW = Sem(ctx, nc, "dW")
    dS = Sem(ctx, nc, "dS")
    pe = Sem(ctx, nc, "pe")
    sc = Sem(ctx, nc, "sc")
    ve = Sem(ctx, nc, "ve")
    ccS = Sem(ctx, nc, "ccS")

    COL = dict(sum=0, sq=1, mean=2, e2=3, var=4, rstd=5, mr=6,
               negmax=7, m1=8, m2=9, den=10, rinv=11)

    def stc(name):
        c = COL[name]
        return st[:, c:c + 1]

    from concourse.hw_specs import get_activation_tables
    _tabs = list(get_activation_tables(nc.m.arch).keys())

    def preload_set(s, name):
        s.add_instruction(mybir.InstLoadActFuncSet(
            name=nc.get_next_instruction_name(),
            act_func_set_id=_tabs.index(name), ins=[], outs=[]))

    def chunk_widths(total, w):
        out = []
        o = 0
        while o < total:
            out.append(min(w, total - o))
            o += w
        return out

    with nc.Block() as blk:
        @blk.sync
        def _(s):
            s.dma_start(id_sb[:, :], idm[:, :]).then_inc(dS.h, 16); dS.add(16)
            s.dma_start(mask_sb[:, :], amask[:, :]).then_inc(dS.h, 16); dS.add(16)
            s.dma_start(bq_sb[:, :], bqT[:, :]).then_inc(dS.h, 16); dS.add(16)
            s.dma_start(bk_sb[:, :], bkT[:, :]).then_inc(dS.h, 16); dS.add(16)
            s.dma_start(bv_sb[:, :], bvT[:, :]).then_inc(dS.h, 16); dS.add(16)
            s.wait_ge(dS.h, dS.n)

        @blk.gpsimd
        def _(gp):
            gp.dma_start(
                bass.AP(x_res, 0, [[NT * D, 128], [D, NT], [1, D]]),
                bass.AP(x_in, 0, [[D, 128], [128 * D, NT], [1, D]]),
            ).then_inc(dW.h, 16); dW.add(16)
            gp.dma_start(
                bass.AP(gw_sb, 0, [[DT * E, 128], [E, DT], [1, E]]),
                bass.AP(gate_wT, 0, [[E, 128], [128 * E, DT], [1, E]]),
            ).then_inc(dW.h, 16); dW.add(16)
            gp.wait_ge(dW.h, dW.n)

        @blk.vector
        def _(v):
            v.wait_ge(dS.h, dS.n)
            v.tensor_scalar_mul(bqs_sb[:, :], bq_sb[:, :], RSQ)

    def emit_ln(ln_T, tag):
        nbat = (DT + 7) // 8
        b_sc, b_ve, b_pe = sc.base(), ve.base(), pe.base()
        with ExitStack() as lctx:
            dump = sbuf("lndump_" + tag, [128, D], BF16, lctx)
            lntok = sbuf("lntok_" + tag, [128, D], BF16, lctx)
            psb = [psum("lnp%d_%s" % (i, tag), [128, 1024], BF16, lctx) for i in range(2)]
            with nc.Block() as blk:
                @blk.scalar
                def _(s):
                    preload_set(s, "natural_log_exp_and_others")
                    for t in range(NT):
                        if t > 0:
                            s.wait_ge(ve.h, b_ve + 3 * (t - 1) + 1)
                        s.activation(dump[:, :], x_res[:, t * D:(t + 1) * D],
                                     AF.Square, accum_out=stc("sq")).then_inc(sc.h, 1)
                        s.wait_ge(ve.h, b_ve + 3 * t + 1)
                        s.activation(stc("rstd"), stc("var"), AF.Ln)
                        s.drain()
                        s.activation(stc("rstd"), stc("rstd"), AF.Exp,
                                     scale=-0.5).then_inc(sc.h, 1)
                sc.add(2 * NT)

                @blk.vector
                def _(v):
                    for t in range(NT):
                        v.reduce_sum(stc("sum"), x_res[:, t * D:(t + 1) * D], axis=AX.X)
                        v.wait_ge(sc.h, b_sc + 2 * t + 1)
                        v.drain()
                        v.tensor_scalar_mul(stc("mean"), stc("sum"), 1.0 / D)
                        v.tensor_scalar_mul(stc("e2"), stc("sq"), 1.0 / D)
                        v.drain()
                        v.tensor_mul(stc("var"), stc("mean"), stc("mean"))
                        v.drain()
                        v.tensor_sub(stc("var"), stc("e2"), stc("var"))
                        v.drain()
                        v.tensor_scalar_add(stc("var"), stc("var"),
                                            LN_EPS).then_inc(ve.h, 1)
                        v.wait_ge(sc.h, b_sc + 2 * t + 2)
                        v.tensor_mul(stc("mr"), stc("mean"), stc("rstd"))
                        v.drain()
                        if t > 0:
                            v.wait_ge(pe.h, b_pe + nbat * t)
                        v.tensor_scalar(lntok[:, :], x_res[:, t * D:(t + 1) * D],
                                        stc("rstd"), stc("mr"),
                                        op0=ALU.mult, op1=ALU.subtract).then_inc(ve.h, 1)
                        lastc = None
                        for b in range(nbat):
                            lo, hi = b * 8, min(DT, b * 8 + 8)
                            v.wait_ge(pe.h, b_pe + nbat * t + b + 1)
                            lastc = v.tensor_copy(
                                bass.AP(ln_T, lo * T + t * 128,
                                        [[NT * D, 128], [T, hi - lo], [1, 128]]),
                                psb[b % 2][:, 0:(hi - lo) * 128])
                        lastc.then_inc(ve.h, 1)
                ve.add(3 * NT)

                @blk.tensor
                def _(te):
                    for t in range(NT):
                        te.wait_ge(ve.h, b_ve + 3 * t + 2)
                        if t > 0:
                            te.wait_ge(ve.h, b_ve + 3 * (t - 1) + 3)
                        for b in range(nbat):
                            lo, hi = b * 8, min(DT, b * 8 + 8)
                            last = None
                            for d in range(lo, hi):
                                last = te.transpose(
                                    psb[b % 2][:, (d - lo) * 128:(d - lo + 1) * 128],
                                    lntok[:, d * 128:(d + 1) * 128], id_sb[:, :])
                            last.then_inc(pe.h, 1)
                pe.add(nbat * NT)
        assert sc.n == b_sc + 2 * NT and ve.n == b_ve + 3 * NT and pe.n == b_pe + nbat * NT

    def emit_routing(ln_T, tag):
        b_pe, b_sc, b_ve = pe.base(), sc.base(), ve.base()
        with ExitStack() as rctx:
            rp = psum("rp_" + tag, [128, 512], F32, rctx)
            lg = sbuf("lg_" + tag, [128, E], F32, rctx)
            ex = sbuf("ex_" + tag, [128, E], F32, rctx)
            zz = sbuf("zz_" + tag, [128, 1], F32, rctx)
            selb = sbuf("sel_" + tag, [128, E], F32, rctx)
            pmb = sbuf("pm_" + tag, [128, E], F32, rctx)
            with nc.Block() as blk:
                @blk.tensor
                def _(te):
                    for t in range(NT):
                        if t > 0:
                            te.wait_ge(sc.h, b_sc + 2 * (t - 1) + 1)
                        last = None
                        for d in range(DT):
                            last = te.matmul(rp[:, 0:E],
                                             bass.AP(ln_T, d * T + t * 128, [[NT * D, 128], [1, 128]]),
                                             gw_sb[:, d * E:(d + 1) * E],
                                             start=(d == 0), stop=(d == DT - 1))
                        last.then_inc(pe.h, 1)
                pe.add(NT)

                @blk.scalar
                def _(s):
                    for t in range(NT):
                        s.wait_ge(pe.h, b_pe + t + 1)
                        if t > 0:
                            s.wait_ge(ve.h, b_ve + 2 * (t - 1) + 2)
                        s.activation(lg[:, :], rp[:, 0:E], AF.Copy).then_inc(sc.h, 1)
                        s.wait_ge(ve.h, b_ve + 2 * t + 1)
                        s.activation(ex[:, :], lg[:, :], AF.Exp, bias=stc("negmax"),
                                     accum_out=zz[:, :]).then_inc(sc.h, 1)
                sc.add(2 * NT)

                @blk.vector
                def _(v):
                    for t in range(NT):
                        v.wait_ge(sc.h, b_sc + 2 * t + 1)
                        v.reduce_max(stc("negmax"), lg[:, :], axis=AX.X,
                                     negate=True).then_inc(ve.h, 1)
                        v.wait_ge(sc.h, b_sc + 2 * t + 2)
                        v.reduce_max(stc("m1"), ex[:, :], axis=AX.X)
                        v.drain()
                        v.tensor_single_scalar(selb[:, :], ex[:, :], stc("m1"), op=ALU.is_ge)
                        v.drain()
                        v.scalar_tensor_tensor(pmb[:, :], selb[:, :], -1.0e9, ex[:, :],
                                               op0=ALU.mult, op1=ALU.add)
                        v.drain()
                        v.reduce_max(stc("m2"), pmb[:, :], axis=AX.X)
                        v.drain()
                        v.tensor_single_scalar(selb[:, :], ex[:, :], stc("m2"), op=ALU.is_ge)
                        v.tensor_tensor(stc("den"), stc("m1"), stc("m2"), op=ALU.add)
                        v.drain()
                        v.scalar_tensor_tensor(stc("den"), zz[:, :], 1.0e-20, stc("den"),
                                               op0=ALU.mult, op1=ALU.add)
                        v.drain()
                        v.reciprocal(stc("rinv"), stc("den"))
                        v.tensor_mul(c_all[:, t * E:(t + 1) * E], ex[:, :], selb[:, :])
                        v.drain()
                        v.tensor_scalar(c_all[:, t * E:(t + 1) * E], c_all[:, t * E:(t + 1) * E],
                                        stc("rinv"), 0.5, op0=ALU.mult,
                                        op1=ALU.mult).then_inc(ve.h, 1)
                ve.add(2 * NT)
        assert pe.n == b_pe + NT and sc.n == b_sc + 2 * NT and ve.n == b_ve + 2 * NT

    hb_last = [None]

    def emit_ffn(ln_T, tag):
        KT = DT
        mid_k = KT // 2
        mid_i = IT // 2
        tg_sizes = [min(8, NT - 8 * tgi) for tgi in range(NTG)]
        experts = []
        for e in range(E):
            experts.append(dict(kind="exp", e=e, dsrc=wed_T, doff=e * I * D))
        for half in range(2):
            experts.append(dict(kind="sh", e=None, dsrc=wsd_T, doff=half * I * D))

        b_pe, b_sc, b_ve, b_dW = pe.base(), sc.base(), ve.base(), dW.base()
        PE_GU = 2 * IT
        PE_DN = 2 * NTG * DC
        PE_EX = PE_GU + PE_DN
        DW_GU = 4 * IT * 16
        DW_DN = 2 * NTG * DC * 16
        DW_EX = DW_GU + DW_DN

        def pe_gu(x, i, half):
            return b_pe + x * PE_EX + 2 * i + half + 1

        def pe_dn(x, tgdc, half):
            return b_pe + x * PE_EX + PE_GU + 2 * tgdc + half + 1

        def dw_gu(x, i, q):
            return b_dW + x * DW_EX + (4 * i + q) * 16

        def dw_dn(x, tgdc, nloads):
            return b_dW + x * DW_EX + DW_GU + (2 * tgdc + nloads) * 16

        def sc_silu(x, i):
            return b_sc + x * IT + i

        def ve_h(x, i):
            return b_ve + x * (IT + NTG * DC) + i

        def ve_x(x, tgdc):
            return b_ve + x * (IT + NTG * DC) + IT + tgdc

        with ExitStack() as fctx:
            wgb = sbuf("wg_" + tag, [128, KT * 128], BF16, fctx)
            wub = sbuf("wu_" + tag, [128, KT * 128], BF16, fctx)
            wdb = sbuf("wd_" + tag, [128, IT * DCW], BF16, fctx)
            sgb = sbuf("sg_" + tag, [128, TC * TCW], BF16, fctx)
            hb = sbuf("h_" + tag, [128, IT * T], BF16, fctx)
            hb_last[0] = hb
            ps = [psum("fp%d_%s" % (i, tag), [128, 512], F32, fctx) for i in range(8)]

            with nc.Block() as blk:
                @blk.gpsimd
                def _(gp):
                    for x, ex in enumerate(experts):
                        for i in range(IT):
                            for q in range(4):
                                srcw = wg5 if q < 2 else wu5
                                hlf = q % 2
                                klo = hlf * mid_k
                                khi = mid_k if hlf == 0 else KT - mid_k
                                dst = wgb if q < 2 else wub
                                if i > 0:
                                    gp.wait_ge(pe.h, pe_gu(x, i - 1, hlf))
                                elif x > 0:
                                    gp.wait_ge(pe.h, pe_dn(x - 1, NTG * DC - 1, hlf))
                                gp.dma_start(
                                    bass.AP(dst, klo * 128, [[KT * 128, 128], [1, khi * 128]]),
                                    bass.AP(srcw, ((x * IT + i) * 128) * (KT * 128) + klo * 128,
                                            [[KT * 128, 128], [1, khi * 128]]),
                                ).then_inc(dW.h, 16)
                        for tgdc in range(NTG * DC):
                            dc = tgdc % DC
                            for hlf in range(2):
                                ilo = hlf * mid_i
                                ihi = mid_i if hlf == 0 else IT - mid_i
                                if tgdc > 0:
                                    gp.wait_ge(pe.h, pe_dn(x, tgdc - 1, hlf))
                                else:
                                    gp.wait_ge(pe.h, pe_gu(x, IT - 1, hlf))
                                gp.dma_start(
                                    bass.AP(wdb, ilo * DCW, [[IT * DCW, 128], [DCW, ihi], [1, DCW]]),
                                    bass.AP(ex["dsrc"], ex["doff"] + (ilo * 128) * D + dc * DCW,
                                            [[D, 128], [128 * D, ihi], [1, DCW]]),
                                ).then_inc(dW.h, 16)
                    dW.add(len(experts) * DW_EX)

                @blk.tensor
                def _(te):
                    for x, ex in enumerate(experts):
                        for i in range(IT):
                            if i > 0:
                                te.wait_ge(sc.h, sc_silu(x, i - 1) + 1)
                                te.wait_ge(ve.h, ve_h(x, i - 1) + 1)
                            elif x > 0:
                                te.wait_ge(sc.h, sc_silu(x - 1, IT - 1) + 1)
                                te.wait_ge(ve.h, ve_x(x - 1, NTG * DC - 1) + 1)
                            for k in range(KT):
                                if k == 0:
                                    te.wait_ge(dW.h, dw_gu(x, i, 4))
                                lastg = None
                                for tc in range(TC):
                                    lastg = te.matmul(ps[tc][:, 0:TCW],
                                                      wgb[:, k * 128:(k + 1) * 128],
                                                      bass.AP(ln_T, k * T + tc * TCW, [[NT * D, 128], [1, TCW]]),
                                                      start=(k == 0), stop=(k == KT - 1))
                                lastu = None
                                for tc in range(TC):
                                    lastu = te.matmul(ps[TC + tc][:, 0:TCW],
                                                      wub[:, k * 128:(k + 1) * 128],
                                                      bass.AP(ln_T, k * T + tc * TCW, [[NT * D, 128], [1, TCW]]),
                                                      start=(k == 0), stop=(k == KT - 1))
                                if k == mid_k - 1 or k == KT - 1:
                                    lastu.then_inc(pe.h, 1)
                        for tgi in range(NTG):
                            ntt = tg_sizes[tgi]
                            for dc in range(DC):
                                tgdc = tgi * DC + dc
                                if tgdc > 0:
                                    te.wait_ge(ve.h, ve_x(x, tgdc - 1) + 1)
                                else:
                                    te.wait_ge(ve.h, ve_h(x, IT - 1) + 1)
                                for i in range(IT):
                                    if i == 0:
                                        te.wait_ge(dW.h, dw_dn(x, tgdc, 2))
                                    last = None
                                    for tt in range(ntt):
                                        tok = (tgi * 8 + tt) * 128
                                        last = te.matmul(
                                            ps[tt][:, 0:DCW],
                                            bass.AP(hb, i * T + tok, [[IT * T, 128], [1, 128]]),
                                            wdb[:, i * DCW:(i + 1) * DCW],
                                            start=(i == 0), stop=(i == IT - 1))
                                    if i == mid_i - 1 or i == IT - 1:
                                        last.then_inc(pe.h, 1)
                    pe.add(len(experts) * PE_EX)

                @blk.scalar
                def _(s):
                    preload_set(s, "silu_and_others")
                    for x, ex in enumerate(experts):
                        for i in range(IT):
                            s.wait_ge(pe.h, pe_gu(x, i, 1))
                            if i > 0 or x > 0:
                                pi = (x, i - 1) if i > 0 else (x - 1, IT - 1)
                                s.wait_ge(ve.h, ve_h(pi[0], pi[1]) + 1)
                            lasts = None
                            for tc in range(TC):
                                lasts = s.activation(sgb[:, tc * TCW:(tc + 1) * TCW],
                                                     ps[tc][:, 0:TCW], AF.Silu)
                            lasts.then_inc(sc.h, 1)
                    sc.add(len(experts) * IT)

                @blk.vector
                def _(v):
                    for x, ex in enumerate(experts):
                        for i in range(IT):
                            v.wait_ge(sc.h, sc_silu(x, i) + 1)
                            lastm = None
                            for tc in range(TC):
                                lastm = v.tensor_tensor(hb[:, i * T + tc * TCW: i * T + (tc + 1) * TCW],
                                                        sgb[:, tc * TCW:(tc + 1) * TCW],
                                                        ps[TC + tc][:, 0:TCW], op=ALU.mult)
                            lastm.then_inc(ve.h, 1)
                        for tgi in range(NTG):
                            ntt = tg_sizes[tgi]
                            for dc in range(DC):
                                tgdc = tgi * DC + dc
                                v.wait_ge(pe.h, pe_dn(x, tgdc, 1))
                                lastx = None
                                for tt in range(ntt):
                                    ti = tgi * 8 + tt
                                    if ex["kind"] == "exp":
                                        cap = c_all[:, ti * E + ex["e"]: ti * E + ex["e"] + 1]
                                    else:
                                        cap = 0.5
                                    lastx = v.scalar_tensor_tensor(
                                        x_res[:, ti * D + dc * DCW: ti * D + (dc + 1) * DCW],
                                        ps[tt][:, 0:DCW], cap,
                                        x_res[:, ti * D + dc * DCW: ti * D + (dc + 1) * DCW],
                                        op0=ALU.mult, op1=ALU.add)
                                if tgdc == NTG * DC - 1:
                                    v.drain().then_inc(ve.h, 1)
                                else:
                                    lastx.then_inc(ve.h, 1)
                    ve.add(len(experts) * (IT + NTG * DC))
        assert pe.n == b_pe + len(experts) * PE_EX
        assert dW.n == b_dW + len(experts) * DW_EX
        assert sc.n == b_sc + len(experts) * IT
        assert ve.n == b_ve + len(experts) * (IT + NTG * DC)

    def emit_qkv(ln_T, tag):
        b_pe, b_ve, b_dW, b_dS = pe.base(), ve.base(), dW.base(), dS.base()
        nvb = (NT + 7) // 8
        PE_H = 3 + nvb
        VE_H = 3 + 1
        W = [wq, wk, wv]
        BIAS = [bqs_sb, bk_sb, bv_sb]
        SCALE = [RSQ, 1.0, 1.0]

        def pe_j(h, j):
            return b_pe + h * PE_H + j + 1

        def pe_vt(h, b):
            return b_pe + h * PE_H + 3 + b + 1

        def ve_j(h, j):
            return b_ve + VE_H * h + j + 1

        def ve_vt(h):
            return b_ve + VE_H * h + 4

        def dw_j(h, j):
            return b_dW + (3 * h + j + 1) * 16

        with ExitStack() as qctx:
            wjb = [sbuf("wj%d_%s" % (j, tag), [128, 2 * DT * 128], BF16, qctx) for j in range(3)]
            js_sb = [sbuf("js%d_%s" % (j, tag), [128, T], BF16, qctx) for j in range(3)]
            vts = sbuf("vts_" + tag, [128, NT * 128], BF16, qctx)
            psq = [psum("qp%d_%s" % (i, tag), [128, 512], F32, qctx) for i in range(4)]
            psb = [psum("qb%d_%s" % (i, tag), [128, 1024], BF16, qctx) for i in range(2)]

            with nc.Block() as blk:
                @blk.gpsimd
                def _(gp):
                    for h in range(H):
                        for j in range(3):
                            buf = wjb[j]
                            par = h % 2
                            if h >= 1:
                                gp.wait_ge(pe.h, pe_j(h - 1, 0))
                            gp.dma_start(
                                bass.AP(buf, par * DT * 128, [[2 * DT * 128, 128], [1, DT * 128]]),
                                bass.AP(W[j], h * 128 * D, [[D, 128], [1, D]]),
                            ).then_inc(dW.h, 16)
                    dW.add(3 * H * 16)

                @blk.tensor
                def _(te):
                    for h in range(H):
                        te.wait_ge(dW.h, b_dW + 3 * (h + 1) * 16)
                        for j in range(3):
                            par = h % 2
                            if j > 0:
                                te.wait_ge(ve.h, ve_j(h, j - 1))
                            elif h > 0:
                                te.wait_ge(ve.h, ve_j(h - 1, 2))
                            last = None
                            for k in range(DT):
                                for tc in range(TC):
                                    last = te.matmul(
                                        psq[tc][:, 0:TCW],
                                        wjb[j][:, (par * DT + k) * 128:(par * DT + k + 1) * 128],
                                        bass.AP(ln_T, k * T + tc * TCW, [[NT * D, 128], [1, TCW]]),
                                        start=(k == 0), stop=(k == DT - 1))
                            last.then_inc(pe.h, 1)
                        te.wait_ge(ve.h, ve_j(h, 2))
                        if h > 0:
                            te.wait_ge(ve.h, ve_vt(h - 1))
                        for b in range(nvb):
                            lo, hi = b * 8, min(NT, b * 8 + 8)
                            last = None
                            for tti in range(lo, hi):
                                last = te.transpose(psb[b % 2][:, (tti - lo) * 128:(tti - lo + 1) * 128],
                                                    js_sb[2][:, tti * 128:(tti + 1) * 128],
                                                    id_sb[:, :])
                            last.then_inc(pe.h, 1)
                    pe.add(H * PE_H)

                @blk.vector
                def _(v):
                    for h in range(H):
                        if h > 0:
                            v.wait_ge(dS.h, b_dS + 3 * h * 16)
                        for j in range(3):
                            v.wait_ge(pe.h, pe_j(h, j))
                            lastq = None
                            for tc in range(TC):
                                if SCALE[j] == 1.0:
                                    lastq = v.tensor_scalar_add(js_sb[j][:, tc * TCW:(tc + 1) * TCW],
                                                                psq[tc][:, 0:TCW], BIAS[j][:, h:h + 1])
                                else:
                                    lastq = v.tensor_scalar(js_sb[j][:, tc * TCW:(tc + 1) * TCW],
                                                            psq[tc][:, 0:TCW], SCALE[j],
                                                            BIAS[j][:, h:h + 1],
                                                            op0=ALU.mult, op1=ALU.add)
                            lastq.then_inc(ve.h, 1)
                        lastv = None
                        for b in range(nvb):
                            lo, hi = b * 8, min(NT, b * 8 + 8)
                            v.wait_ge(pe.h, pe_vt(h, b))
                            lastv = v.tensor_copy(vts[:, lo * 128:lo * 128 + (hi - lo) * 128],
                                                  psb[b % 2][:, 0:(hi - lo) * 128])
                        lastv.then_inc(ve.h, 1)
                    ve.add(H * VE_H)

                @blk.sync
                def _(s):
                    for h in range(H):
                        for j in range(3):
                            s.wait_ge(ve.h, ve_j(h, j))
                            if j == 0:
                                s.dma_start(bass.AP(q_hbm, h * 128 * T, [[T, 128], [1, T]]),
                                            js_sb[0][:, :]).then_inc(dS.h, 16)
                            elif j == 1:
                                s.dma_start(bass.AP(kv_send[h // HG], (h % HG) * 128 * T,
                                                    [[T, 128], [1, T]]),
                                            js_sb[1][:, :]).then_inc(dS.h, 16)
                            else:
                                s.wait_ge(ve.h, ve_vt(h))
                                s.dma_start(
                                    bass.AP(kv_send[h // HG], KSZ + (h % HG) * T * 128,
                                            [[128, 128], [128 * 128, NT], [1, 128]]),
                                    bass.AP(vts, 0, [[NT * 128, 128], [128, NT], [1, 128]]),
                                ).then_inc(dS.h, 16)
                    s.wait_ge(dS.h, dS.n + 3 * H * 16)
                    dS.add(3 * H * 16)
        assert pe.n == b_pe + H * PE_H and ve.n == b_ve + H * VE_H

        with nc.Block() as blk:
            @blk.gpsimd
            def _(gp):
                for i in range(NCC):
                    gp.collective_compute(
                        "AllGather", ALU.bypass,
                        ins=[kv_send[i].ap().opt()],
                        outs=[kv_gath[i].ap().opt()],
                        replica_groups=g.get("RG") or [[0, 1], [2, 3], [4, 5], [6, 7]],
                    ).then_inc(ccS.h)
                    ccS.add(1)
                    if i < NCC - 1:
                        gp.wait_ge(ccS.h, ccS.n)

    def emit_attn(tag):
        SEQ_TOT = 2 * T
        assert NT % 2 == 0
        b_pe, b_sc, b_ve, b_dS = pe.base(), sc.base(), ve.base(), dS.base()
        cc_base = ccS.n - NCC
        pos = []
        for i in range(NT):
            ext = 256 * (i + 1)
            cw = chunk_widths(ext, 512)
            ntr = ext // 128
            nb = (ntr + 7) // 8
            pos.append((ext, cw, ntr, nb))

        def prev3(i):
            return NT - 1 - ((NT - 1 - i) % 3)

        pe_i, sc_i, ve_i = {}, {}, {}
        npe = nsc = nve = 0
        for h in range(H):
            for r in range(NT + 2):
                if r < NT:
                    for ci in range(len(pos[r][1])):
                        npe += 1; pe_i[(h, "s", r, ci)] = npe
                j = r - 2
                if 0 <= j < NT:
                    for b in range(pos[j][3]):
                        npe += 1; pe_i[(h, "t", j, b)] = npe
                    npe += 1; pe_i[(h, "av", j)] = npe
            for i in range(NT):
                for ci in range(len(pos[i][1])):
                    nsc += 1; sc_i[(h, "ev", i, ci)] = nsc
                if i >= 2:
                    nsc += 1; sc_i[(h, "z", i - 2)] = nsc
                nsc += 1; sc_i[(h, "exp", i)] = nsc
            nsc += 1; sc_i[(h, "z", NT - 2)] = nsc
            nsc += 1; sc_i[(h, "z", NT - 1)] = nsc
            for i in range(NT):
                if i >= 2:
                    for b in range(pos[i - 2][3]):
                        nve += 1; ve_i[(h, "te", i - 2, b)] = nve
                nve += 1; ve_i[(h, "nm", i)] = nve
                nve += 1; ve_i[(h, "sca", i)] = nve
            for j in (NT - 2, NT - 1):
                for b in range(pos[j][3]):
                    nve += 1; ve_i[(h, "te", j, b)] = nve

        def P(k):
            return b_pe + pe_i[k]

        def S(k):
            return b_sc + sc_i[k]

        def V(k):
            return b_ve + ve_i[k]

        def DSH(h):
            return b_dS + (h + 1) * 6 * 16

        with ExitStack() as actx:
            k_sb = sbuf("k_" + tag, [128, SEQ_TOT], BF16, actx)
            vts = sbuf("vt_" + tag, [128, SEQ_TOT], BF16, actx)
            q_sb = sbuf("q_" + tag, [128, T], BF16, actx)
            rows = [sbuf("rows%d_%s" % (i, tag), [128, SEQ_TOT], F32, actx) for i in range(2)]
            att = [sbuf("att%d_%s" % (i, tag), [128, SEQ_TOT], BF16, actx) for i in range(3)]
            attT = sbuf("attT_" + tag, [128, SEQ_TOT], BF16, actx)
            zall = sbuf("zall_" + tag, [128, T], BF16, actx)
            pss = [psum("ap%d_%s" % (i, tag), [128, 512], F32, actx) for i in range(4)]
            psz = psum("az_" + tag, [128, 512], F32, actx)
            psb = [psum("ab%d_%s" % (i, tag), [128, 1024], BF16, actx) for i in range(2)]
            NMC = [7, 8]
            DENC = [10, 12]
            RIC = [11, 13]

            def stp(cols, i):
                c = cols[i % 2]
                return st[:, c:c + 1]

            def emit_te_evac(v, h, j):
                for b in range(pos[j][3]):
                    lo, hi = b * 8, min(pos[j][2], b * 8 + 8)
                    v.wait_ge(pe.h, P((h, "t", j, b)))
                    v.tensor_copy(attT[:, lo * 128:lo * 128 + (hi - lo) * 128],
                                  psb[b % 2][:, 0:(hi - lo) * 128]).then_inc(ve.h, 1)

            with nc.Block() as blk:
                @blk.sync
                def _(s):
                    for h in range(H):
                        if h == 0:
                            s.wait_ge(ccS.h, cc_base + 1)
                        elif h == HG:
                            s.wait_ge(ccS.h, cc_base + NCC)
                        if h > 0:
                            s.wait_ge(sc.h, S((h - 1, "z", NT - 1)))
                        gb = kv_gath[h // HG]
                        hh = h % HG
                        for r in range(2):
                            s.dma_start(
                                bass.AP(k_sb, r * 128, [[SEQ_TOT, 128], [256, NT], [1, 128]]),
                                bass.AP(gb, r * 2 * KSZ + hh * 128 * T,
                                        [[T, 128], [128, NT], [1, 128]]),
                            ).then_inc(dS.h, 16)
                            s.dma_start(
                                bass.AP(vts, r * 128, [[SEQ_TOT, 128], [256, NT], [1, 128]]),
                                bass.AP(gb, r * 2 * KSZ + KSZ + hh * T * 128,
                                        [[128, 128], [128 * 128, NT], [1, 128]]),
                            ).then_inc(dS.h, 16)
                        s.dma_start(q_sb[:, :],
                                    bass.AP(q_hbm, h * 128 * T, [[T, 128], [1, T]])
                                    ).then_inc(dS.h, 16)
                        s.wait_ge(sc.h, S((h, "z", NT - 1)))
                        s.dma_start(bass.AP(z_hbm, h * 128 * T, [[T, 128], [1, T]]),
                                    zall[:, :]).then_inc(dS.h, 16)
                        s.wait_ge(dS.h, DSH(h))
                    dS.add(H * 6 * 16)

                @blk.tensor
                def _(te):
                    for h in range(H):
                        te.wait_ge(dS.h, DSH(h) - 16)
                        for r in range(NT + 2):
                            if r < NT:
                                ext, cw, ntr, nb = pos[r]
                                base_bank = (r % 2) * 2
                                off = 0
                                for ci, w in enumerate(cw):
                                    if ci >= 2:
                                        te.wait_ge(sc.h, S((h, "ev", r, ci - 2)))
                                    elif r >= 2:
                                        te.wait_ge(sc.h, S((h, "ev", r - 2, len(pos[r - 2][1]) - 1)))
                                    elif h > 0:
                                        te.wait_ge(sc.h, S((h - 1, "ev", NT - 2 + r, len(pos[NT - 2 + r][1]) - 1)))
                                    te.matmul(pss[base_bank + ci % 2][:, 0:w],
                                              q_sb[:, r * 128:(r + 1) * 128],
                                              k_sb[:, off:off + w],
                                              start=True, stop=True).then_inc(pe.h, 1)
                                    off += w
                            j = r - 2
                            if 0 <= j < NT:
                                ext, cw, ntr, nb = pos[j]
                                te.wait_ge(ve.h, V((h, "sca", j)))
                                for b in range(nb):
                                    lo, hi = b * 8, min(ntr, b * 8 + 8)
                                    if b >= 2:
                                        te.wait_ge(ve.h, V((h, "te", j, b - 2)))
                                    elif j > 0:
                                        te.wait_ge(ve.h, V((h, "te", j - 1, pos[j - 1][3] - 1)))
                                    elif h > 0:
                                        te.wait_ge(ve.h, V((h - 1, "te", NT - 1, pos[NT - 1][3] - 1)))
                                    last = None
                                    for kt in range(lo, hi):
                                        last = te.transpose(
                                            psb[b % 2][:, (kt - lo) * 128:(kt - lo + 1) * 128],
                                            att[j % 3][:, kt * 128:(kt + 1) * 128], id_sb[:, :])
                                    last.then_inc(pe.h, 1)
                                te.wait_ge(ve.h, V((h, "te", j, nb - 1)))
                                if j > 0:
                                    te.wait_ge(sc.h, S((h, "z", j - 1)))
                                elif h > 0:
                                    te.wait_ge(sc.h, S((h - 1, "z", NT - 1)))
                                last = None
                                for kt in range(ntr):
                                    last = te.matmul(psz[:, 0:128],
                                                     vts[:, kt * 128:(kt + 1) * 128],
                                                     attT[:, kt * 128:(kt + 1) * 128],
                                                     start=(kt == 0), stop=(kt == ntr - 1))
                                last.then_inc(pe.h, 1)
                    pe.add(npe)

                @blk.scalar
                def _(s):
                    preload_set(s, "natural_log_exp_and_others")
                    for h in range(H):
                        for i in range(NT):
                            ext, cw, ntr, nb = pos[i]
                            base_bank = (i % 2) * 2
                            off = 0
                            for ci, w in enumerate(cw):
                                s.wait_ge(pe.h, P((h, "s", i, ci)))
                                if ci == 0 and i >= 2:
                                    s.wait_ge(ve.h, V((h, "nm", i - 2)))
                                s.activation(rows[i % 2][:, off:off + w],
                                             pss[base_bank + ci % 2][:, 0:w],
                                             AF.Copy).then_inc(sc.h, 1)
                                off += w
                            if i >= 2:
                                s.wait_ge(pe.h, P((h, "av", i - 2)))
                                s.activation(zall[:, (i - 2) * 128:(i - 1) * 128],
                                             psz[:, 0:128], AF.Copy).then_inc(sc.h, 1)
                            elif i == 0 and h > 0:
                                s.wait_ge(dS.h, DSH(h - 1))
                            s.wait_ge(ve.h, V((h, "nm", i)))
                            if i >= 3:
                                s.wait_ge(pe.h, P((h, "t", i - 3, pos[i - 3][3] - 1)))
                            elif h > 0:
                                pj = prev3(i)
                                s.wait_ge(pe.h, P((h - 1, "t", pj, pos[pj][3] - 1)))
                            s.drain()
                            s.activation(att[i % 3][:, 0:ext], rows[i % 2][:, 0:ext], AF.Exp,
                                         bias=stp(NMC, i),
                                         accum_out=stp(DENC, i)).then_inc(sc.h, 1)
                        for j in (NT - 2, NT - 1):
                            s.wait_ge(pe.h, P((h, "av", j)))
                            s.activation(zall[:, j * 128:(j + 1) * 128],
                                         psz[:, 0:128], AF.Copy).then_inc(sc.h, 1)
                    sc.add(nsc)

                @blk.vector
                def _(v):
                    for h in range(H):
                        for i in range(NT):
                            ext, cw, ntr, nb = pos[i]
                            if i >= 2:
                                emit_te_evac(v, h, i - 2)
                            v.wait_ge(sc.h, S((h, "ev", i, len(cw) - 1)))
                            v.tensor_add(rows[i % 2][:, ext - 256:ext],
                                         rows[i % 2][:, ext - 256:ext], mask_sb[:, :])
                            v.drain()
                            v.reduce_max(stp(NMC, i), rows[i % 2][:, 0:ext], axis=AX.X,
                                         negate=True).then_inc(ve.h, 1)
                            v.wait_ge(sc.h, S((h, "exp", i)))
                            v.reciprocal(stp(RIC, i), stp(DENC, i))
                            v.drain()
                            v.tensor_scalar_mul(att[i % 3][:, 0:ext], att[i % 3][:, 0:ext],
                                                stp(RIC, i)).then_inc(ve.h, 1)
                        for j in (NT - 2, NT - 1):
                            emit_te_evac(v, h, j)
                    ve.add(nve)
        assert pe.n == b_pe + npe and sc.n == b_sc + nsc and ve.n == b_ve + nve

    def emit_oproj(tag):
        b_pe, b_sc, b_ve, b_dS, b_dW = pe.base(), sc.base(), ve.base(), dS.base(), dW.base()
        NG8 = (NT + 7) // 8
        with ExitStack() as octx:
            wob = sbuf("wo_" + tag, [128, H * D], BF16, octx)
            zt8 = sbuf("zt_" + tag, [128, H * 8 * 128], BF16, octx)
            bob = sbuf("bo_" + tag, [128, D], F32, octx)
            tmp = [sbuf("otmp%d_%s" % (i, tag), [128, 512], F32, octx) for i in range(2)]
            pso = [psum("op%d_%s" % (i, tag), [128, 512], F32, octx) for i in range(min(4, NT * DC))]
            nps = len(pso)

            with nc.Block() as blk:
                @blk.gpsimd
                def _(gp):
                    gp.dma_start(
                        bass.AP(wob, 0, [[H * D, 128], [D, H], [1, D]]),
                        bass.AP(wo, 0, [[D, 128], [128 * D, H], [1, D]]),
                    ).then_inc(dW.h, 16)
                    dW.add(16)

                @blk.sync
                def _(s):
                    s.dma_start(bob[:, :], bass.AP(bo, 0, [[0, 128], [1, D]])).then_inc(dS.h, 16)
                    s.wait_ge(dS.h, b_dS + 16)
                    for gp8 in range(NG8):
                        ntt = min(8, NT - gp8 * 8)
                        if gp8 > 0:
                            s.wait_ge(pe.h, b_pe + gp8 * 8 * DC)
                        s.dma_start(
                            bass.AP(zt8, 0, [[H * 8 * 128, 128], [8 * 128, H], [1, ntt * 128]]),
                            bass.AP(z_hbm, gp8 * 8 * 128, [[T, 128], [128 * T, H], [1, ntt * 128]]),
                        ).then_inc(dS.h, 16)
                        s.wait_ge(dS.h, b_dS + (gp8 + 2) * 16)
                    dS.add((NG8 + 1) * 16)

                @blk.tensor
                def _(te):
                    te.wait_ge(dW.h, b_dW + 16)
                    for gp8 in range(NG8):
                        ntt = min(8, NT - gp8 * 8)
                        te.wait_ge(dS.h, b_dS + (gp8 + 2) * 16)
                        for tt in range(ntt):
                            t = gp8 * 8 + tt
                            for dc in range(DC):
                                if t * DC + dc >= nps:
                                    te.wait_ge(ve.h, b_ve + t * DC + dc - nps + 1)
                                last = None
                                for h in range(H):
                                    last = te.matmul(pso[(t * DC + dc) % nps][:, 0:DCW],
                                                     zt8[:, h * 8 * 128 + tt * 128: h * 8 * 128 + (tt + 1) * 128],
                                                     wob[:, h * D + dc * DCW: h * D + (dc + 1) * DCW],
                                                     start=(h == 0), stop=(h == H - 1))
                                last.then_inc(pe.h, 1)
                    pe.add(NT * DC)

                @blk.vector
                def _(v):
                    v.wait_ge(dS.h, b_dS + 16)
                    for t in range(NT):
                        for dc in range(DC):
                            gi = t * DC + dc
                            v.wait_ge(pe.h, b_pe + gi + 1)
                            v.tensor_add(tmp[gi % 2][:, 0:DCW], pso[gi % nps][:, 0:DCW],
                                         bob[:, dc * DCW:(dc + 1) * DCW])
                            v.drain()
                            v.tensor_add(x_res[:, t * D + dc * DCW: t * D + (dc + 1) * DCW],
                                         x_res[:, t * D + dc * DCW: t * D + (dc + 1) * DCW],
                                         tmp[gi % 2][:, 0:DCW]).then_inc(ve.h, 1)
                    ve.add(NT * DC)

    def emit_out():
        b_sc, b_dS = sc.base(), dS.base()
        with ExitStack() as octx:
            of = [sbuf("of%d" % i, [128, D], F32, octx) for i in range(2)]
            with nc.Block() as blk:
                @blk.scalar
                def _(s):
                    for t in range(NT):
                        if t >= 2:
                            s.wait_ge(dS.h, b_dS + (t - 1) * 16)
                        s.activation(of[t % 2][:, :], x_res[:, t * D:(t + 1) * D],
                                     AF.Copy).then_inc(sc.h, 1)
                    sc.add(NT)

                @blk.sync
                def _(s):
                    for t in range(NT):
                        s.wait_ge(sc.h, b_sc + t + 1)
                        s.dma_start(bass.AP(y_out, t * 128 * D, [[D, 128], [1, D]]),
                                    of[t % 2][:, :]).then_inc(dS.h, 16)
                        s.wait_ge(dS.h, b_dS + (t + 1) * 16)
                    dS.add(NT * 16)

    def emit_dump(dst):
        b_sc, b_dS = sc.base(), dS.base()
        with ExitStack() as octx:
            of = [sbuf("dof%d_%s" % (i, dst.name), [128, D], F32, octx) for i in range(2)]
            with nc.Block() as blk:
                @blk.scalar
                def _(s):
                    for t in range(NT):
                        if t >= 2:
                            s.wait_ge(dS.h, b_dS + (t - 1) * 16)
                        s.activation(of[t % 2][:, :], x_res[:, t * D:(t + 1) * D],
                                     AF.Copy).then_inc(sc.h, 1)
                    sc.add(NT)

                @blk.sync
                def _(s):
                    for t in range(NT):
                        s.wait_ge(sc.h, b_sc + t + 1)
                        s.dma_start(bass.AP(dst, t * 128 * D, [[D, 128], [1, D]]),
                                    of[t % 2][:, :]).then_inc(dS.h, 16)
                        s.wait_ge(dS.h, b_dS + (t + 1) * 16)
                    dS.add(NT * 16)

    STOP = g.get("STOP")
    with ExitStack() as inv1:
        ln_T = sbuf("lnT1", [128, NT * D], BF16, inv1)
        emit_ln(ln_T, "l1")
        if DBG:
            emit_sbuf_dump(dbg_lnT, ln_T, NT * D)
            emit_sbuf_dump(dbg_st, st, 32)
        if STOP == "ln1":
            return nc
        emit_routing(ln_T, "r1")
        if DBG:
            emit_sbuf_dump(dbg_c, c_all, NT * E)
        if STOP == "route1":
            return nc
        emit_ffn(ln_T, "f1")
        if DBG:
            emit_sbuf_dump(dbg_h, hb_last[0], IT * T)
    if DBG >= 1:
        emit_dump(dbg_outs[0])
    if STOP == "ffn1":
        return nc
    with ExitStack() as att:
        ln_T = sbuf("lnT2", [128, NT * D], BF16, att)
        emit_ln(ln_T, "l2")
        emit_qkv(ln_T, "qk")
    emit_attn("at")
    emit_oproj("op")
    if DBG >= 2:
        emit_dump(dbg_outs[1])
    with ExitStack() as inv2:
        ln_T = sbuf("lnT3", [128, NT * D], BF16, inv2)
        emit_ln(ln_T, "l3")
        emit_routing(ln_T, "r3")
        emit_ffn(ln_T, "f3")
    emit_out()
    return nc


def make_inputs(core, full, g):
    D, T, I, E, H = g["D"], g["T"], g["I"], g["E"], g["H"]
    NT = T // 128
    b, r = core // 2, core % 2
    xb = np.ascontiguousarray(
        full["L0_attn_x"][b].reshape(2 * NT, 128, D)[r::2])
    mask = np.where(np.arange(256)[None, :] <= 128 * r + np.arange(128)[:, None],
                    np.float32(0), np.float32(-1e9 if False else -100000.0)).astype(np.float32)
    E_, I_, H_ = g["E"], g["I"], g["H"]
    DT_, IT_ = D // 128, I_ // 128

    def gu5(expert_mat, shared_mat):
        egT = expert_mat.transpose(0, 2, 1)
        e5 = egT.reshape(E_, DT_, 128, IT_, 128).transpose(0, 3, 4, 1, 2)
        e5 = egT.reshape(E_, DT_, 128, IT_, 128).transpose(0, 3, 2, 1, 4)
        e5 = e5.reshape(E_, IT_, 128, DT_ * 128)
        shT = shared_mat.T
        s5 = shT.reshape(DT_, 128, 2, IT_, 128).transpose(2, 3, 1, 0, 4)
        s5 = s5.reshape(2, IT_, 128, DT_ * 128)
        return np.ascontiguousarray(np.concatenate([e5, s5], axis=0))

    def qkv4(w):
        return np.ascontiguousarray(
            w.reshape(H_, DT_, 128, 128).transpose(0, 2, 1, 3).reshape(H_, 128, D))

    im = dict(
        x_in=xb,
        wq=qkv4(full["L0_attn_W_Q"]), wk=qkv4(full["L0_attn_W_K"]),
        wv=qkv4(full["L0_attn_W_V"]),
        wo=full["L0_attn_W_O"],
        bqT=np.ascontiguousarray(full["L0_attn_b_Q"].T),
        bkT=np.ascontiguousarray(full["L0_attn_b_K"].T),
        bvT=np.ascontiguousarray(full["L0_attn_b_V"].T),
        bo=full["L0_attn_b_O"],
        gate_wT=np.ascontiguousarray(full["L0_ffn_gate_w"].T),
        wg5=gu5(full["L0_ffn_expert_gate"], full["L0_ffn_shared_gate"]),
        wu5=gu5(full["L0_ffn_expert_up"], full["L0_ffn_shared_up"]),
        wed_T=np.ascontiguousarray(full["L0_ffn_expert_down"].transpose(0, 2, 1)),
        wsd_T=np.ascontiguousarray(full["L0_ffn_shared_down"].T),
        idm=np.eye(128, dtype=ml_dtypes.bfloat16),
        amask=mask,
    )
    return im


def kernel(**inputs):
    g = FULL_GEO
    D, T = g["D"], g["T"]
    NT = T // 128
    nc = build_nc(g)
    shared = None
    in_maps = []
    for c in range(8):
        im = make_inputs(c, inputs, g)
        in_maps.append(im)
    res = run_bass_kernel_spmd(nc, in_maps, core_ids=list(range(8)))
    out = np.zeros((4, 4096, D), np.float32)
    for c in range(8):
        b, r = c // 2, c % 2
        y = res.results[c]["y_out"]
        out[b].reshape(2 * NT, 128, D)[r::2] = y
    return out


# version 08
# speedup vs baseline: 1.0653x; 1.0303x over previous
import numpy as np
import ml_dtypes
from contextlib import ExitStack

import concourse.bass as bass
import concourse.mybir as mybir
from concourse.bass_utils import run_bass_kernel_spmd

F32 = mybir.dt.float32
BF16 = mybir.dt.bfloat16
AF = mybir.ActivationFunctionType
ALU = mybir.AluOpType
AX = mybir.AxisListType

RSQ = float(1.0 / np.sqrt(np.float32(128.0)))
LN_EPS = 1e-5

FULL_GEO = dict(D=2048, T=2048, I=1408, E=8, H=16)


class Sem:
    def __init__(self, ctx, nc, name):
        self.h = ctx.enter_context(nc.semaphore(name))
        self.n = 0

    def base(self):
        return self.n

    def add(self, amt=1):
        self.n += amt
        return self.n


def build_nc(g):
    D, T, I, E, H = g["D"], g["T"], g["I"], g["E"], g["H"]
    ISH = 2 * I
    DT = D // 128
    NT = T // 128
    IT = I // 128
    TC = max(1, T // 512)
    TCW = min(T, 512)
    NEXP = E + 2
    NTG = (NT + 7) // 8
    DC = max(1, D // 512)
    DCW = min(D, 512)
    assert D % 128 == 0 and T % 128 == 0 and I % 128 == 0 and T % TCW == 0 and D % DCW == 0

    nc = bass.Bass("TRN2", target_bir_lowering=False)

    x_in = nc.dram_tensor("x_in", [NT, 128, D], F32, kind="ExternalInput")
    wq = nc.dram_tensor("wq", [H, 128, D], F32, kind="ExternalInput")
    wk = nc.dram_tensor("wk", [H, 128, D], F32, kind="ExternalInput")
    wv = nc.dram_tensor("wv", [H, 128, D], F32, kind="ExternalInput")
    wo = nc.dram_tensor("wo", [H, 128, D], F32, kind="ExternalInput")
    bqT = nc.dram_tensor("bqT", [128, H], F32, kind="ExternalInput")
    bkT = nc.dram_tensor("bkT", [128, H], F32, kind="ExternalInput")
    bvT = nc.dram_tensor("bvT", [128, H], F32, kind="ExternalInput")
    bo = nc.dram_tensor("bo", [D], F32, kind="ExternalInput")
    gate_wT = nc.dram_tensor("gate_wT", [D, E], F32, kind="ExternalInput")
    wg5 = nc.dram_tensor("wg5", [E + 2, IT, 128, DT * 128], F32, kind="ExternalInput")
    wu5 = nc.dram_tensor("wu5", [E + 2, IT, 128, DT * 128], F32, kind="ExternalInput")
    wed_T = nc.dram_tensor("wed_T", [E, I, D], F32, kind="ExternalInput")
    wsd_T = nc.dram_tensor("wsd_T", [ISH, D], F32, kind="ExternalInput")
    idm = nc.dram_tensor("idm", [128, 128], BF16, kind="ExternalInput")
    amask = nc.dram_tensor("amask", [128, 256], F32, kind="ExternalInput")
    y_out = nc.dram_tensor("y_out", [NT, 128, D], F32, kind="ExternalOutput")
    DBG = g.get("DBG", 0)
    dbg_outs = [nc.dram_tensor("dbg%d" % i, [NT, 128, D], F32, kind="ExternalOutput")
                for i in range(DBG)]
    if DBG:
        dbg_lnT = nc.dram_tensor("dbg_lnT", [128, NT * D], BF16, kind="ExternalOutput")
        dbg_c = nc.dram_tensor("dbg_c", [128, NT * E], F32, kind="ExternalOutput")
        dbg_h = nc.dram_tensor("dbg_h", [128, IT * T], BF16, kind="ExternalOutput")
        dbg_st = nc.dram_tensor("dbg_st", [128, 32], F32, kind="ExternalOutput")

    def emit_sbuf_dump(dst, src_sb, cols):
        with nc.Block() as blk:
            @blk.sync
            def _(s):
                s.dma_start(bass.AP(dst, 0, [[cols, 128], [1, cols]]),
                            src_sb[:, 0:cols]).then_inc(dS.h, 16)
                s.wait_ge(dS.h, dS.add(16))

    HG = max(1, H // 2)
    NCC = (H + HG - 1) // HG
    KSZ = HG * 128 * T
    kv_send = [nc.dram_tensor("kv_send%d" % i, [2 * KSZ // 512, 512], BF16)
               for i in range(NCC)]
    kv_gath = [nc.dram_tensor("kv_gath%d" % i, [4 * KSZ // 512, 512], BF16)
               for i in range(NCC)]
    q_hbm = nc.dram_tensor("q_hbm", [H, 128, T], BF16)
    z_hbm = nc.dram_tensor("z_hbm", [H, 128, T], BF16)

    ctx = ExitStack()
    nc._kernel_ctx = ctx

    def sbuf(name, shape, dt, scope=None):
        return (scope or ctx).enter_context(nc.sbuf_tensor(name, shape, dt))

    def psum(name, shape, dt, scope=None):
        return (scope or ctx).enter_context(nc.psum_tensor(name, shape, dt))

    x_res = sbuf("x_res", [128, NT * D], BF16)
    id_sb = sbuf("id_sb", [128, 128], BF16)
    st = sbuf("st", [128, 32], F32)
    c_all = sbuf("c_all", [128, NT * E], F32)
    bq_sb = sbuf("bq_sb", [128, H], F32)
    bk_sb = sbuf("bk_sb", [128, H], F32)
    bv_sb = sbuf("bv_sb", [128, H], F32)
    bqs_sb = sbuf("bqs_sb", [128, H], F32)
    gw_sb = sbuf("gw_sb", [128, DT * E], BF16)
    mask_sb = sbuf("mask_sb", [128, 256], F32)

    sems = ExitStack()
    ctx.enter_context(sems)
    dW = Sem(ctx, nc, "dW")
    dS = Sem(ctx, nc, "dS")
    pe = Sem(ctx, nc, "pe")
    sc = Sem(ctx, nc, "sc")
    ve = Sem(ctx, nc, "ve")
    ccS = Sem(ctx, nc, "ccS")

    COL = dict(sum=0, sq=1, mean=2, e2=3, var=4, rstd=5, mr=6,
               negmax=7, m1=8, m2=9, den=10, rinv=11)

    def stc(name):
        c = COL[name]
        return st[:, c:c + 1]

    from concourse.hw_specs import get_activation_tables
    _tabs = list(get_activation_tables(nc.m.arch).keys())

    def preload_set(s, name):
        s.add_instruction(mybir.InstLoadActFuncSet(
            name=nc.get_next_instruction_name(),
            act_func_set_id=_tabs.index(name), ins=[], outs=[]))

    def chunk_widths(total, w):
        out = []
        o = 0
        while o < total:
            out.append(min(w, total - o))
            o += w
        return out

    with nc.Block() as blk:
        @blk.sync
        def _(s):
            s.dma_start(id_sb[:, :], idm[:, :]).then_inc(dS.h, 16); dS.add(16)
            s.dma_start(mask_sb[:, :], amask[:, :]).then_inc(dS.h, 16); dS.add(16)
            s.dma_start(bq_sb[:, :], bqT[:, :]).then_inc(dS.h, 16); dS.add(16)
            s.dma_start(bk_sb[:, :], bkT[:, :]).then_inc(dS.h, 16); dS.add(16)
            s.dma_start(bv_sb[:, :], bvT[:, :]).then_inc(dS.h, 16); dS.add(16)
            s.wait_ge(dS.h, dS.n)

        @blk.gpsimd
        def _(gp):
            gp.dma_start(
                bass.AP(x_res, 0, [[NT * D, 128], [D, NT], [1, D]]),
                bass.AP(x_in, 0, [[D, 128], [128 * D, NT], [1, D]]),
            ).then_inc(dW.h, 16); dW.add(16)
            gp.dma_start(
                bass.AP(gw_sb, 0, [[DT * E, 128], [E, DT], [1, E]]),
                bass.AP(gate_wT, 0, [[E, 128], [128 * E, DT], [1, E]]),
            ).then_inc(dW.h, 16); dW.add(16)
            gp.wait_ge(dW.h, dW.n)

        @blk.vector
        def _(v):
            v.wait_ge(dS.h, dS.n)
            v.tensor_scalar_mul(bqs_sb[:, :], bq_sb[:, :], RSQ)

    def emit_ln(ln_T, tag):
        nbat = (DT + 7) // 8
        b_sc, b_ve, b_pe = sc.base(), ve.base(), pe.base()
        with ExitStack() as lctx:
            dump = sbuf("lndump_" + tag, [128, D], BF16, lctx)
            lntok = sbuf("lntok_" + tag, [128, D], BF16, lctx)
            psb = [psum("lnp%d_%s" % (i, tag), [128, 1024], BF16, lctx) for i in range(2)]
            with nc.Block() as blk:
                @blk.scalar
                def _(s):
                    preload_set(s, "natural_log_exp_and_others")
                    for t in range(NT):
                        if t > 0:
                            s.wait_ge(ve.h, b_ve + 3 * (t - 1) + 1)
                        s.activation(dump[:, :], x_res[:, t * D:(t + 1) * D],
                                     AF.Square, accum_out=stc("sq")).then_inc(sc.h, 1)
                        s.wait_ge(ve.h, b_ve + 3 * t + 1)
                        s.activation(stc("rstd"), stc("var"), AF.Ln)
                        s.drain()
                        s.activation(stc("rstd"), stc("rstd"), AF.Exp,
                                     scale=-0.5).then_inc(sc.h, 1)
                sc.add(2 * NT)

                @blk.vector
                def _(v):
                    for t in range(NT):
                        v.reduce_sum(stc("sum"), x_res[:, t * D:(t + 1) * D], axis=AX.X)
                        v.wait_ge(sc.h, b_sc + 2 * t + 1)
                        v.drain()
                        v.tensor_scalar_mul(stc("mean"), stc("sum"), 1.0 / D)
                        v.tensor_scalar_mul(stc("e2"), stc("sq"), 1.0 / D)
                        v.drain()
                        v.tensor_mul(stc("var"), stc("mean"), stc("mean"))
                        v.drain()
                        v.tensor_sub(stc("var"), stc("e2"), stc("var"))
                        v.drain()
                        v.tensor_scalar_add(stc("var"), stc("var"),
                                            LN_EPS).then_inc(ve.h, 1)
                        v.wait_ge(sc.h, b_sc + 2 * t + 2)
                        v.tensor_mul(stc("mr"), stc("mean"), stc("rstd"))
                        v.drain()
                        if t > 0:
                            v.wait_ge(pe.h, b_pe + nbat * t)
                        v.tensor_scalar(lntok[:, :], x_res[:, t * D:(t + 1) * D],
                                        stc("rstd"), stc("mr"),
                                        op0=ALU.mult, op1=ALU.subtract).then_inc(ve.h, 1)
                        lastc = None
                        for b in range(nbat):
                            lo, hi = b * 8, min(DT, b * 8 + 8)
                            v.wait_ge(pe.h, b_pe + nbat * t + b + 1)
                            lastc = v.tensor_copy(
                                bass.AP(ln_T, lo * T + t * 128,
                                        [[NT * D, 128], [T, hi - lo], [1, 128]]),
                                psb[b % 2][:, 0:(hi - lo) * 128])
                        lastc.then_inc(ve.h, 1)
                ve.add(3 * NT)

                @blk.tensor
                def _(te):
                    for t in range(NT):
                        te.wait_ge(ve.h, b_ve + 3 * t + 2)
                        if t > 0:
                            te.wait_ge(ve.h, b_ve + 3 * (t - 1) + 3)
                        for b in range(nbat):
                            lo, hi = b * 8, min(DT, b * 8 + 8)
                            last = None
                            for d in range(lo, hi):
                                last = te.transpose(
                                    psb[b % 2][:, (d - lo) * 128:(d - lo + 1) * 128],
                                    lntok[:, d * 128:(d + 1) * 128], id_sb[:, :])
                            last.then_inc(pe.h, 1)
                pe.add(nbat * NT)
        assert sc.n == b_sc + 2 * NT and ve.n == b_ve + 3 * NT and pe.n == b_pe + nbat * NT

    def emit_routing(ln_T, tag):
        b_pe, b_sc, b_ve = pe.base(), sc.base(), ve.base()
        with ExitStack() as rctx:
            rp = psum("rp_" + tag, [128, 512], F32, rctx)
            lg = sbuf("lg_" + tag, [128, E], F32, rctx)
            ex = sbuf("ex_" + tag, [128, E], F32, rctx)
            zz = sbuf("zz_" + tag, [128, 1], F32, rctx)
            selb = sbuf("sel_" + tag, [128, E], F32, rctx)
            pmb = sbuf("pm_" + tag, [128, E], F32, rctx)
            with nc.Block() as blk:
                @blk.tensor
                def _(te):
                    for t in range(NT):
                        if t > 0:
                            te.wait_ge(sc.h, b_sc + 2 * (t - 1) + 1)
                        last = None
                        for d in range(DT):
                            last = te.matmul(rp[:, 0:E],
                                             bass.AP(ln_T, d * T + t * 128, [[NT * D, 128], [1, 128]]),
                                             gw_sb[:, d * E:(d + 1) * E],
                                             start=(d == 0), stop=(d == DT - 1))
                        last.then_inc(pe.h, 1)
                pe.add(NT)

                @blk.scalar
                def _(s):
                    for t in range(NT):
                        s.wait_ge(pe.h, b_pe + t + 1)
                        if t > 0:
                            s.wait_ge(ve.h, b_ve + 2 * (t - 1) + 2)
                        s.activation(lg[:, :], rp[:, 0:E], AF.Copy).then_inc(sc.h, 1)
                        s.wait_ge(ve.h, b_ve + 2 * t + 1)
                        s.activation(ex[:, :], lg[:, :], AF.Exp, bias=stc("negmax"),
                                     accum_out=zz[:, :]).then_inc(sc.h, 1)
                sc.add(2 * NT)

                @blk.vector
                def _(v):
                    for t in range(NT):
                        v.wait_ge(sc.h, b_sc + 2 * t + 1)
                        v.reduce_max(stc("negmax"), lg[:, :], axis=AX.X,
                                     negate=True).then_inc(ve.h, 1)
                        v.wait_ge(sc.h, b_sc + 2 * t + 2)
                        v.reduce_max(stc("m1"), ex[:, :], axis=AX.X)
                        v.drain()
                        v.tensor_single_scalar(selb[:, :], ex[:, :], stc("m1"), op=ALU.is_ge)
                        v.drain()
                        v.scalar_tensor_tensor(pmb[:, :], selb[:, :], -1.0e9, ex[:, :],
                                               op0=ALU.mult, op1=ALU.add)
                        v.drain()
                        v.reduce_max(stc("m2"), pmb[:, :], axis=AX.X)
                        v.drain()
                        v.tensor_single_scalar(selb[:, :], ex[:, :], stc("m2"), op=ALU.is_ge)
                        v.tensor_tensor(stc("den"), stc("m1"), stc("m2"), op=ALU.add)
                        v.drain()
                        v.scalar_tensor_tensor(stc("den"), zz[:, :], 1.0e-20, stc("den"),
                                               op0=ALU.mult, op1=ALU.add)
                        v.drain()
                        v.reciprocal(stc("rinv"), stc("den"))
                        v.tensor_mul(c_all[:, t * E:(t + 1) * E], ex[:, :], selb[:, :])
                        v.drain()
                        v.tensor_scalar(c_all[:, t * E:(t + 1) * E], c_all[:, t * E:(t + 1) * E],
                                        stc("rinv"), 0.5, op0=ALU.mult,
                                        op1=ALU.mult).then_inc(ve.h, 1)
                ve.add(2 * NT)
        assert pe.n == b_pe + NT and sc.n == b_sc + 2 * NT and ve.n == b_ve + 2 * NT

    hb_last = [None]

    def emit_ffn(ln_T, tag):
        KT = DT
        mid_k = KT // 2
        mid_i = max(1, IT // 2)
        gu_waves = [(0, TC)] if TC <= 2 else [(0, TC // 2), (TC // 2, TC)]
        tg_sizes = [min(8, NT - 8 * tgi) for tgi in range(NTG)]

        def dn_waves(ntt):
            return [(0, ntt)] if ntt <= 4 else [(0, 4), (4, ntt)]

        experts = []
        for e in range(E):
            experts.append(dict(kind="exp", e=e, dsrc=wed_T, doff=e * I * D))
        for half in range(2):
            experts.append(dict(kind="sh", e=None, dsrc=wsd_T, doff=half * I * D))
        NX = len(experts)
        NW = len(gu_waves)

        b_pe, b_sc, b_ve, b_dW = pe.base(), sc.base(), ve.base(), dW.base()

        pe_i, sc_i, ve_i, dw_i = {}, {}, {}, {}
        npe = nsc = nve = ndw = 0
        for x in range(NX):
            for i in range(IT):
                for q in range(4):
                    ndw += 16; dw_i[(x, "gu", i, q)] = ndw
            for c in range(NTG * DC):
                for hlf in range(2):
                    ndw += 16; dw_i[(x, "dn", c, hlf)] = ndw
            for i in range(IT):
                for w in range(NW):
                    npe += 1; pe_i[(x, "gu", i, w)] = npe
            for tgi in range(NTG):
                ntt = tg_sizes[tgi]
                for dc in range(DC):
                    c = tgi * DC + dc
                    dws = dn_waves(ntt)
                    for wi in range(len(dws)):
                        if wi == len(dws) - 1:
                            npe += 1; pe_i[(x, "dnmid", c)] = npe
                        npe += 1; pe_i[(x, "dn", c, wi)] = npe
            for i in range(IT):
                for w in range(NW):
                    nsc += 1; sc_i[(x, i, w)] = nsc
            for i in range(IT):
                for w in range(NW):
                    nve += 1; ve_i[(x, "h", i, w)] = nve
            for tgi in range(NTG):
                ntt = tg_sizes[tgi]
                for dc in range(DC):
                    c = tgi * DC + dc
                    for wi in range(len(dn_waves(ntt))):
                        nve += 1; ve_i[(x, "x", c, wi)] = nve

        def PF(k):
            return b_pe + pe_i[k]

        def SF(k):
            return b_sc + sc_i[k]

        def VF(k):
            return b_ve + ve_i[k]

        def DWF(k):
            return b_dW + dw_i[k]

        NCH = NTG * DC

        with ExitStack() as fctx:
            wgb = [sbuf("wg%d_%s" % (p, tag), [128, KT * 128], BF16, fctx) for p in range(2)]
            wub = [sbuf("wu%d_%s" % (p, tag), [128, KT * 128], BF16, fctx) for p in range(2)]
            wdb = sbuf("wd_" + tag, [128, IT * DCW], BF16, fctx)
            sgb = sbuf("sg_" + tag, [128, TC * TCW], BF16, fctx)
            hb = sbuf("h_" + tag, [128, IT * T], BF16, fctx)
            hb_last[0] = hb
            ps = [psum("fp%d_%s" % (i, tag), [128, 512], F32, fctx) for i in range(8)]

            with nc.Block() as blk:
                @blk.gpsimd
                def _(gp):
                    for x in range(NX):
                        for i in range(IT):
                            gidx = x * IT + i
                            par = gidx % 2
                            for q in range(4):
                                srcw = wg5 if q < 2 else wu5
                                hlf = q % 2
                                klo = hlf * mid_k
                                khi = mid_k if hlf == 0 else KT - mid_k
                                dst = (wgb if q < 2 else wub)[par]
                                if i >= 1:
                                    gp.wait_ge(pe.h, PF((x, "gu", i - 1, 0)))
                                elif x > 0:
                                    gp.wait_ge(pe.h, PF((x - 1, "dnmid", NCH - 1)))
                                gp.dma_start(
                                    bass.AP(dst, klo * 128, [[KT * 128, 128], [1, khi * 128]]),
                                    bass.AP(srcw, (gidx * 128) * (KT * 128) + klo * 128,
                                            [[KT * 128, 128], [1, khi * 128]]),
                                ).then_inc(dW.h, 16)
                        ex = experts[x]
                        for c in range(NCH):
                            dc = c % DC
                            for hlf in range(2):
                                ilo = hlf * mid_i
                                ihi = mid_i if hlf == 0 else IT - mid_i
                                if c > 0:
                                    gp.wait_ge(pe.h, PF((x, "dnmid", c - 1)) if hlf == 0
                                               else PF((x, "dn", c - 1, len(dn_waves(tg_sizes[(c - 1) // DC])) - 1)))
                                else:
                                    gp.wait_ge(pe.h, PF((x, "gu", IT - 1, NW - 1)))
                                gp.dma_start(
                                    bass.AP(wdb, ilo * DCW, [[IT * DCW, 128], [DCW, ihi], [1, DCW]]),
                                    bass.AP(ex["dsrc"], ex["doff"] + (ilo * 128) * D + dc * DCW,
                                            [[D, 128], [128 * D, ihi], [1, DCW]]),
                                ).then_inc(dW.h, 16)
                    dW.add(ndw)

                @blk.tensor
                def _(te):
                    for x in range(NX):
                        for i in range(IT):
                            gidx = x * IT + i
                            par = gidx % 2
                            for w, (tclo, tchi) in enumerate(gu_waves):
                                if i > 0:
                                    te.wait_ge(sc.h, SF((x, i - 1, w)))
                                    te.wait_ge(ve.h, VF((x, "h", i - 1, w)))
                                elif x > 0:
                                    te.wait_ge(sc.h, SF((x - 1, IT - 1, w)))
                                    lc = NCH - 1
                                    lw = len(dn_waves(tg_sizes[lc // DC])) - 1
                                    te.wait_ge(ve.h, VF((x - 1, "x", lc, lw)))
                                if w == 0:
                                    te.wait_ge(dW.h, DWF((x, "gu", i, 3)))
                                last = None
                                for k in range(KT):
                                    for tc in range(tclo, tchi):
                                        te.matmul(ps[tc][:, 0:TCW],
                                                  wgb[par][:, k * 128:(k + 1) * 128],
                                                  bass.AP(ln_T, k * T + tc * TCW, [[NT * D, 128], [1, TCW]]),
                                                  start=(k == 0), stop=(k == KT - 1))
                                    for tc in range(tclo, tchi):
                                        last = te.matmul(ps[TC + tc][:, 0:TCW],
                                                         wub[par][:, k * 128:(k + 1) * 128],
                                                         bass.AP(ln_T, k * T + tc * TCW, [[NT * D, 128], [1, TCW]]),
                                                         start=(k == 0), stop=(k == KT - 1))
                                last.then_inc(pe.h, 1)
                        for tgi in range(NTG):
                            ntt = tg_sizes[tgi]
                            for dc in range(DC):
                                c = tgi * DC + dc
                                dws = dn_waves(ntt)
                                for wi, (ttlo, tthi) in enumerate(dws):
                                    if c == 0 and wi == 0:
                                        te.wait_ge(sc.h, SF((x, IT - 1, NW - 1)))
                                        te.wait_ge(ve.h, VF((x, "h", IT - 1, NW - 1)))
                                    if c > 0:
                                        pdws = dn_waves(tg_sizes[(c - 1) // DC])
                                        te.wait_ge(ve.h, VF((x, "x", c - 1, min(wi, len(pdws) - 1))))
                                    if wi == 0:
                                        te.wait_ge(dW.h, DWF((x, "dn", c, 1)))
                                    last = None
                                    for i in range(IT):
                                        for tt in range(ttlo, tthi):
                                            tok = (tgi * 8 + tt) * 128
                                            last = te.matmul(
                                                ps[tt][:, 0:DCW],
                                                bass.AP(hb, i * T + tok, [[IT * T, 128], [1, 128]]),
                                                wdb[:, i * DCW:(i + 1) * DCW],
                                                start=(i == 0), stop=(i == IT - 1))
                                        if wi == len(dws) - 1 and i == mid_i - 1 and tt == tthi - 1:
                                            last.then_inc(pe.h, 1)
                                    last.then_inc(pe.h, 1)
                    pe.add(npe)

                @blk.scalar
                def _(s):
                    preload_set(s, "silu_and_others")
                    for x in range(NX):
                        for i in range(IT):
                            for w, (tclo, tchi) in enumerate(gu_waves):
                                s.wait_ge(pe.h, PF((x, "gu", i, w)))
                                if i > 0:
                                    s.wait_ge(ve.h, VF((x, "h", i - 1, w)))
                                elif x > 0:
                                    s.wait_ge(ve.h, VF((x - 1, "h", IT - 1, w)))
                                lasts = None
                                for tc in range(tclo, tchi):
                                    lasts = s.activation(sgb[:, tc * TCW:(tc + 1) * TCW],
                                                         ps[tc][:, 0:TCW], AF.Silu)
                                lasts.then_inc(sc.h, 1)
                    sc.add(nsc)

                @blk.vector
                def _(v):
                    for x in range(NX):
                        ex = experts[x]
                        for i in range(IT):
                            for w, (tclo, tchi) in enumerate(gu_waves):
                                v.wait_ge(sc.h, SF((x, i, w)))
                                lastm = None
                                for tc in range(tclo, tchi):
                                    lastm = v.tensor_tensor(
                                        hb[:, i * T + tc * TCW: i * T + (tc + 1) * TCW],
                                        sgb[:, tc * TCW:(tc + 1) * TCW],
                                        ps[TC + tc][:, 0:TCW], op=ALU.mult)
                                lastm.then_inc(ve.h, 1)
                        for tgi in range(NTG):
                            ntt = tg_sizes[tgi]
                            for dc in range(DC):
                                c = tgi * DC + dc
                                dws = dn_waves(ntt)
                                for wi, (ttlo, tthi) in enumerate(dws):
                                    v.wait_ge(pe.h, PF((x, "dn", c, wi)))
                                    lastx = None
                                    for tt in range(ttlo, tthi):
                                        ti = tgi * 8 + tt
                                        if ex["kind"] == "exp":
                                            cap = c_all[:, ti * E + ex["e"]: ti * E + ex["e"] + 1]
                                        else:
                                            cap = 0.5
                                        lastx = v.scalar_tensor_tensor(
                                            x_res[:, ti * D + dc * DCW: ti * D + (dc + 1) * DCW],
                                            ps[tt][:, 0:DCW], cap,
                                            x_res[:, ti * D + dc * DCW: ti * D + (dc + 1) * DCW],
                                            op0=ALU.mult, op1=ALU.add)
                                    if c == NCH - 1 and wi == len(dws) - 1:
                                        v.drain().then_inc(ve.h, 1)
                                    else:
                                        lastx.then_inc(ve.h, 1)
                    ve.add(nve)
        assert pe.n == b_pe + npe and sc.n == b_sc + nsc
        assert ve.n == b_ve + nve and dW.n == b_dW + ndw

    def emit_qkv(ln_T, tag):
        b_pe, b_ve, b_dW, b_dS = pe.base(), ve.base(), dW.base(), dS.base()
        nvb = (NT + 7) // 8
        PE_H = 3 + nvb
        VE_H = 3 + 1
        W = [wq, wk, wv]
        BIAS = [bqs_sb, bk_sb, bv_sb]
        SCALE = [RSQ, 1.0, 1.0]

        def pe_j(h, j):
            return b_pe + h * PE_H + j + 1

        def pe_vt(h, b):
            return b_pe + h * PE_H + 3 + b + 1

        def ve_j(h, j):
            return b_ve + VE_H * h + j + 1

        def ve_vt(h):
            return b_ve + VE_H * h + 4

        def dw_j(h, j):
            return b_dW + (3 * h + j + 1) * 16

        with ExitStack() as qctx:
            wjb = [sbuf("wj%d_%s" % (j, tag), [128, 2 * DT * 128], BF16, qctx) for j in range(3)]
            js_sb = [sbuf("js%d_%s" % (j, tag), [128, T], BF16, qctx) for j in range(3)]
            vts = sbuf("vts_" + tag, [128, NT * 128], BF16, qctx)
            psq = [psum("qp%d_%s" % (i, tag), [128, 512], F32, qctx) for i in range(4)]
            psb = [psum("qb%d_%s" % (i, tag), [128, 1024], BF16, qctx) for i in range(2)]

            with nc.Block() as blk:
                @blk.gpsimd
                def _(gp):
                    for h in range(H):
                        for j in range(3):
                            buf = wjb[j]
                            par = h % 2
                            if h >= 1:
                                gp.wait_ge(pe.h, pe_j(h - 1, 0))
                            gp.dma_start(
                                bass.AP(buf, par * DT * 128, [[2 * DT * 128, 128], [1, DT * 128]]),
                                bass.AP(W[j], h * 128 * D, [[D, 128], [1, D]]),
                            ).then_inc(dW.h, 16)
                    dW.add(3 * H * 16)

                @blk.tensor
                def _(te):
                    for h in range(H):
                        te.wait_ge(dW.h, b_dW + 3 * (h + 1) * 16)
                        for j in range(3):
                            par = h % 2
                            if j > 0:
                                te.wait_ge(ve.h, ve_j(h, j - 1))
                            elif h > 0:
                                te.wait_ge(ve.h, ve_j(h - 1, 2))
                            last = None
                            for k in range(DT):
                                for tc in range(TC):
                                    last = te.matmul(
                                        psq[tc][:, 0:TCW],
                                        wjb[j][:, (par * DT + k) * 128:(par * DT + k + 1) * 128],
                                        bass.AP(ln_T, k * T + tc * TCW, [[NT * D, 128], [1, TCW]]),
                                        start=(k == 0), stop=(k == DT - 1))
                            last.then_inc(pe.h, 1)
                        te.wait_ge(ve.h, ve_j(h, 2))
                        if h > 0:
                            te.wait_ge(ve.h, ve_vt(h - 1))
                        for b in range(nvb):
                            lo, hi = b * 8, min(NT, b * 8 + 8)
                            last = None
                            for tti in range(lo, hi):
                                last = te.transpose(psb[b % 2][:, (tti - lo) * 128:(tti - lo + 1) * 128],
                                                    js_sb[2][:, tti * 128:(tti + 1) * 128],
                                                    id_sb[:, :])
                            last.then_inc(pe.h, 1)
                    pe.add(H * PE_H)

                @blk.vector
                def _(v):
                    for h in range(H):
                        if h > 0:
                            v.wait_ge(dS.h, b_dS + 3 * h * 16)
                        for j in range(3):
                            v.wait_ge(pe.h, pe_j(h, j))
                            lastq = None
                            for tc in range(TC):
                                if SCALE[j] == 1.0:
                                    lastq = v.tensor_scalar_add(js_sb[j][:, tc * TCW:(tc + 1) * TCW],
                                                                psq[tc][:, 0:TCW], BIAS[j][:, h:h + 1])
                                else:
                                    lastq = v.tensor_scalar(js_sb[j][:, tc * TCW:(tc + 1) * TCW],
                                                            psq[tc][:, 0:TCW], SCALE[j],
                                                            BIAS[j][:, h:h + 1],
                                                            op0=ALU.mult, op1=ALU.add)
                            lastq.then_inc(ve.h, 1)
                        lastv = None
                        for b in range(nvb):
                            lo, hi = b * 8, min(NT, b * 8 + 8)
                            v.wait_ge(pe.h, pe_vt(h, b))
                            lastv = v.tensor_copy(vts[:, lo * 128:lo * 128 + (hi - lo) * 128],
                                                  psb[b % 2][:, 0:(hi - lo) * 128])
                        lastv.then_inc(ve.h, 1)
                    ve.add(H * VE_H)

                @blk.sync
                def _(s):
                    for h in range(H):
                        for j in range(3):
                            s.wait_ge(ve.h, ve_j(h, j))
                            if j == 0:
                                s.dma_start(bass.AP(q_hbm, h * 128 * T, [[T, 128], [1, T]]),
                                            js_sb[0][:, :]).then_inc(dS.h, 16)
                            elif j == 1:
                                s.dma_start(bass.AP(kv_send[h // HG], (h % HG) * 128 * T,
                                                    [[T, 128], [1, T]]),
                                            js_sb[1][:, :]).then_inc(dS.h, 16)
                            else:
                                s.wait_ge(ve.h, ve_vt(h))
                                s.dma_start(
                                    bass.AP(kv_send[h // HG], KSZ + (h % HG) * T * 128,
                                            [[128, 128], [128 * 128, NT], [1, 128]]),
                                    bass.AP(vts, 0, [[NT * 128, 128], [128, NT], [1, 128]]),
                                ).then_inc(dS.h, 16)
                    s.wait_ge(dS.h, dS.n + 3 * H * 16)
                    dS.add(3 * H * 16)
        assert pe.n == b_pe + H * PE_H and ve.n == b_ve + H * VE_H

        with nc.Block() as blk:
            @blk.gpsimd
            def _(gp):
                for i in range(NCC):
                    gp.collective_compute(
                        "AllGather", ALU.bypass,
                        ins=[kv_send[i].ap().opt()],
                        outs=[kv_gath[i].ap().opt()],
                        replica_groups=g.get("RG") or [[0, 1], [2, 3], [4, 5], [6, 7]],
                    ).then_inc(ccS.h)
                    ccS.add(1)
                    if i < NCC - 1:
                        gp.wait_ge(ccS.h, ccS.n)

    def emit_attn(tag):
        SEQ_TOT = 2 * T
        assert NT % 2 == 0
        b_pe, b_sc, b_ve, b_dS = pe.base(), sc.base(), ve.base(), dS.base()
        cc_base = ccS.n - NCC
        pos = []
        for i in range(NT):
            ext = 256 * (i + 1)
            cw = chunk_widths(ext, 512)
            ntr = ext // 128
            nb = (ntr + 7) // 8
            pos.append((ext, cw, ntr, nb))

        def prev3(i):
            return NT - 1 - ((NT - 1 - i) % 3)

        pe_i, sc_i, ve_i = {}, {}, {}
        npe = nsc = nve = 0
        for h in range(H):
            for r in range(NT + 2):
                if r < NT:
                    for ci in range(len(pos[r][1])):
                        npe += 1; pe_i[(h, "s", r, ci)] = npe
                j = r - 2
                if 0 <= j < NT:
                    for b in range(pos[j][3]):
                        npe += 1; pe_i[(h, "t", j, b)] = npe
                    npe += 1; pe_i[(h, "av", j)] = npe
            for i in range(NT):
                for ci in range(len(pos[i][1])):
                    nsc += 1; sc_i[(h, "ev", i, ci)] = nsc
                if i >= 2:
                    nsc += 1; sc_i[(h, "z", i - 2)] = nsc
                nsc += 1; sc_i[(h, "exp", i)] = nsc
            nsc += 1; sc_i[(h, "z", NT - 2)] = nsc
            nsc += 1; sc_i[(h, "z", NT - 1)] = nsc
            for i in range(NT):
                if i >= 2:
                    for b in range(pos[i - 2][3]):
                        nve += 1; ve_i[(h, "te", i - 2, b)] = nve
                nve += 1; ve_i[(h, "nm", i)] = nve
                nve += 1; ve_i[(h, "sca", i)] = nve
            for j in (NT - 2, NT - 1):
                for b in range(pos[j][3]):
                    nve += 1; ve_i[(h, "te", j, b)] = nve

        def P(k):
            return b_pe + pe_i[k]

        def S(k):
            return b_sc + sc_i[k]

        def V(k):
            return b_ve + ve_i[k]

        def DSH(h):
            return b_dS + (h + 1) * 6 * 16

        with ExitStack() as actx:
            k_sb = sbuf("k_" + tag, [128, SEQ_TOT], BF16, actx)
            vts = sbuf("vt_" + tag, [128, SEQ_TOT], BF16, actx)
            q_sb = sbuf("q_" + tag, [128, T], BF16, actx)
            rows = [sbuf("rows%d_%s" % (i, tag), [128, SEQ_TOT], F32, actx) for i in range(2)]
            att = [sbuf("att%d_%s" % (i, tag), [128, SEQ_TOT], BF16, actx) for i in range(3)]
            attT = sbuf("attT_" + tag, [128, SEQ_TOT], BF16, actx)
            zall = sbuf("zall_" + tag, [128, T], BF16, actx)
            pss = [psum("ap%d_%s" % (i, tag), [128, 512], F32, actx) for i in range(4)]
            psz = psum("az_" + tag, [128, 512], F32, actx)
            psb = [psum("ab%d_%s" % (i, tag), [128, 1024], BF16, actx) for i in range(2)]
            NMC = [7, 8]
            DENC = [10, 12]
            RIC = [11, 13]

            def stp(cols, i):
                c = cols[i % 2]
                return st[:, c:c + 1]

            def emit_te_evac(v, h, j):
                for b in range(pos[j][3]):
                    lo, hi = b * 8, min(pos[j][2], b * 8 + 8)
                    v.wait_ge(pe.h, P((h, "t", j, b)))
                    v.tensor_copy(attT[:, lo * 128:lo * 128 + (hi - lo) * 128],
                                  psb[b % 2][:, 0:(hi - lo) * 128]).then_inc(ve.h, 1)

            with nc.Block() as blk:
                @blk.sync
                def _(s):
                    for h in range(H):
                        if h == 0:
                            s.wait_ge(ccS.h, cc_base + 1)
                        elif h == HG:
                            s.wait_ge(ccS.h, cc_base + NCC)
                        if h > 0:
                            s.wait_ge(sc.h, S((h - 1, "z", NT - 1)))
                        gb = kv_gath[h // HG]
                        hh = h % HG
                        for r in range(2):
                            s.dma_start(
                                bass.AP(k_sb, r * 128, [[SEQ_TOT, 128], [256, NT], [1, 128]]),
                                bass.AP(gb, r * 2 * KSZ + hh * 128 * T,
                                        [[T, 128], [128, NT], [1, 128]]),
                            ).then_inc(dS.h, 16)
                            s.dma_start(
                                bass.AP(vts, r * 128, [[SEQ_TOT, 128], [256, NT], [1, 128]]),
                                bass.AP(gb, r * 2 * KSZ + KSZ + hh * T * 128,
                                        [[128, 128], [128 * 128, NT], [1, 128]]),
                            ).then_inc(dS.h, 16)
                        s.dma_start(q_sb[:, :],
                                    bass.AP(q_hbm, h * 128 * T, [[T, 128], [1, T]])
                                    ).then_inc(dS.h, 16)
                        s.wait_ge(sc.h, S((h, "z", NT - 1)))
                        s.dma_start(bass.AP(z_hbm, h * 128 * T, [[T, 128], [1, T]]),
                                    zall[:, :]).then_inc(dS.h, 16)
                        s.wait_ge(dS.h, DSH(h))
                    dS.add(H * 6 * 16)

                @blk.tensor
                def _(te):
                    for h in range(H):
                        te.wait_ge(dS.h, DSH(h) - 16)
                        for r in range(NT + 2):
                            if r < NT:
                                ext, cw, ntr, nb = pos[r]
                                base_bank = (r % 2) * 2
                                off = 0
                                for ci, w in enumerate(cw):
                                    if ci >= 2:
                                        te.wait_ge(sc.h, S((h, "ev", r, ci - 2)))
                                    elif r >= 2:
                                        te.wait_ge(sc.h, S((h, "ev", r - 2, len(pos[r - 2][1]) - 1)))
                                    elif h > 0:
                                        te.wait_ge(sc.h, S((h - 1, "ev", NT - 2 + r, len(pos[NT - 2 + r][1]) - 1)))
                                    te.matmul(pss[base_bank + ci % 2][:, 0:w],
                                              q_sb[:, r * 128:(r + 1) * 128],
                                              k_sb[:, off:off + w],
                                              start=True, stop=True).then_inc(pe.h, 1)
                                    off += w
                            j = r - 2
                            if 0 <= j < NT:
                                ext, cw, ntr, nb = pos[j]
                                te.wait_ge(ve.h, V((h, "sca", j)))
                                for b in range(nb):
                                    lo, hi = b * 8, min(ntr, b * 8 + 8)
                                    if b >= 2:
                                        te.wait_ge(ve.h, V((h, "te", j, b - 2)))
                                    elif j > 0:
                                        te.wait_ge(ve.h, V((h, "te", j - 1, pos[j - 1][3] - 1)))
                                    elif h > 0:
                                        te.wait_ge(ve.h, V((h - 1, "te", NT - 1, pos[NT - 1][3] - 1)))
                                    last = None
                                    for kt in range(lo, hi):
                                        last = te.transpose(
                                            psb[b % 2][:, (kt - lo) * 128:(kt - lo + 1) * 128],
                                            att[j % 3][:, kt * 128:(kt + 1) * 128], id_sb[:, :])
                                    last.then_inc(pe.h, 1)
                                te.wait_ge(ve.h, V((h, "te", j, nb - 1)))
                                if j > 0:
                                    te.wait_ge(sc.h, S((h, "z", j - 1)))
                                elif h > 0:
                                    te.wait_ge(sc.h, S((h - 1, "z", NT - 1)))
                                last = None
                                for kt in range(ntr):
                                    last = te.matmul(psz[:, 0:128],
                                                     vts[:, kt * 128:(kt + 1) * 128],
                                                     attT[:, kt * 128:(kt + 1) * 128],
                                                     start=(kt == 0), stop=(kt == ntr - 1))
                                last.then_inc(pe.h, 1)
                    pe.add(npe)

                @blk.scalar
                def _(s):
                    preload_set(s, "natural_log_exp_and_others")
                    for h in range(H):
                        for i in range(NT):
                            ext, cw, ntr, nb = pos[i]
                            base_bank = (i % 2) * 2
                            off = 0
                            for ci, w in enumerate(cw):
                                s.wait_ge(pe.h, P((h, "s", i, ci)))
                                if ci == 0 and i >= 2:
                                    s.wait_ge(ve.h, V((h, "nm", i - 2)))
                                s.activation(rows[i % 2][:, off:off + w],
                                             pss[base_bank + ci % 2][:, 0:w],
                                             AF.Copy).then_inc(sc.h, 1)
                                off += w
                            if i >= 2:
                                s.wait_ge(pe.h, P((h, "av", i - 2)))
                                s.activation(zall[:, (i - 2) * 128:(i - 1) * 128],
                                             psz[:, 0:128], AF.Copy).then_inc(sc.h, 1)
                            elif i == 0 and h > 0:
                                s.wait_ge(dS.h, DSH(h - 1))
                            s.wait_ge(ve.h, V((h, "nm", i)))
                            if i >= 3:
                                s.wait_ge(pe.h, P((h, "t", i - 3, pos[i - 3][3] - 1)))
                            elif h > 0:
                                pj = prev3(i)
                                s.wait_ge(pe.h, P((h - 1, "t", pj, pos[pj][3] - 1)))
                            s.drain()
                            s.activation(att[i % 3][:, 0:ext], rows[i % 2][:, 0:ext], AF.Exp,
                                         bias=stp(NMC, i),
                                         accum_out=stp(DENC, i)).then_inc(sc.h, 1)
                        for j in (NT - 2, NT - 1):
                            s.wait_ge(pe.h, P((h, "av", j)))
                            s.activation(zall[:, j * 128:(j + 1) * 128],
                                         psz[:, 0:128], AF.Copy).then_inc(sc.h, 1)
                    sc.add(nsc)

                @blk.vector
                def _(v):
                    for h in range(H):
                        for i in range(NT):
                            ext, cw, ntr, nb = pos[i]
                            if i >= 2:
                                emit_te_evac(v, h, i - 2)
                            v.wait_ge(sc.h, S((h, "ev", i, len(cw) - 1)))
                            v.tensor_add(rows[i % 2][:, ext - 256:ext],
                                         rows[i % 2][:, ext - 256:ext], mask_sb[:, :])
                            v.drain()
                            v.reduce_max(stp(NMC, i), rows[i % 2][:, 0:ext], axis=AX.X,
                                         negate=True).then_inc(ve.h, 1)
                            v.wait_ge(sc.h, S((h, "exp", i)))
                            v.reciprocal(stp(RIC, i), stp(DENC, i))
                            v.drain()
                            v.tensor_scalar_mul(att[i % 3][:, 0:ext], att[i % 3][:, 0:ext],
                                                stp(RIC, i)).then_inc(ve.h, 1)
                        for j in (NT - 2, NT - 1):
                            emit_te_evac(v, h, j)
                    ve.add(nve)
        assert pe.n == b_pe + npe and sc.n == b_sc + nsc and ve.n == b_ve + nve

    def emit_oproj(tag):
        b_pe, b_sc, b_ve, b_dS, b_dW = pe.base(), sc.base(), ve.base(), dS.base(), dW.base()
        NG8 = (NT + 7) // 8
        with ExitStack() as octx:
            wob = sbuf("wo_" + tag, [128, H * D], BF16, octx)
            zt8 = sbuf("zt_" + tag, [128, H * 8 * 128], BF16, octx)
            bob = sbuf("bo_" + tag, [128, D], F32, octx)
            tmp = [sbuf("otmp%d_%s" % (i, tag), [128, 512], F32, octx) for i in range(2)]
            pso = [psum("op%d_%s" % (i, tag), [128, 512], F32, octx) for i in range(min(4, NT * DC))]
            nps = len(pso)

            with nc.Block() as blk:
                @blk.gpsimd
                def _(gp):
                    gp.dma_start(
                        bass.AP(wob, 0, [[H * D, 128], [D, H], [1, D]]),
                        bass.AP(wo, 0, [[D, 128], [128 * D, H], [1, D]]),
                    ).then_inc(dW.h, 16)
                    dW.add(16)

                @blk.sync
                def _(s):
                    s.dma_start(bob[:, :], bass.AP(bo, 0, [[0, 128], [1, D]])).then_inc(dS.h, 16)
                    s.wait_ge(dS.h, b_dS + 16)
                    for gp8 in range(NG8):
                        ntt = min(8, NT - gp8 * 8)
                        if gp8 > 0:
                            s.wait_ge(pe.h, b_pe + gp8 * 8 * DC)
                        s.dma_start(
                            bass.AP(zt8, 0, [[H * 8 * 128, 128], [8 * 128, H], [1, ntt * 128]]),
                            bass.AP(z_hbm, gp8 * 8 * 128, [[T, 128], [128 * T, H], [1, ntt * 128]]),
                        ).then_inc(dS.h, 16)
                        s.wait_ge(dS.h, b_dS + (gp8 + 2) * 16)
                    dS.add((NG8 + 1) * 16)

                @blk.tensor
                def _(te):
                    te.wait_ge(dW.h, b_dW + 16)
                    for gp8 in range(NG8):
                        ntt = min(8, NT - gp8 * 8)
                        te.wait_ge(dS.h, b_dS + (gp8 + 2) * 16)
                        for tt in range(ntt):
                            t = gp8 * 8 + tt
                            for dc in range(DC):
                                if t * DC + dc >= nps:
                                    te.wait_ge(ve.h, b_ve + t * DC + dc - nps + 1)
                                last = None
                                for h in range(H):
                                    last = te.matmul(pso[(t * DC + dc) % nps][:, 0:DCW],
                                                     zt8[:, h * 8 * 128 + tt * 128: h * 8 * 128 + (tt + 1) * 128],
                                                     wob[:, h * D + dc * DCW: h * D + (dc + 1) * DCW],
                                                     start=(h == 0), stop=(h == H - 1))
                                last.then_inc(pe.h, 1)
                    pe.add(NT * DC)

                @blk.vector
                def _(v):
                    v.wait_ge(dS.h, b_dS + 16)
                    for t in range(NT):
                        for dc in range(DC):
                            gi = t * DC + dc
                            v.wait_ge(pe.h, b_pe + gi + 1)
                            v.tensor_add(tmp[gi % 2][:, 0:DCW], pso[gi % nps][:, 0:DCW],
                                         bob[:, dc * DCW:(dc + 1) * DCW])
                            v.drain()
                            v.tensor_add(x_res[:, t * D + dc * DCW: t * D + (dc + 1) * DCW],
                                         x_res[:, t * D + dc * DCW: t * D + (dc + 1) * DCW],
                                         tmp[gi % 2][:, 0:DCW]).then_inc(ve.h, 1)
                    ve.add(NT * DC)

    def emit_out():
        b_sc, b_dS = sc.base(), dS.base()
        with ExitStack() as octx:
            of = [sbuf("of%d" % i, [128, D], F32, octx) for i in range(2)]
            with nc.Block() as blk:
                @blk.scalar
                def _(s):
                    for t in range(NT):
                        if t >= 2:
                            s.wait_ge(dS.h, b_dS + (t - 1) * 16)
                        s.activation(of[t % 2][:, :], x_res[:, t * D:(t + 1) * D],
                                     AF.Copy).then_inc(sc.h, 1)
                    sc.add(NT)

                @blk.sync
                def _(s):
                    for t in range(NT):
                        s.wait_ge(sc.h, b_sc + t + 1)
                        s.dma_start(bass.AP(y_out, t * 128 * D, [[D, 128], [1, D]]),
                                    of[t % 2][:, :]).then_inc(dS.h, 16)
                        s.wait_ge(dS.h, b_dS + (t + 1) * 16)
                    dS.add(NT * 16)

    def emit_dump(dst):
        b_sc, b_dS = sc.base(), dS.base()
        with ExitStack() as octx:
            of = [sbuf("dof%d_%s" % (i, dst.name), [128, D], F32, octx) for i in range(2)]
            with nc.Block() as blk:
                @blk.scalar
                def _(s):
                    for t in range(NT):
                        if t >= 2:
                            s.wait_ge(dS.h, b_dS + (t - 1) * 16)
                        s.activation(of[t % 2][:, :], x_res[:, t * D:(t + 1) * D],
                                     AF.Copy).then_inc(sc.h, 1)
                    sc.add(NT)

                @blk.sync
                def _(s):
                    for t in range(NT):
                        s.wait_ge(sc.h, b_sc + t + 1)
                        s.dma_start(bass.AP(dst, t * 128 * D, [[D, 128], [1, D]]),
                                    of[t % 2][:, :]).then_inc(dS.h, 16)
                        s.wait_ge(dS.h, b_dS + (t + 1) * 16)
                    dS.add(NT * 16)

    STOP = g.get("STOP")
    with ExitStack() as inv1:
        ln_T = sbuf("lnT1", [128, NT * D], BF16, inv1)
        emit_ln(ln_T, "l1")
        if DBG:
            emit_sbuf_dump(dbg_lnT, ln_T, NT * D)
            emit_sbuf_dump(dbg_st, st, 32)
        if STOP == "ln1":
            return nc
        emit_routing(ln_T, "r1")
        if DBG:
            emit_sbuf_dump(dbg_c, c_all, NT * E)
        if STOP == "route1":
            return nc
        emit_ffn(ln_T, "f1")
        if DBG:
            emit_sbuf_dump(dbg_h, hb_last[0], IT * T)
    if DBG >= 1:
        emit_dump(dbg_outs[0])
    if STOP == "ffn1":
        return nc
    with ExitStack() as att:
        ln_T = sbuf("lnT2", [128, NT * D], BF16, att)
        emit_ln(ln_T, "l2")
        emit_qkv(ln_T, "qk")
    emit_attn("at")
    emit_oproj("op")
    if DBG >= 2:
        emit_dump(dbg_outs[1])
    with ExitStack() as inv2:
        ln_T = sbuf("lnT3", [128, NT * D], BF16, inv2)
        emit_ln(ln_T, "l3")
        emit_routing(ln_T, "r3")
        emit_ffn(ln_T, "f3")
    emit_out()
    return nc


def make_inputs(core, full, g):
    D, T, I, E, H = g["D"], g["T"], g["I"], g["E"], g["H"]
    NT = T // 128
    b, r = core // 2, core % 2
    xb = np.ascontiguousarray(
        full["L0_attn_x"][b].reshape(2 * NT, 128, D)[r::2])
    mask = np.where(np.arange(256)[None, :] <= 128 * r + np.arange(128)[:, None],
                    np.float32(0), np.float32(-1e9 if False else -100000.0)).astype(np.float32)
    E_, I_, H_ = g["E"], g["I"], g["H"]
    DT_, IT_ = D // 128, I_ // 128

    def gu5(expert_mat, shared_mat):
        egT = expert_mat.transpose(0, 2, 1)
        e5 = egT.reshape(E_, DT_, 128, IT_, 128).transpose(0, 3, 4, 1, 2)
        e5 = egT.reshape(E_, DT_, 128, IT_, 128).transpose(0, 3, 2, 1, 4)
        e5 = e5.reshape(E_, IT_, 128, DT_ * 128)
        shT = shared_mat.T
        s5 = shT.reshape(DT_, 128, 2, IT_, 128).transpose(2, 3, 1, 0, 4)
        s5 = s5.reshape(2, IT_, 128, DT_ * 128)
        return np.ascontiguousarray(np.concatenate([e5, s5], axis=0))

    def qkv4(w):
        return np.ascontiguousarray(
            w.reshape(H_, DT_, 128, 128).transpose(0, 2, 1, 3).reshape(H_, 128, D))

    im = dict(
        x_in=xb,
        wq=qkv4(full["L0_attn_W_Q"]), wk=qkv4(full["L0_attn_W_K"]),
        wv=qkv4(full["L0_attn_W_V"]),
        wo=full["L0_attn_W_O"],
        bqT=np.ascontiguousarray(full["L0_attn_b_Q"].T),
        bkT=np.ascontiguousarray(full["L0_attn_b_K"].T),
        bvT=np.ascontiguousarray(full["L0_attn_b_V"].T),
        bo=full["L0_attn_b_O"],
        gate_wT=np.ascontiguousarray(full["L0_ffn_gate_w"].T),
        wg5=gu5(full["L0_ffn_expert_gate"], full["L0_ffn_shared_gate"]),
        wu5=gu5(full["L0_ffn_expert_up"], full["L0_ffn_shared_up"]),
        wed_T=np.ascontiguousarray(full["L0_ffn_expert_down"].transpose(0, 2, 1)),
        wsd_T=np.ascontiguousarray(full["L0_ffn_shared_down"].T),
        idm=np.eye(128, dtype=ml_dtypes.bfloat16),
        amask=mask,
    )
    return im


def kernel(**inputs):
    g = FULL_GEO
    D, T = g["D"], g["T"]
    NT = T // 128
    nc = build_nc(g)
    shared = None
    in_maps = []
    for c in range(8):
        im = make_inputs(c, inputs, g)
        in_maps.append(im)
    res = run_bass_kernel_spmd(nc, in_maps, core_ids=list(range(8)))
    out = np.zeros((4, 4096, D), np.float32)
    for c in range(8):
        b, r = c // 2, c % 2
        y = res.results[c]["y_out"]
        out[b].reshape(2 * NT, 128, D)[r::2] = y
    return out


# version 09
# speedup vs baseline: 1.1973x; 1.1240x over previous
import numpy as np
import ml_dtypes
from contextlib import ExitStack

import concourse.bass as bass
import concourse.mybir as mybir
from concourse.bass_utils import run_bass_kernel_spmd

F32 = mybir.dt.float32
BF16 = mybir.dt.bfloat16
AF = mybir.ActivationFunctionType
ALU = mybir.AluOpType
AX = mybir.AxisListType

RSQ = float(1.0 / np.sqrt(np.float32(128.0)))
LN_EPS = 1e-5

FULL_GEO = dict(D=2048, T=2048, I=1408, E=8, H=16)


class Sem:
    def __init__(self, ctx, nc, name):
        self.h = ctx.enter_context(nc.semaphore(name))
        self.n = 0

    def base(self):
        return self.n

    def add(self, amt=1):
        self.n += amt
        return self.n


def build_nc(g):
    D, T, I, E, H = g["D"], g["T"], g["I"], g["E"], g["H"]
    ISH = 2 * I
    DT = D // 128
    NT = T // 128
    IT = I // 128
    TC = max(1, T // 512)
    TCW = min(T, 512)
    NEXP = E + 2
    NTG = (NT + 7) // 8
    DC = max(1, D // 512)
    DCW = min(D, 512)
    assert D % 128 == 0 and T % 128 == 0 and I % 128 == 0 and T % TCW == 0 and D % DCW == 0

    nc = bass.Bass("TRN2", target_bir_lowering=False)

    x_in = nc.dram_tensor("x_in", [NT, 128, D], F32, kind="ExternalInput")
    wq = nc.dram_tensor("wq", [H, 128, D], F32, kind="ExternalInput")
    wk = nc.dram_tensor("wk", [H, 128, D], F32, kind="ExternalInput")
    wv = nc.dram_tensor("wv", [H, 128, D], F32, kind="ExternalInput")
    wo = nc.dram_tensor("wo", [H, 128, D], F32, kind="ExternalInput")
    bqT = nc.dram_tensor("bqT", [128, H], F32, kind="ExternalInput")
    bkT = nc.dram_tensor("bkT", [128, H], F32, kind="ExternalInput")
    bvT = nc.dram_tensor("bvT", [128, H], F32, kind="ExternalInput")
    bo = nc.dram_tensor("bo", [D], F32, kind="ExternalInput")
    gate_wT = nc.dram_tensor("gate_wT", [D, E], F32, kind="ExternalInput")
    wg5 = nc.dram_tensor("wg5", [E + 2, IT, 128, DT * 128], F32, kind="ExternalInput")
    wu5 = nc.dram_tensor("wu5", [E + 2, IT, 128, DT * 128], F32, kind="ExternalInput")
    wed_T = nc.dram_tensor("wed_T", [E, I, D], F32, kind="ExternalInput")
    wsd_T = nc.dram_tensor("wsd_T", [ISH, D], F32, kind="ExternalInput")
    idm = nc.dram_tensor("idm", [128, 128], BF16, kind="ExternalInput")
    amask = nc.dram_tensor("amask", [128, 256], F32, kind="ExternalInput")
    y_out = nc.dram_tensor("y_out", [NT, 128, D], F32, kind="ExternalOutput")
    DBG = g.get("DBG", 0)
    dbg_outs = [nc.dram_tensor("dbg%d" % i, [NT, 128, D], F32, kind="ExternalOutput")
                for i in range(DBG)]
    if DBG:
        dbg_lnT = nc.dram_tensor("dbg_lnT", [128, NT * D], BF16, kind="ExternalOutput")
        dbg_c = nc.dram_tensor("dbg_c", [128, NT * E], F32, kind="ExternalOutput")
        dbg_h = nc.dram_tensor("dbg_h", [128, IT * T], BF16, kind="ExternalOutput")
        dbg_st = nc.dram_tensor("dbg_st", [128, 32], F32, kind="ExternalOutput")

    def emit_sbuf_dump(dst, src_sb, cols):
        with nc.Block() as blk:
            @blk.sync
            def _(s):
                s.dma_start(bass.AP(dst, 0, [[cols, 128], [1, cols]]),
                            src_sb[:, 0:cols]).then_inc(dS.h, 16)
                s.wait_ge(dS.h, dS.add(16))

    HG = max(1, H // 2)
    NCC = (H + HG - 1) // HG
    KSZ = HG * 128 * T
    kv_send = [nc.dram_tensor("kv_send%d" % i, [2 * KSZ // 512, 512], BF16)
               for i in range(NCC)]
    kv_gath = [nc.dram_tensor("kv_gath%d" % i, [4 * KSZ // 512, 512], BF16)
               for i in range(NCC)]
    q_hbm = nc.dram_tensor("q_hbm", [H, 128, T], BF16)
    z_hbm = nc.dram_tensor("z_hbm", [H, 128, T], BF16)

    ctx = ExitStack()
    nc._kernel_ctx = ctx

    def sbuf(name, shape, dt, scope=None):
        return (scope or ctx).enter_context(nc.sbuf_tensor(name, shape, dt))

    def psum(name, shape, dt, scope=None):
        return (scope or ctx).enter_context(nc.psum_tensor(name, shape, dt))

    x_res = sbuf("x_res", [128, NT * D], BF16)
    id_sb = sbuf("id_sb", [128, 128], BF16)
    st = sbuf("st", [128, 32], F32)
    c_all = sbuf("c_all", [128, NT * E], F32)
    bq_sb = sbuf("bq_sb", [128, H], F32)
    bk_sb = sbuf("bk_sb", [128, H], F32)
    bv_sb = sbuf("bv_sb", [128, H], F32)
    bqs_sb = sbuf("bqs_sb", [128, H], F32)
    gw_sb = sbuf("gw_sb", [128, DT * E], BF16)
    mask_sb = sbuf("mask_sb", [128, 256], F32)

    sems = ExitStack()
    ctx.enter_context(sems)
    dW = Sem(ctx, nc, "dW")
    dS = Sem(ctx, nc, "dS")
    pe = Sem(ctx, nc, "pe")
    sc = Sem(ctx, nc, "sc")
    ve = Sem(ctx, nc, "ve")
    ccS = Sem(ctx, nc, "ccS")

    COL = dict(sum=0, sq=1, mean=2, e2=3, var=4, rstd=5, mr=6,
               negmax=7, m1=8, m2=9, den=10, rinv=11)

    def stc(name):
        c = COL[name]
        return st[:, c:c + 1]

    from concourse.hw_specs import get_activation_tables
    _tabs = list(get_activation_tables(nc.m.arch).keys())

    def preload_set(s, name):
        s.add_instruction(mybir.InstLoadActFuncSet(
            name=nc.get_next_instruction_name(),
            act_func_set_id=_tabs.index(name), ins=[], outs=[]))

    def chunk_widths(total, w):
        out = []
        o = 0
        while o < total:
            out.append(min(w, total - o))
            o += w
        return out

    with nc.Block() as blk:
        @blk.sync
        def _(s):
            s.dma_start(id_sb[:, :], idm[:, :]).then_inc(dS.h, 16); dS.add(16)
            s.dma_start(mask_sb[:, :], amask[:, :]).then_inc(dS.h, 16); dS.add(16)
            s.dma_start(bq_sb[:, :], bqT[:, :]).then_inc(dS.h, 16); dS.add(16)
            s.dma_start(bk_sb[:, :], bkT[:, :]).then_inc(dS.h, 16); dS.add(16)
            s.dma_start(bv_sb[:, :], bvT[:, :]).then_inc(dS.h, 16); dS.add(16)
            s.wait_ge(dS.h, dS.n)

        @blk.gpsimd
        def _(gp):
            gp.dma_start(
                bass.AP(x_res, 0, [[NT * D, 128], [D, NT], [1, D]]),
                bass.AP(x_in, 0, [[D, 128], [128 * D, NT], [1, D]]),
            ).then_inc(dW.h, 16); dW.add(16)
            gp.dma_start(
                bass.AP(gw_sb, 0, [[DT * E, 128], [E, DT], [1, E]]),
                bass.AP(gate_wT, 0, [[E, 128], [128 * E, DT], [1, E]]),
            ).then_inc(dW.h, 16); dW.add(16)
            gp.wait_ge(dW.h, dW.n)

        @blk.vector
        def _(v):
            v.wait_ge(dS.h, dS.n)
            v.tensor_scalar_mul(bqs_sb[:, :], bq_sb[:, :], RSQ)

    def emit_ln(ln_T, tag):
        nbat = (DT + 7) // 8
        b_sc, b_ve, b_pe = sc.base(), ve.base(), pe.base()
        with ExitStack() as lctx:
            dump = sbuf("lndump_" + tag, [128, D], BF16, lctx)
            lntok = sbuf("lntok_" + tag, [128, D], BF16, lctx)
            psb = [psum("lnp%d_%s" % (i, tag), [128, 1024], BF16, lctx) for i in range(2)]
            with nc.Block() as blk:
                @blk.scalar
                def _(s):
                    preload_set(s, "natural_log_exp_and_others")
                    for t in range(NT):
                        if t > 0:
                            s.wait_ge(ve.h, b_ve + 3 * (t - 1) + 1)
                        s.activation(dump[:, :], x_res[:, t * D:(t + 1) * D],
                                     AF.Square, accum_out=stc("sq")).then_inc(sc.h, 1)
                        s.wait_ge(ve.h, b_ve + 3 * t + 1)
                        s.activation(stc("rstd"), stc("var"), AF.Ln)
                        s.drain()
                        s.activation(stc("rstd"), stc("rstd"), AF.Exp,
                                     scale=-0.5).then_inc(sc.h, 1)
                sc.add(2 * NT)

                @blk.vector
                def _(v):
                    for t in range(NT):
                        v.reduce_sum(stc("sum"), x_res[:, t * D:(t + 1) * D], axis=AX.X)
                        v.wait_ge(sc.h, b_sc + 2 * t + 1)
                        v.drain()
                        v.tensor_scalar_mul(stc("mean"), stc("sum"), 1.0 / D)
                        v.tensor_scalar_mul(stc("e2"), stc("sq"), 1.0 / D)
                        v.drain()
                        v.tensor_mul(stc("var"), stc("mean"), stc("mean"))
                        v.drain()
                        v.tensor_sub(stc("var"), stc("e2"), stc("var"))
                        v.drain()
                        v.tensor_scalar_add(stc("var"), stc("var"),
                                            LN_EPS).then_inc(ve.h, 1)
                        v.wait_ge(sc.h, b_sc + 2 * t + 2)
                        v.tensor_mul(stc("mr"), stc("mean"), stc("rstd"))
                        v.drain()
                        if t > 0:
                            v.wait_ge(pe.h, b_pe + nbat * t)
                        v.tensor_scalar(lntok[:, :], x_res[:, t * D:(t + 1) * D],
                                        stc("rstd"), stc("mr"),
                                        op0=ALU.mult, op1=ALU.subtract).then_inc(ve.h, 1)
                        lastc = None
                        for b in range(nbat):
                            lo, hi = b * 8, min(DT, b * 8 + 8)
                            v.wait_ge(pe.h, b_pe + nbat * t + b + 1)
                            lastc = v.tensor_copy(
                                bass.AP(ln_T, lo * T + t * 128,
                                        [[NT * D, 128], [T, hi - lo], [1, 128]]),
                                psb[b % 2][:, 0:(hi - lo) * 128])
                        lastc.then_inc(ve.h, 1)
                ve.add(3 * NT)

                @blk.tensor
                def _(te):
                    for t in range(NT):
                        te.wait_ge(ve.h, b_ve + 3 * t + 2)
                        if t > 0:
                            te.wait_ge(ve.h, b_ve + 3 * (t - 1) + 3)
                        for b in range(nbat):
                            lo, hi = b * 8, min(DT, b * 8 + 8)
                            last = None
                            for d in range(lo, hi):
                                last = te.transpose(
                                    psb[b % 2][:, (d - lo) * 128:(d - lo + 1) * 128],
                                    lntok[:, d * 128:(d + 1) * 128], id_sb[:, :])
                            last.then_inc(pe.h, 1)
                pe.add(nbat * NT)
        assert sc.n == b_sc + 2 * NT and ve.n == b_ve + 3 * NT and pe.n == b_pe + nbat * NT

    def emit_routing(ln_T, tag):
        b_pe, b_sc, b_ve = pe.base(), sc.base(), ve.base()
        with ExitStack() as rctx:
            rp = psum("rp_" + tag, [128, 512], F32, rctx)
            lg = sbuf("lg_" + tag, [128, E], F32, rctx)
            ex = sbuf("ex_" + tag, [128, E], F32, rctx)
            zz = sbuf("zz_" + tag, [128, 1], F32, rctx)
            selb = sbuf("sel_" + tag, [128, E], F32, rctx)
            pmb = sbuf("pm_" + tag, [128, E], F32, rctx)
            with nc.Block() as blk:
                @blk.tensor
                def _(te):
                    for t in range(NT):
                        if t > 0:
                            te.wait_ge(sc.h, b_sc + 2 * (t - 1) + 1)
                        last = None
                        for d in range(DT):
                            last = te.matmul(rp[:, 0:E],
                                             bass.AP(ln_T, d * T + t * 128, [[NT * D, 128], [1, 128]]),
                                             gw_sb[:, d * E:(d + 1) * E],
                                             start=(d == 0), stop=(d == DT - 1))
                        last.then_inc(pe.h, 1)
                pe.add(NT)

                @blk.scalar
                def _(s):
                    for t in range(NT):
                        s.wait_ge(pe.h, b_pe + t + 1)
                        if t > 0:
                            s.wait_ge(ve.h, b_ve + 2 * (t - 1) + 2)
                        s.activation(lg[:, :], rp[:, 0:E], AF.Copy).then_inc(sc.h, 1)
                        s.wait_ge(ve.h, b_ve + 2 * t + 1)
                        s.activation(ex[:, :], lg[:, :], AF.Exp, bias=stc("negmax"),
                                     accum_out=zz[:, :]).then_inc(sc.h, 1)
                sc.add(2 * NT)

                @blk.vector
                def _(v):
                    for t in range(NT):
                        v.wait_ge(sc.h, b_sc + 2 * t + 1)
                        v.reduce_max(stc("negmax"), lg[:, :], axis=AX.X,
                                     negate=True).then_inc(ve.h, 1)
                        v.wait_ge(sc.h, b_sc + 2 * t + 2)
                        v.reduce_max(stc("m1"), ex[:, :], axis=AX.X)
                        v.drain()
                        v.tensor_single_scalar(selb[:, :], ex[:, :], stc("m1"), op=ALU.is_ge)
                        v.drain()
                        v.scalar_tensor_tensor(pmb[:, :], selb[:, :], -1.0e9, ex[:, :],
                                               op0=ALU.mult, op1=ALU.add)
                        v.drain()
                        v.reduce_max(stc("m2"), pmb[:, :], axis=AX.X)
                        v.drain()
                        v.tensor_single_scalar(selb[:, :], ex[:, :], stc("m2"), op=ALU.is_ge)
                        v.tensor_tensor(stc("den"), stc("m1"), stc("m2"), op=ALU.add)
                        v.drain()
                        v.scalar_tensor_tensor(stc("den"), zz[:, :], 1.0e-20, stc("den"),
                                               op0=ALU.mult, op1=ALU.add)
                        v.drain()
                        v.reciprocal(stc("rinv"), stc("den"))
                        v.tensor_mul(c_all[:, t * E:(t + 1) * E], ex[:, :], selb[:, :])
                        v.drain()
                        v.tensor_scalar(c_all[:, t * E:(t + 1) * E], c_all[:, t * E:(t + 1) * E],
                                        stc("rinv"), 0.5, op0=ALU.mult,
                                        op1=ALU.mult).then_inc(ve.h, 1)
                ve.add(2 * NT)
        assert pe.n == b_pe + NT and sc.n == b_sc + 2 * NT and ve.n == b_ve + 2 * NT

    hb_last = [None]

    def emit_ffn(ln_T, tag):
        KT = DT
        mid_k = KT // 2
        mid_i = max(1, IT // 2)
        gu_waves = [(0, TC)] if TC <= 2 else [(0, TC // 2), (TC // 2, TC)]
        tg_sizes = [min(8, NT - 8 * tgi) for tgi in range(NTG)]

        def dn_waves(ntt):
            return [(0, ntt)] if ntt <= 4 else [(0, 4), (4, ntt)]

        experts = []
        for e in range(E):
            experts.append(dict(kind="exp", e=e, dsrc=wed_T, doff=e * I * D))
        for half in range(2):
            experts.append(dict(kind="sh", e=None, dsrc=wsd_T, doff=half * I * D))
        NX = len(experts)
        NW = len(gu_waves)

        b_pe, b_sc, b_ve, b_dW = pe.base(), sc.base(), ve.base(), dW.base()

        pe_i, sc_i, ve_i, dw_i = {}, {}, {}, {}
        npe = nsc = nve = ndw = 0
        for x in range(NX):
            for i in range(IT):
                for q in range(4):
                    ndw += 16; dw_i[(x, "gu", i, q)] = ndw
            for dc in range(DC):
                for hlf in range(2):
                    ndw += 16; dw_i[(x, "dn", dc, hlf)] = ndw
            for i in range(IT):
                for w in range(NW):
                    npe += 1; pe_i[(x, "gu", i, w)] = npe
            for dc in range(DC):
                for tgi in range(NTG):
                    ntt = tg_sizes[tgi]
                    c = dc * NTG + tgi
                    dws = dn_waves(ntt)
                    for wi in range(len(dws)):
                        if wi == len(dws) - 1:
                            npe += 1; pe_i[(x, "dnmid", c)] = npe
                        npe += 1; pe_i[(x, "dn", c, wi)] = npe
            for i in range(IT):
                for w in range(NW):
                    nsc += 1; sc_i[(x, i, w)] = nsc
            for i in range(IT):
                for w in range(NW):
                    nve += 1; ve_i[(x, "h", i, w)] = nve
            for dc in range(DC):
                for tgi in range(NTG):
                    ntt = tg_sizes[tgi]
                    c = dc * NTG + tgi
                    for wi in range(len(dn_waves(ntt))):
                        nve += 1; ve_i[(x, "x", c, wi)] = nve

        def PF(k):
            return b_pe + pe_i[k]

        def SF(k):
            return b_sc + sc_i[k]

        def VF(k):
            return b_ve + ve_i[k]

        def DWF(k):
            return b_dW + dw_i[k]

        NCH = NTG * DC

        with ExitStack() as fctx:
            wgb = [sbuf("wg%d_%s" % (p, tag), [128, KT * 128], BF16, fctx) for p in range(2)]
            wub = [sbuf("wu%d_%s" % (p, tag), [128, KT * 128], BF16, fctx) for p in range(2)]
            wdb = sbuf("wd_" + tag, [128, IT * DCW], BF16, fctx)
            sgb = sbuf("sg_" + tag, [128, TC * TCW], BF16, fctx)
            hb = sbuf("h_" + tag, [128, IT * T], BF16, fctx)
            hb_last[0] = hb
            ps = [psum("fp%d_%s" % (i, tag), [128, 512], F32, fctx) for i in range(8)]

            with nc.Block() as blk:
                @blk.gpsimd
                def _(gp):
                    for x in range(NX):
                        for i in range(IT):
                            gidx = x * IT + i
                            par = gidx % 2
                            for q in range(4):
                                srcw = wg5 if q < 2 else wu5
                                hlf = q % 2
                                klo = hlf * mid_k
                                khi = mid_k if hlf == 0 else KT - mid_k
                                dst = (wgb if q < 2 else wub)[par]
                                if i >= 1:
                                    gp.wait_ge(pe.h, PF((x, "gu", i - 1, 0)))
                                elif x > 0:
                                    gp.wait_ge(pe.h, PF((x - 1, "dnmid", NCH - 1)))
                                gp.dma_start(
                                    bass.AP(dst, klo * 128, [[KT * 128, 128], [1, khi * 128]]),
                                    bass.AP(srcw, (gidx * 128) * (KT * 128) + klo * 128,
                                            [[KT * 128, 128], [1, khi * 128]]),
                                ).then_inc(dW.h, 16)
                        ex = experts[x]
                        for dc in range(DC):
                            for hlf in range(2):
                                ilo = hlf * mid_i
                                ihi = mid_i if hlf == 0 else IT - mid_i
                                if dc > 0:
                                    pc = dc * NTG - 1
                                    gp.wait_ge(pe.h, PF((x, "dnmid", pc)) if hlf == 0
                                               else PF((x, "dn", pc, len(dn_waves(tg_sizes[NTG - 1])) - 1)))
                                else:
                                    gp.wait_ge(pe.h, PF((x, "gu", IT - 1, NW - 1)))
                                gp.dma_start(
                                    bass.AP(wdb, ilo * DCW, [[IT * DCW, 128], [DCW, ihi], [1, DCW]]),
                                    bass.AP(ex["dsrc"], ex["doff"] + (ilo * 128) * D + dc * DCW,
                                            [[D, 128], [128 * D, ihi], [1, DCW]]),
                                ).then_inc(dW.h, 16)
                    dW.add(ndw)

                @blk.tensor
                def _(te):
                    for x in range(NX):
                        for i in range(IT):
                            gidx = x * IT + i
                            par = gidx % 2
                            for w, (tclo, tchi) in enumerate(gu_waves):
                                if i > 0:
                                    te.wait_ge(sc.h, SF((x, i - 1, w)))
                                    te.wait_ge(ve.h, VF((x, "h", i - 1, w)))
                                elif x > 0:
                                    te.wait_ge(sc.h, SF((x - 1, IT - 1, w)))
                                    lc = NCH - 1
                                    lw = len(dn_waves(tg_sizes[lc // DC])) - 1
                                    te.wait_ge(ve.h, VF((x - 1, "x", lc, lw)))
                                if w == 0:
                                    te.wait_ge(dW.h, DWF((x, "gu", i, 3)))
                                last = None
                                for k in range(KT):
                                    for tc in range(tclo, tchi):
                                        te.matmul(ps[tc][:, 0:TCW],
                                                  wgb[par][:, k * 128:(k + 1) * 128],
                                                  bass.AP(ln_T, k * T + tc * TCW, [[NT * D, 128], [1, TCW]]),
                                                  start=(k == 0), stop=(k == KT - 1))
                                    for tc in range(tclo, tchi):
                                        last = te.matmul(ps[TC + tc][:, 0:TCW],
                                                         wub[par][:, k * 128:(k + 1) * 128],
                                                         bass.AP(ln_T, k * T + tc * TCW, [[NT * D, 128], [1, TCW]]),
                                                         start=(k == 0), stop=(k == KT - 1))
                                last.then_inc(pe.h, 1)
                        for dc in range(DC):
                            for tgi in range(NTG):
                                ntt = tg_sizes[tgi]
                                c = dc * NTG + tgi
                                dws = dn_waves(ntt)
                                for wi, (ttlo, tthi) in enumerate(dws):
                                    if c == 0 and wi == 0:
                                        te.wait_ge(sc.h, SF((x, IT - 1, NW - 1)))
                                        te.wait_ge(ve.h, VF((x, "h", IT - 1, NW - 1)))
                                    if c > 0:
                                        pdws = dn_waves(tg_sizes[(c - 1) % NTG])
                                        te.wait_ge(ve.h, VF((x, "x", c - 1, min(wi, len(pdws) - 1))))
                                    if tgi == 0 and wi == 0:
                                        te.wait_ge(dW.h, DWF((x, "dn", dc, 1)))
                                    last = None
                                    for i in range(IT):
                                        for tt in range(ttlo, tthi):
                                            tok = (tgi * 8 + tt) * 128
                                            last = te.matmul(
                                                ps[tt][:, 0:DCW],
                                                bass.AP(hb, i * T + tok, [[IT * T, 128], [1, 128]]),
                                                wdb[:, i * DCW:(i + 1) * DCW],
                                                start=(i == 0), stop=(i == IT - 1))
                                        if wi == len(dws) - 1 and i == mid_i - 1 and tt == tthi - 1:
                                            last.then_inc(pe.h, 1)
                                    last.then_inc(pe.h, 1)
                    pe.add(npe)

                @blk.scalar
                def _(s):
                    preload_set(s, "silu_and_others")
                    for x in range(NX):
                        for i in range(IT):
                            for w, (tclo, tchi) in enumerate(gu_waves):
                                s.wait_ge(pe.h, PF((x, "gu", i, w)))
                                if i > 0:
                                    s.wait_ge(ve.h, VF((x, "h", i - 1, w)))
                                elif x > 0:
                                    s.wait_ge(ve.h, VF((x - 1, "h", IT - 1, w)))
                                lasts = None
                                for tc in range(tclo, tchi):
                                    lasts = s.activation(sgb[:, tc * TCW:(tc + 1) * TCW],
                                                         ps[tc][:, 0:TCW], AF.Silu)
                                lasts.then_inc(sc.h, 1)
                    sc.add(nsc)

                @blk.vector
                def _(v):
                    for x in range(NX):
                        ex = experts[x]
                        for i in range(IT):
                            for w, (tclo, tchi) in enumerate(gu_waves):
                                v.wait_ge(sc.h, SF((x, i, w)))
                                lastm = None
                                for tc in range(tclo, tchi):
                                    lastm = v.tensor_tensor(
                                        hb[:, i * T + tc * TCW: i * T + (tc + 1) * TCW],
                                        sgb[:, tc * TCW:(tc + 1) * TCW],
                                        ps[TC + tc][:, 0:TCW], op=ALU.mult)
                                lastm.then_inc(ve.h, 1)
                        for dc in range(DC):
                            for tgi in range(NTG):
                                ntt = tg_sizes[tgi]
                                c = dc * NTG + tgi
                                dws = dn_waves(ntt)
                                for wi, (ttlo, tthi) in enumerate(dws):
                                    v.wait_ge(pe.h, PF((x, "dn", c, wi)))
                                    lastx = None
                                    for tt in range(ttlo, tthi):
                                        ti = tgi * 8 + tt
                                        if ex["kind"] == "exp":
                                            cap = c_all[:, ti * E + ex["e"]: ti * E + ex["e"] + 1]
                                        else:
                                            cap = 0.5
                                        lastx = v.scalar_tensor_tensor(
                                            x_res[:, ti * D + dc * DCW: ti * D + (dc + 1) * DCW],
                                            ps[tt][:, 0:DCW], cap,
                                            x_res[:, ti * D + dc * DCW: ti * D + (dc + 1) * DCW],
                                            op0=ALU.mult, op1=ALU.add)
                                    if c == NCH - 1 and wi == len(dws) - 1:
                                        v.drain().then_inc(ve.h, 1)
                                    else:
                                        lastx.then_inc(ve.h, 1)
                    ve.add(nve)
        assert pe.n == b_pe + npe and sc.n == b_sc + nsc
        assert ve.n == b_ve + nve and dW.n == b_dW + ndw

    def emit_qkv(ln_T, tag):
        b_pe, b_ve, b_dW, b_dS = pe.base(), ve.base(), dW.base(), dS.base()
        nvb = (NT + 7) // 8
        PE_H = 3 + nvb
        VE_H = 3 + 1
        W = [wq, wk, wv]
        BIAS = [bqs_sb, bk_sb, bv_sb]
        SCALE = [RSQ, 1.0, 1.0]

        def pe_j(h, j):
            return b_pe + h * PE_H + j + 1

        def pe_vt(h, b):
            return b_pe + h * PE_H + 3 + b + 1

        def ve_j(h, j):
            return b_ve + VE_H * h + j + 1

        def ve_vt(h):
            return b_ve + VE_H * h + 4

        def dw_j(h, j):
            return b_dW + (3 * h + j + 1) * 16

        with ExitStack() as qctx:
            wjb = [sbuf("wj%d_%s" % (j, tag), [128, 2 * DT * 128], BF16, qctx) for j in range(3)]
            js_sb = [sbuf("js%d_%s" % (j, tag), [128, T], BF16, qctx) for j in range(3)]
            vts = sbuf("vts_" + tag, [128, NT * 128], BF16, qctx)
            psq = [psum("qp%d_%s" % (i, tag), [128, 512], F32, qctx) for i in range(4)]
            psb = [psum("qb%d_%s" % (i, tag), [128, 1024], BF16, qctx) for i in range(2)]

            with nc.Block() as blk:
                @blk.gpsimd
                def _(gp):
                    for h in range(H):
                        for j in range(3):
                            buf = wjb[j]
                            par = h % 2
                            if h >= 1:
                                gp.wait_ge(pe.h, pe_j(h - 1, 0))
                            gp.dma_start(
                                bass.AP(buf, par * DT * 128, [[2 * DT * 128, 128], [1, DT * 128]]),
                                bass.AP(W[j], h * 128 * D, [[D, 128], [1, D]]),
                            ).then_inc(dW.h, 16)
                    dW.add(3 * H * 16)

                @blk.tensor
                def _(te):
                    for h in range(H):
                        te.wait_ge(dW.h, b_dW + 3 * (h + 1) * 16)
                        for j in range(3):
                            par = h % 2
                            if j > 0:
                                te.wait_ge(ve.h, ve_j(h, j - 1))
                            elif h > 0:
                                te.wait_ge(ve.h, ve_j(h - 1, 2))
                            last = None
                            for k in range(DT):
                                for tc in range(TC):
                                    last = te.matmul(
                                        psq[tc][:, 0:TCW],
                                        wjb[j][:, (par * DT + k) * 128:(par * DT + k + 1) * 128],
                                        bass.AP(ln_T, k * T + tc * TCW, [[NT * D, 128], [1, TCW]]),
                                        start=(k == 0), stop=(k == DT - 1))
                            last.then_inc(pe.h, 1)
                        te.wait_ge(ve.h, ve_j(h, 2))
                        if h > 0:
                            te.wait_ge(ve.h, ve_vt(h - 1))
                        for b in range(nvb):
                            lo, hi = b * 8, min(NT, b * 8 + 8)
                            last = None
                            for tti in range(lo, hi):
                                last = te.transpose(psb[b % 2][:, (tti - lo) * 128:(tti - lo + 1) * 128],
                                                    js_sb[2][:, tti * 128:(tti + 1) * 128],
                                                    id_sb[:, :])
                            last.then_inc(pe.h, 1)
                    pe.add(H * PE_H)

                @blk.vector
                def _(v):
                    for h in range(H):
                        if h > 0:
                            v.wait_ge(dS.h, b_dS + 3 * h * 16)
                        for j in range(3):
                            v.wait_ge(pe.h, pe_j(h, j))
                            lastq = None
                            for tc in range(TC):
                                if SCALE[j] == 1.0:
                                    lastq = v.tensor_scalar_add(js_sb[j][:, tc * TCW:(tc + 1) * TCW],
                                                                psq[tc][:, 0:TCW], BIAS[j][:, h:h + 1])
                                else:
                                    lastq = v.tensor_scalar(js_sb[j][:, tc * TCW:(tc + 1) * TCW],
                                                            psq[tc][:, 0:TCW], SCALE[j],
                                                            BIAS[j][:, h:h + 1],
                                                            op0=ALU.mult, op1=ALU.add)
                            lastq.then_inc(ve.h, 1)
                        lastv = None
                        for b in range(nvb):
                            lo, hi = b * 8, min(NT, b * 8 + 8)
                            v.wait_ge(pe.h, pe_vt(h, b))
                            lastv = v.tensor_copy(vts[:, lo * 128:lo * 128 + (hi - lo) * 128],
                                                  psb[b % 2][:, 0:(hi - lo) * 128])
                        lastv.then_inc(ve.h, 1)
                    ve.add(H * VE_H)

                @blk.sync
                def _(s):
                    for h in range(H):
                        for j in range(3):
                            s.wait_ge(ve.h, ve_j(h, j))
                            if j == 0:
                                s.dma_start(bass.AP(q_hbm, h * 128 * T, [[T, 128], [1, T]]),
                                            js_sb[0][:, :]).then_inc(dS.h, 16)
                            elif j == 1:
                                s.dma_start(bass.AP(kv_send[h // HG], (h % HG) * 128 * T,
                                                    [[T, 128], [1, T]]),
                                            js_sb[1][:, :]).then_inc(dS.h, 16)
                            else:
                                s.wait_ge(ve.h, ve_vt(h))
                                s.dma_start(
                                    bass.AP(kv_send[h // HG], KSZ + (h % HG) * T * 128,
                                            [[128, 128], [128 * 128, NT], [1, 128]]),
                                    bass.AP(vts, 0, [[NT * 128, 128], [128, NT], [1, 128]]),
                                ).then_inc(dS.h, 16)
                    s.wait_ge(dS.h, dS.n + 3 * H * 16)
                    dS.add(3 * H * 16)
        assert pe.n == b_pe + H * PE_H and ve.n == b_ve + H * VE_H

        with nc.Block() as blk:
            @blk.gpsimd
            def _(gp):
                for i in range(NCC):
                    gp.collective_compute(
                        "AllGather", ALU.bypass,
                        ins=[kv_send[i].ap().opt()],
                        outs=[kv_gath[i].ap().opt()],
                        replica_groups=g.get("RG") or [[0, 1], [2, 3], [4, 5], [6, 7]],
                    ).then_inc(ccS.h)
                    ccS.add(1)
                    if i < NCC - 1:
                        gp.wait_ge(ccS.h, ccS.n)

    def emit_attn(tag):
        SEQ_TOT = 2 * T
        assert NT % 2 == 0
        b_pe, b_sc, b_ve, b_dS = pe.base(), sc.base(), ve.base(), dS.base()
        cc_base = ccS.n - NCC
        pos = []
        for i in range(NT):
            ext = 256 * (i + 1)
            cw = chunk_widths(ext, 512)
            ntr = ext // 128
            nb = (ntr + 7) // 8
            pos.append((ext, cw, ntr, nb))

        def prev3(i):
            return NT - 1 - ((NT - 1 - i) % 3)

        pe_i, sc_i, ve_i = {}, {}, {}
        npe = nsc = nve = 0
        for h in range(H):
            for r in range(NT + 2):
                if r < NT:
                    for ci in range(len(pos[r][1])):
                        npe += 1; pe_i[(h, "s", r, ci)] = npe
                j = r - 2
                if 0 <= j < NT:
                    for b in range(pos[j][3]):
                        npe += 1; pe_i[(h, "t", j, b)] = npe
                    npe += 1; pe_i[(h, "av", j)] = npe
            for i in range(NT):
                for ci in range(len(pos[i][1])):
                    nsc += 1; sc_i[(h, "ev", i, ci)] = nsc
                if i >= 2:
                    nsc += 1; sc_i[(h, "z", i - 2)] = nsc
                nsc += 1; sc_i[(h, "exp", i)] = nsc
            nsc += 1; sc_i[(h, "z", NT - 2)] = nsc
            nsc += 1; sc_i[(h, "z", NT - 1)] = nsc
            for i in range(NT):
                if i >= 2:
                    for b in range(pos[i - 2][3]):
                        nve += 1; ve_i[(h, "te", i - 2, b)] = nve
                nve += 1; ve_i[(h, "nm", i)] = nve
                nve += 1; ve_i[(h, "sca", i)] = nve
            for j in (NT - 2, NT - 1):
                for b in range(pos[j][3]):
                    nve += 1; ve_i[(h, "te", j, b)] = nve

        def P(k):
            return b_pe + pe_i[k]

        def S(k):
            return b_sc + sc_i[k]

        def V(k):
            return b_ve + ve_i[k]

        def DSH(h):
            return b_dS + (h + 1) * 6 * 16

        with ExitStack() as actx:
            k_sb = sbuf("k_" + tag, [128, SEQ_TOT], BF16, actx)
            vts = sbuf("vt_" + tag, [128, SEQ_TOT], BF16, actx)
            q_sb = sbuf("q_" + tag, [128, T], BF16, actx)
            rows = [sbuf("rows%d_%s" % (i, tag), [128, SEQ_TOT], F32, actx) for i in range(2)]
            att = [sbuf("att%d_%s" % (i, tag), [128, SEQ_TOT], BF16, actx) for i in range(3)]
            attT = sbuf("attT_" + tag, [128, SEQ_TOT], BF16, actx)
            zall = sbuf("zall_" + tag, [128, T], BF16, actx)
            pss = [psum("ap%d_%s" % (i, tag), [128, 512], F32, actx) for i in range(4)]
            psz = psum("az_" + tag, [128, 512], F32, actx)
            psb = [psum("ab%d_%s" % (i, tag), [128, 1024], BF16, actx) for i in range(2)]
            NMC = [7, 8]
            DENC = [10, 12]
            RIC = [11, 13]

            def stp(cols, i):
                c = cols[i % 2]
                return st[:, c:c + 1]

            def emit_te_evac(v, h, j):
                for b in range(pos[j][3]):
                    lo, hi = b * 8, min(pos[j][2], b * 8 + 8)
                    v.wait_ge(pe.h, P((h, "t", j, b)))
                    v.tensor_copy(attT[:, lo * 128:lo * 128 + (hi - lo) * 128],
                                  psb[b % 2][:, 0:(hi - lo) * 128]).then_inc(ve.h, 1)

            with nc.Block() as blk:
                @blk.sync
                def _(s):
                    for h in range(H):
                        if h == 0:
                            s.wait_ge(ccS.h, cc_base + 1)
                        elif h == HG:
                            s.wait_ge(ccS.h, cc_base + NCC)
                        if h > 0:
                            s.wait_ge(sc.h, S((h - 1, "z", NT - 1)))
                        gb = kv_gath[h // HG]
                        hh = h % HG
                        for r in range(2):
                            s.dma_start(
                                bass.AP(k_sb, r * 128, [[SEQ_TOT, 128], [256, NT], [1, 128]]),
                                bass.AP(gb, r * 2 * KSZ + hh * 128 * T,
                                        [[T, 128], [128, NT], [1, 128]]),
                            ).then_inc(dS.h, 16)
                            s.dma_start(
                                bass.AP(vts, r * 128, [[SEQ_TOT, 128], [256, NT], [1, 128]]),
                                bass.AP(gb, r * 2 * KSZ + KSZ + hh * T * 128,
                                        [[128, 128], [128 * 128, NT], [1, 128]]),
                            ).then_inc(dS.h, 16)
                        s.dma_start(q_sb[:, :],
                                    bass.AP(q_hbm, h * 128 * T, [[T, 128], [1, T]])
                                    ).then_inc(dS.h, 16)
                        s.wait_ge(sc.h, S((h, "z", NT - 1)))
                        s.dma_start(bass.AP(z_hbm, h * 128 * T, [[T, 128], [1, T]]),
                                    zall[:, :]).then_inc(dS.h, 16)
                        s.wait_ge(dS.h, DSH(h))
                    dS.add(H * 6 * 16)

                @blk.tensor
                def _(te):
                    for h in range(H):
                        te.wait_ge(dS.h, DSH(h) - 16)
                        for r in range(NT + 2):
                            if r < NT:
                                ext, cw, ntr, nb = pos[r]
                                base_bank = (r % 2) * 2
                                off = 0
                                for ci, w in enumerate(cw):
                                    if ci >= 2:
                                        te.wait_ge(sc.h, S((h, "ev", r, ci - 2)))
                                    elif r >= 2:
                                        te.wait_ge(sc.h, S((h, "ev", r - 2, len(pos[r - 2][1]) - 1)))
                                    elif h > 0:
                                        te.wait_ge(sc.h, S((h - 1, "ev", NT - 2 + r, len(pos[NT - 2 + r][1]) - 1)))
                                    te.matmul(pss[base_bank + ci % 2][:, 0:w],
                                              q_sb[:, r * 128:(r + 1) * 128],
                                              k_sb[:, off:off + w],
                                              start=True, stop=True).then_inc(pe.h, 1)
                                    off += w
                            j = r - 2
                            if 0 <= j < NT:
                                ext, cw, ntr, nb = pos[j]
                                te.wait_ge(ve.h, V((h, "sca", j)))
                                for b in range(nb):
                                    lo, hi = b * 8, min(ntr, b * 8 + 8)
                                    if b >= 2:
                                        te.wait_ge(ve.h, V((h, "te", j, b - 2)))
                                    elif j > 0:
                                        te.wait_ge(ve.h, V((h, "te", j - 1, pos[j - 1][3] - 1)))
                                    elif h > 0:
                                        te.wait_ge(ve.h, V((h - 1, "te", NT - 1, pos[NT - 1][3] - 1)))
                                    last = None
                                    for kt in range(lo, hi):
                                        last = te.transpose(
                                            psb[b % 2][:, (kt - lo) * 128:(kt - lo + 1) * 128],
                                            att[j % 3][:, kt * 128:(kt + 1) * 128], id_sb[:, :])
                                    last.then_inc(pe.h, 1)
                                te.wait_ge(ve.h, V((h, "te", j, nb - 1)))
                                if j > 0:
                                    te.wait_ge(sc.h, S((h, "z", j - 1)))
                                elif h > 0:
                                    te.wait_ge(sc.h, S((h - 1, "z", NT - 1)))
                                last = None
                                for kt in range(ntr):
                                    last = te.matmul(psz[:, 0:128],
                                                     vts[:, kt * 128:(kt + 1) * 128],
                                                     attT[:, kt * 128:(kt + 1) * 128],
                                                     start=(kt == 0), stop=(kt == ntr - 1))
                                last.then_inc(pe.h, 1)
                    pe.add(npe)

                @blk.scalar
                def _(s):
                    preload_set(s, "natural_log_exp_and_others")
                    for h in range(H):
                        for i in range(NT):
                            ext, cw, ntr, nb = pos[i]
                            base_bank = (i % 2) * 2
                            off = 0
                            for ci, w in enumerate(cw):
                                s.wait_ge(pe.h, P((h, "s", i, ci)))
                                if ci == 0 and i >= 2:
                                    s.wait_ge(ve.h, V((h, "nm", i - 2)))
                                s.activation(rows[i % 2][:, off:off + w],
                                             pss[base_bank + ci % 2][:, 0:w],
                                             AF.Copy).then_inc(sc.h, 1)
                                off += w
                            if i >= 2:
                                s.wait_ge(pe.h, P((h, "av", i - 2)))
                                s.activation(zall[:, (i - 2) * 128:(i - 1) * 128],
                                             psz[:, 0:128], AF.Copy).then_inc(sc.h, 1)
                            elif i == 0 and h > 0:
                                s.wait_ge(dS.h, DSH(h - 1))
                            s.wait_ge(ve.h, V((h, "nm", i)))
                            if i >= 3:
                                s.wait_ge(pe.h, P((h, "t", i - 3, pos[i - 3][3] - 1)))
                            elif h > 0:
                                pj = prev3(i)
                                s.wait_ge(pe.h, P((h - 1, "t", pj, pos[pj][3] - 1)))
                            s.drain()
                            s.activation(att[i % 3][:, 0:ext], rows[i % 2][:, 0:ext], AF.Exp,
                                         bias=stp(NMC, i),
                                         accum_out=stp(DENC, i)).then_inc(sc.h, 1)
                        for j in (NT - 2, NT - 1):
                            s.wait_ge(pe.h, P((h, "av", j)))
                            s.activation(zall[:, j * 128:(j + 1) * 128],
                                         psz[:, 0:128], AF.Copy).then_inc(sc.h, 1)
                    sc.add(nsc)

                @blk.vector
                def _(v):
                    for h in range(H):
                        for i in range(NT):
                            ext, cw, ntr, nb = pos[i]
                            if i >= 2:
                                emit_te_evac(v, h, i - 2)
                            v.wait_ge(sc.h, S((h, "ev", i, len(cw) - 1)))
                            v.tensor_add(rows[i % 2][:, ext - 256:ext],
                                         rows[i % 2][:, ext - 256:ext], mask_sb[:, :])
                            v.drain()
                            v.reduce_max(stp(NMC, i), rows[i % 2][:, 0:ext], axis=AX.X,
                                         negate=True).then_inc(ve.h, 1)
                            v.wait_ge(sc.h, S((h, "exp", i)))
                            v.reciprocal(stp(RIC, i), stp(DENC, i))
                            v.drain()
                            v.tensor_scalar_mul(att[i % 3][:, 0:ext], att[i % 3][:, 0:ext],
                                                stp(RIC, i)).then_inc(ve.h, 1)
                        for j in (NT - 2, NT - 1):
                            emit_te_evac(v, h, j)
                    ve.add(nve)
        assert pe.n == b_pe + npe and sc.n == b_sc + nsc and ve.n == b_ve + nve

    def emit_oproj(tag):
        b_pe, b_sc, b_ve, b_dS, b_dW = pe.base(), sc.base(), ve.base(), dS.base(), dW.base()
        NG8 = (NT + 7) // 8
        with ExitStack() as octx:
            wob = sbuf("wo_" + tag, [128, H * D], BF16, octx)
            zt8 = sbuf("zt_" + tag, [128, H * 8 * 128], BF16, octx)
            bob = sbuf("bo_" + tag, [128, D], F32, octx)
            tmp = [sbuf("otmp%d_%s" % (i, tag), [128, 512], F32, octx) for i in range(2)]
            pso = [psum("op%d_%s" % (i, tag), [128, 512], F32, octx) for i in range(min(4, NT * DC))]
            nps = len(pso)

            with nc.Block() as blk:
                @blk.gpsimd
                def _(gp):
                    gp.dma_start(
                        bass.AP(wob, 0, [[H * D, 128], [D, H], [1, D]]),
                        bass.AP(wo, 0, [[D, 128], [128 * D, H], [1, D]]),
                    ).then_inc(dW.h, 16)
                    dW.add(16)

                @blk.sync
                def _(s):
                    s.dma_start(bob[:, :], bass.AP(bo, 0, [[0, 128], [1, D]])).then_inc(dS.h, 16)
                    s.wait_ge(dS.h, b_dS + 16)
                    for gp8 in range(NG8):
                        ntt = min(8, NT - gp8 * 8)
                        if gp8 > 0:
                            s.wait_ge(pe.h, b_pe + gp8 * 8 * DC)
                        s.dma_start(
                            bass.AP(zt8, 0, [[H * 8 * 128, 128], [8 * 128, H], [1, ntt * 128]]),
                            bass.AP(z_hbm, gp8 * 8 * 128, [[T, 128], [128 * T, H], [1, ntt * 128]]),
                        ).then_inc(dS.h, 16)
                        s.wait_ge(dS.h, b_dS + (gp8 + 2) * 16)
                    dS.add((NG8 + 1) * 16)

                @blk.tensor
                def _(te):
                    te.wait_ge(dW.h, b_dW + 16)
                    for gp8 in range(NG8):
                        ntt = min(8, NT - gp8 * 8)
                        te.wait_ge(dS.h, b_dS + (gp8 + 2) * 16)
                        for tt in range(ntt):
                            t = gp8 * 8 + tt
                            for dc in range(DC):
                                if t * DC + dc >= nps:
                                    te.wait_ge(ve.h, b_ve + t * DC + dc - nps + 1)
                                last = None
                                for h in range(H):
                                    last = te.matmul(pso[(t * DC + dc) % nps][:, 0:DCW],
                                                     zt8[:, h * 8 * 128 + tt * 128: h * 8 * 128 + (tt + 1) * 128],
                                                     wob[:, h * D + dc * DCW: h * D + (dc + 1) * DCW],
                                                     start=(h == 0), stop=(h == H - 1))
                                last.then_inc(pe.h, 1)
                    pe.add(NT * DC)

                @blk.vector
                def _(v):
                    v.wait_ge(dS.h, b_dS + 16)
                    for t in range(NT):
                        for dc in range(DC):
                            gi = t * DC + dc
                            v.wait_ge(pe.h, b_pe + gi + 1)
                            v.tensor_add(tmp[gi % 2][:, 0:DCW], pso[gi % nps][:, 0:DCW],
                                         bob[:, dc * DCW:(dc + 1) * DCW])
                            v.drain()
                            v.tensor_add(x_res[:, t * D + dc * DCW: t * D + (dc + 1) * DCW],
                                         x_res[:, t * D + dc * DCW: t * D + (dc + 1) * DCW],
                                         tmp[gi % 2][:, 0:DCW]).then_inc(ve.h, 1)
                    ve.add(NT * DC)

    def emit_out():
        b_sc, b_dS = sc.base(), dS.base()
        with ExitStack() as octx:
            of = [sbuf("of%d" % i, [128, D], F32, octx) for i in range(2)]
            with nc.Block() as blk:
                @blk.scalar
                def _(s):
                    for t in range(NT):
                        if t >= 2:
                            s.wait_ge(dS.h, b_dS + (t - 1) * 16)
                        s.activation(of[t % 2][:, :], x_res[:, t * D:(t + 1) * D],
                                     AF.Copy).then_inc(sc.h, 1)
                    sc.add(NT)

                @blk.sync
                def _(s):
                    for t in range(NT):
                        s.wait_ge(sc.h, b_sc + t + 1)
                        s.dma_start(bass.AP(y_out, t * 128 * D, [[D, 128], [1, D]]),
                                    of[t % 2][:, :]).then_inc(dS.h, 16)
                        s.wait_ge(dS.h, b_dS + (t + 1) * 16)
                    dS.add(NT * 16)

    def emit_dump(dst):
        b_sc, b_dS = sc.base(), dS.base()
        with ExitStack() as octx:
            of = [sbuf("dof%d_%s" % (i, dst.name), [128, D], F32, octx) for i in range(2)]
            with nc.Block() as blk:
                @blk.scalar
                def _(s):
                    for t in range(NT):
                        if t >= 2:
                            s.wait_ge(dS.h, b_dS + (t - 1) * 16)
                        s.activation(of[t % 2][:, :], x_res[:, t * D:(t + 1) * D],
                                     AF.Copy).then_inc(sc.h, 1)
                    sc.add(NT)

                @blk.sync
                def _(s):
                    for t in range(NT):
                        s.wait_ge(sc.h, b_sc + t + 1)
                        s.dma_start(bass.AP(dst, t * 128 * D, [[D, 128], [1, D]]),
                                    of[t % 2][:, :]).then_inc(dS.h, 16)
                        s.wait_ge(dS.h, b_dS + (t + 1) * 16)
                    dS.add(NT * 16)

    STOP = g.get("STOP")
    with ExitStack() as inv1:
        ln_T = sbuf("lnT1", [128, NT * D], BF16, inv1)
        emit_ln(ln_T, "l1")
        if DBG:
            emit_sbuf_dump(dbg_lnT, ln_T, NT * D)
            emit_sbuf_dump(dbg_st, st, 32)
        if STOP == "ln1":
            return nc
        emit_routing(ln_T, "r1")
        if DBG:
            emit_sbuf_dump(dbg_c, c_all, NT * E)
        if STOP == "route1":
            return nc
        emit_ffn(ln_T, "f1")
        if DBG:
            emit_sbuf_dump(dbg_h, hb_last[0], IT * T)
    if DBG >= 1:
        emit_dump(dbg_outs[0])
    if STOP == "ffn1":
        return nc
    with ExitStack() as att:
        ln_T = sbuf("lnT2", [128, NT * D], BF16, att)
        emit_ln(ln_T, "l2")
        emit_qkv(ln_T, "qk")
    emit_attn("at")
    emit_oproj("op")
    if DBG >= 2:
        emit_dump(dbg_outs[1])
    with ExitStack() as inv2:
        ln_T = sbuf("lnT3", [128, NT * D], BF16, inv2)
        emit_ln(ln_T, "l3")
        emit_routing(ln_T, "r3")
        emit_ffn(ln_T, "f3")
    emit_out()
    return nc


def make_inputs(core, full, g):
    D, T, I, E, H = g["D"], g["T"], g["I"], g["E"], g["H"]
    NT = T // 128
    b, r = core // 2, core % 2
    xb = np.ascontiguousarray(
        full["L0_attn_x"][b].reshape(2 * NT, 128, D)[r::2])
    mask = np.where(np.arange(256)[None, :] <= 128 * r + np.arange(128)[:, None],
                    np.float32(0), np.float32(-1e9 if False else -100000.0)).astype(np.float32)
    E_, I_, H_ = g["E"], g["I"], g["H"]
    DT_, IT_ = D // 128, I_ // 128

    def gu5(expert_mat, shared_mat):
        egT = expert_mat.transpose(0, 2, 1)
        e5 = egT.reshape(E_, DT_, 128, IT_, 128).transpose(0, 3, 4, 1, 2)
        e5 = egT.reshape(E_, DT_, 128, IT_, 128).transpose(0, 3, 2, 1, 4)
        e5 = e5.reshape(E_, IT_, 128, DT_ * 128)
        shT = shared_mat.T
        s5 = shT.reshape(DT_, 128, 2, IT_, 128).transpose(2, 3, 1, 0, 4)
        s5 = s5.reshape(2, IT_, 128, DT_ * 128)
        return np.ascontiguousarray(np.concatenate([e5, s5], axis=0))

    def qkv4(w):
        return np.ascontiguousarray(
            w.reshape(H_, DT_, 128, 128).transpose(0, 2, 1, 3).reshape(H_, 128, D))

    im = dict(
        x_in=xb,
        wq=qkv4(full["L0_attn_W_Q"]), wk=qkv4(full["L0_attn_W_K"]),
        wv=qkv4(full["L0_attn_W_V"]),
        wo=full["L0_attn_W_O"],
        bqT=np.ascontiguousarray(full["L0_attn_b_Q"].T),
        bkT=np.ascontiguousarray(full["L0_attn_b_K"].T),
        bvT=np.ascontiguousarray(full["L0_attn_b_V"].T),
        bo=full["L0_attn_b_O"],
        gate_wT=np.ascontiguousarray(full["L0_ffn_gate_w"].T),
        wg5=gu5(full["L0_ffn_expert_gate"], full["L0_ffn_shared_gate"]),
        wu5=gu5(full["L0_ffn_expert_up"], full["L0_ffn_shared_up"]),
        wed_T=np.ascontiguousarray(full["L0_ffn_expert_down"].transpose(0, 2, 1)),
        wsd_T=np.ascontiguousarray(full["L0_ffn_shared_down"].T),
        idm=np.eye(128, dtype=ml_dtypes.bfloat16),
        amask=mask,
    )
    return im


def kernel(**inputs):
    g = FULL_GEO
    D, T = g["D"], g["T"]
    NT = T // 128
    nc = build_nc(g)
    shared = None
    in_maps = []
    for c in range(8):
        im = make_inputs(c, inputs, g)
        in_maps.append(im)
    res = run_bass_kernel_spmd(nc, in_maps, core_ids=list(range(8)))
    out = np.zeros((4, 4096, D), np.float32)
    for c in range(8):
        b, r = c // 2, c % 2
        y = res.results[c]["y_out"]
        out[b].reshape(2 * NT, 128, D)[r::2] = y
    return out


# version 11
# speedup vs baseline: 1.2576x; 1.0503x over previous
import numpy as np
import ml_dtypes
from contextlib import ExitStack

import concourse.bass as bass
import concourse.mybir as mybir
from concourse.bass_core_rust import IndirectOffsetOnAxis
from concourse.bass_utils import run_bass_kernel_spmd

F32 = mybir.dt.float32
BF16 = mybir.dt.bfloat16
AF = mybir.ActivationFunctionType
ALU = mybir.AluOpType
AX = mybir.AxisListType

RSQ = float(1.0 / np.sqrt(np.float32(128.0)))
LN_EPS = 1e-5

FULL_GEO = dict(D=2048, T=2048, I=1408, E=8, H=16)


class Sem:
    def __init__(self, ctx, nc, name):
        self.h = ctx.enter_context(nc.semaphore(name))
        self.n = 0

    def base(self):
        return self.n

    def add(self, amt=1):
        self.n += amt
        return self.n


def build_nc(g):
    D, T, I, E, H = g["D"], g["T"], g["I"], g["E"], g["H"]
    ISH = 2 * I
    DT = D // 128
    NT = T // 128
    IT = I // 128
    TC = max(1, T // 512)
    TCW = min(T, 512)
    NEXP = E + 2
    CAP = g.get("CAP", T // 2)
    assert CAP % 128 == 0 and CAP % NT == 0
    CT = CAP // NT
    CH = min(512, CAP)
    NHF = (CAP + CH - 1) // CH
    NST = CH // 128
    ZROW = E * CAP
    NTG = (NT + 7) // 8
    DC = max(1, D // 512)
    DCW = min(D, 512)
    assert D % 128 == 0 and T % 128 == 0 and I % 128 == 0 and T % TCW == 0 and D % DCW == 0

    nc = bass.Bass("TRN2", target_bir_lowering=False)

    x_in = nc.dram_tensor("x_in", [NT, 128, D], F32, kind="ExternalInput")
    wq = nc.dram_tensor("wq", [H, 128, D], F32, kind="ExternalInput")
    wk = nc.dram_tensor("wk", [H, 128, D], F32, kind="ExternalInput")
    wv = nc.dram_tensor("wv", [H, 128, D], F32, kind="ExternalInput")
    wo = nc.dram_tensor("wo", [H, 128, D], F32, kind="ExternalInput")
    bqT = nc.dram_tensor("bqT", [128, H], F32, kind="ExternalInput")
    bkT = nc.dram_tensor("bkT", [128, H], F32, kind="ExternalInput")
    bvT = nc.dram_tensor("bvT", [128, H], F32, kind="ExternalInput")
    bo = nc.dram_tensor("bo", [D], F32, kind="ExternalInput")
    gate_wT = nc.dram_tensor("gate_wT", [D, E], F32, kind="ExternalInput")
    wg5 = nc.dram_tensor("wg5", [E + 2, IT, 128, DT * 128], F32, kind="ExternalInput")
    wu5 = nc.dram_tensor("wu5", [E + 2, IT, 128, DT * 128], F32, kind="ExternalInput")
    wed5 = nc.dram_tensor("wed5", [E, DC, 128, IT * DCW], F32, kind="ExternalInput")
    wsd5 = nc.dram_tensor("wsd5", [2, DC, 128, IT * DCW], F32, kind="ExternalInput")
    idm = nc.dram_tensor("idm", [128, 128], BF16, kind="ExternalInput")
    ltm = nc.dram_tensor("ltm", [128, 128], BF16, kind="ExternalInput")
    eoff = nc.dram_tensor("eoff", [128, E], F32, kind="ExternalInput")
    amask = nc.dram_tensor("amask", [128, 256], F32, kind="ExternalInput")
    y_out = nc.dram_tensor("y_out", [NT, 128, D], F32, kind="ExternalOutput")
    DBG = g.get("DBG", 0)
    dbg_outs = [nc.dram_tensor("dbg%d" % i, [NT, 128, D], F32, kind="ExternalOutput")
                for i in range(DBG)]
    if g.get("DBGSL"):
        dbg_sl = nc.dram_tensor("dbg_sl", [128, NT * E], mybir.dt.int32, kind="ExternalOutput")
        dbg_p1 = nc.dram_tensor("dbg_p1", [128, NT], mybir.dt.int32, kind="ExternalOutput")
        dbg_p2 = nc.dram_tensor("dbg_p2", [128, NT], mybir.dt.int32, kind="ExternalOutput")
        dbg_cc = nc.dram_tensor("dbg_cc", [128, NT * E], F32, kind="ExternalOutput")
    if DBG:
        dbg_lnT = nc.dram_tensor("dbg_lnT", [128, NT * D], BF16, kind="ExternalOutput")
        dbg_c = nc.dram_tensor("dbg_c", [128, NT * E], F32, kind="ExternalOutput")
        dbg_h = nc.dram_tensor("dbg_h", [128, IT * T], BF16, kind="ExternalOutput")
        dbg_st = nc.dram_tensor("dbg_st", [128, 32], F32, kind="ExternalOutput")

    def emit_sbuf_dump(dst, src_sb, cols):
        with nc.Block() as blk:
            @blk.sync
            def _(s):
                s.dma_start(bass.AP(dst, 0, [[cols, 128], [1, cols]]),
                            src_sb[:, 0:cols]).then_inc(dS.h, 16)
                s.wait_ge(dS.h, dS.add(16))

    HG = max(1, H // 2)
    NCC = (H + HG - 1) // HG
    KSZ = HG * 128 * T
    kv_send = [nc.dram_tensor("kv_send%d" % i, [2 * KSZ // 512, 512], BF16)
               for i in range(NCC)]
    kv_gath = [nc.dram_tensor("kv_gath%d" % i, [4 * KSZ // 512, 512], BF16)
               for i in range(NCC)]
    q_hbm = nc.dram_tensor("q_hbm", [H, 128, T], BF16)
    z_hbm = nc.dram_tensor("z_hbm", [H, 128, T], BF16)
    ln_hbm = nc.dram_tensor("ln_hbm", [T, D], BF16)
    cbuf = nc.dram_tensor("cbuf", [E, CAP, D], BF16)
    ybuf = nc.dram_tensor("ybuf", [E * CAP + 128, D], BF16)

    ctx = ExitStack()
    nc._kernel_ctx = ctx

    def sbuf(name, shape, dt, scope=None):
        return (scope or ctx).enter_context(nc.sbuf_tensor(name, shape, dt))

    def psum(name, shape, dt, scope=None):
        return (scope or ctx).enter_context(nc.psum_tensor(name, shape, dt))

    x_res = sbuf("x_res", [128, NT * D], BF16)
    id_sb = sbuf("id_sb", [128, 128], BF16)
    st = sbuf("st", [128, 32], F32)
    c_all = sbuf("c_all", [128, NT * E], F32)
    bq_sb = sbuf("bq_sb", [128, H], F32)
    bk_sb = sbuf("bk_sb", [128, H], F32)
    bv_sb = sbuf("bv_sb", [128, H], F32)
    bqs_sb = sbuf("bqs_sb", [128, H], F32)
    gw_sb = sbuf("gw_sb", [128, DT * E], BF16)
    mask_sb = sbuf("mask_sb", [128, 256], F32)
    lt_sb = sbuf("lt_sb", [128, 128], BF16)
    eo_sb = sbuf("eo_sb", [128, E], F32)
    sslot_i = sbuf("sslot_i", [128, NT * E], mybir.dt.int32)
    ptr1_i = sbuf("ptr1_i", [128, NT], mybir.dt.int32)
    ptr2_i = sbuf("ptr2_i", [128, NT], mybir.dt.int32)
    c1b = sbuf("c1b", [128, NT], F32)
    c2b = sbuf("c2b", [128, NT], F32)
    rstg = sbuf("rstg", [128, D], BF16)

    sems = ExitStack()
    ctx.enter_context(sems)
    dW = Sem(ctx, nc, "dW")
    dS = Sem(ctx, nc, "dS")
    pe = Sem(ctx, nc, "pe")
    sc = Sem(ctx, nc, "sc")
    ve = Sem(ctx, nc, "ve")
    ccS = Sem(ctx, nc, "ccS")
    dX = Sem(ctx, nc, "dX")
    dY = Sem(ctx, nc, "dY")

    COL = dict(sum=0, sq=1, mean=2, e2=3, var=4, rstd=5, mr=6,
               negmax=7, m1=8, m2=9, den=10, rinv=11)

    def stc(name):
        c = COL[name]
        return st[:, c:c + 1]

    from concourse.hw_specs import get_activation_tables
    _tabs = list(get_activation_tables(nc.m.arch).keys())

    def preload_set(s, name):
        s.add_instruction(mybir.InstLoadActFuncSet(
            name=nc.get_next_instruction_name(),
            act_func_set_id=_tabs.index(name), ins=[], outs=[]))

    def chunk_widths(total, w):
        out = []
        o = 0
        while o < total:
            out.append(min(w, total - o))
            o += w
        return out

    with nc.Block() as blk:
        @blk.sync
        def _(s):
            s.dma_start(id_sb[:, :], idm[:, :]).then_inc(dS.h, 16); dS.add(16)
            s.dma_start(mask_sb[:, :], amask[:, :]).then_inc(dS.h, 16); dS.add(16)
            s.dma_start(bq_sb[:, :], bqT[:, :]).then_inc(dS.h, 16); dS.add(16)
            s.dma_start(bk_sb[:, :], bkT[:, :]).then_inc(dS.h, 16); dS.add(16)
            s.dma_start(bv_sb[:, :], bvT[:, :]).then_inc(dS.h, 16); dS.add(16)
            s.dma_start(lt_sb[:, :], ltm[:, :]).then_inc(dS.h, 16); dS.add(16)
            s.dma_start(eo_sb[:, :], eoff[:, :]).then_inc(dS.h, 16); dS.add(16)
            s.wait_ge(dS.h, dS.n)

        @blk.gpsimd
        def _(gp):
            gp.dma_start(
                bass.AP(x_res, 0, [[NT * D, 128], [D, NT], [1, D]]),
                bass.AP(x_in, 0, [[D, 128], [128 * D, NT], [1, D]]),
            ).then_inc(dW.h, 16); dW.add(16)
            gp.dma_start(
                bass.AP(gw_sb, 0, [[DT * E, 128], [E, DT], [1, E]]),
                bass.AP(gate_wT, 0, [[E, 128], [128 * E, DT], [1, E]]),
            ).then_inc(dW.h, 16); dW.add(16)
            gp.wait_ge(dW.h, dW.n)
            gp.wait_ge(ve.h, ve.add(1))
            for j in range(E * CAP // 128):
                gp.dma_start(bass.AP(cbuf, j * 128 * D, [[D, 128], [1, D]]),
                             rstg[:, :]).then_inc(dW.h, 16); dW.add(16)
            gp.dma_start(bass.AP(ybuf, ZROW * D, [[D, 128], [1, D]]),
                         rstg[:, :]).then_inc(dW.h, 16); dW.add(16)

        @blk.vector
        def _(v):
            v.memset(rstg[:, :], 0.0)
            v.drain().then_inc(ve.h, 1)
            v.wait_ge(dS.h, dS.n)
            v.tensor_scalar_mul(bqs_sb[:, :], bq_sb[:, :], RSQ)

    zfill_dW = dW.n

    def emit_ln(ln_T, tag, store_rows=False):
        nbat = (DT + 7) // 8
        b_sc, b_ve, b_pe = sc.base(), ve.base(), pe.base()
        b_dS = dS.base()
        with ExitStack() as lctx:
            dump = sbuf("lndump_" + tag, [128, D], BF16, lctx)
            lntok = sbuf("lntok_" + tag, [128, D], BF16, lctx)
            psb = [psum("lnp%d_%s" % (i, tag), [128, 1024], BF16, lctx) for i in range(2)]
            with nc.Block() as blk:
                @blk.scalar
                def _(s):
                    preload_set(s, "natural_log_exp_and_others")
                    for t in range(NT):
                        if t > 0:
                            s.wait_ge(ve.h, b_ve + 3 * (t - 1) + 1)
                        s.activation(dump[:, :], x_res[:, t * D:(t + 1) * D],
                                     AF.Square, accum_out=stc("sq")).then_inc(sc.h, 1)
                        s.wait_ge(ve.h, b_ve + 3 * t + 1)
                        s.activation(stc("rstd"), stc("var"), AF.Ln)
                        s.drain()
                        s.activation(stc("rstd"), stc("rstd"), AF.Exp,
                                     scale=-0.5).then_inc(sc.h, 1)
                sc.add(2 * NT)

                @blk.vector
                def _(v):
                    for t in range(NT):
                        v.reduce_sum(stc("sum"), x_res[:, t * D:(t + 1) * D], axis=AX.X)
                        v.wait_ge(sc.h, b_sc + 2 * t + 1)
                        v.drain()
                        v.tensor_scalar_mul(stc("mean"), stc("sum"), 1.0 / D)
                        v.tensor_scalar_mul(stc("e2"), stc("sq"), 1.0 / D)
                        v.drain()
                        v.tensor_mul(stc("var"), stc("mean"), stc("mean"))
                        v.drain()
                        v.tensor_sub(stc("var"), stc("e2"), stc("var"))
                        v.drain()
                        v.tensor_scalar_add(stc("var"), stc("var"),
                                            LN_EPS).then_inc(ve.h, 1)
                        v.wait_ge(sc.h, b_sc + 2 * t + 2)
                        v.tensor_mul(stc("mr"), stc("mean"), stc("rstd"))
                        v.drain()
                        if t > 0:
                            v.wait_ge(pe.h, b_pe + nbat * t)
                        if store_rows and t > 0:
                            v.wait_ge(dS.h, b_dS + t * 16)
                        v.tensor_scalar(lntok[:, :], x_res[:, t * D:(t + 1) * D],
                                        stc("rstd"), stc("mr"),
                                        op0=ALU.mult, op1=ALU.subtract).then_inc(ve.h, 1)
                        lastc = None
                        for b in range(nbat):
                            lo, hi = b * 8, min(DT, b * 8 + 8)
                            v.wait_ge(pe.h, b_pe + nbat * t + b + 1)
                            lastc = v.tensor_copy(
                                bass.AP(ln_T, lo * T + t * 128,
                                        [[NT * D, 128], [T, hi - lo], [1, 128]]),
                                psb[b % 2][:, 0:(hi - lo) * 128])
                        lastc.then_inc(ve.h, 1)
                ve.add(3 * NT)

                @blk.tensor
                def _(te):
                    for t in range(NT):
                        te.wait_ge(ve.h, b_ve + 3 * t + 2)
                        if t > 0:
                            te.wait_ge(ve.h, b_ve + 3 * (t - 1) + 3)
                        for b in range(nbat):
                            lo, hi = b * 8, min(DT, b * 8 + 8)
                            last = None
                            for d in range(lo, hi):
                                last = te.transpose(
                                    psb[b % 2][:, (d - lo) * 128:(d - lo + 1) * 128],
                                    lntok[:, d * 128:(d + 1) * 128], id_sb[:, :])
                            last.then_inc(pe.h, 1)
                pe.add(nbat * NT)

                if store_rows:
                    @blk.sync
                    def _(s):
                        for t in range(NT):
                            s.wait_ge(ve.h, b_ve + 3 * t + 2)
                            s.dma_start(bass.AP(ln_hbm, t * 128 * D, [[D, 128], [1, D]]),
                                        lntok[:, :]).then_inc(dS.h, 16)
                        s.wait_ge(dS.h, b_dS + NT * 16)
                    dS.add(NT * 16)
        assert sc.n == b_sc + 2 * NT and ve.n == b_ve + 3 * NT and pe.n == b_pe + nbat * NT

    def emit_routing(ln_T, tag):
        b_pe, b_sc, b_ve, b_dW = pe.base(), sc.base(), ve.base(), dW.base()
        with ExitStack() as rctx:
            rp = psum("rp_" + tag, [128, 512], F32, rctx)
            rp2 = psum("rp2_" + tag, [128, 512], F32, rctx)
            lg = sbuf("lg_" + tag, [128, E], F32, rctx)
            ex = sbuf("ex_" + tag, [128, E], F32, rctx)
            zz = sbuf("zz_" + tag, [128, 1], F32, rctx)
            selb = sbuf("sel_" + tag, [128, E], F32, rctx)
            pmb = sbuf("pm_" + tag, [128, E], F32, rctx)
            selr = sbuf("selr_" + tag, [128, E], F32, rctx)
            selbf = sbuf("selbf_" + tag, [128, E], BF16, rctx)
            rkb = sbuf("rk_" + tag, [128, E], F32, rctx)
            slot = sbuf("slot_" + tag, [128, E], F32, rctx)
            selv = sbuf("selv_" + tag, [128, E], F32, rctx)
            t8a = sbuf("t8a_" + tag, [128, E], F32, rctx)
            t8b = sbuf("t8b_" + tag, [128, E], F32, rctx)
            t1a = sbuf("t1a_" + tag, [128, 1], F32, rctx)
            t1b = sbuf("t1b_" + tag, [128, 1], F32, rctx)
            with nc.Block() as blk:
                @blk.tensor
                def _(te):
                    for t in range(NT):
                        if t > 0:
                            te.wait_ge(sc.h, b_sc + 2 * (t - 1) + 1)
                        last = None
                        for d in range(DT):
                            last = te.matmul(rp[:, 0:E],
                                             bass.AP(ln_T, d * T + t * 128, [[NT * D, 128], [1, 128]]),
                                             gw_sb[:, d * E:(d + 1) * E],
                                             start=(d == 0), stop=(d == DT - 1))
                        last.then_inc(pe.h, 1)
                        te.wait_ge(ve.h, b_ve + 5 * t + 3)
                        te.matmul(rp2[:, 0:E], lt_sb[:, :], selbf[:, :],
                                  start=True, stop=True).then_inc(pe.h, 1)
                pe.add(2 * NT)

                @blk.scalar
                def _(s):
                    for t in range(NT):
                        s.wait_ge(pe.h, b_pe + 2 * t + 1)
                        if t > 0:
                            s.wait_ge(ve.h, b_ve + 5 * (t - 1) + 2)
                        s.activation(lg[:, :], rp[:, 0:E], AF.Copy).then_inc(sc.h, 1)
                        s.wait_ge(ve.h, b_ve + 5 * t + 1)
                        s.activation(ex[:, :], lg[:, :], AF.Exp, bias=stc("negmax"),
                                     accum_out=zz[:, :]).then_inc(sc.h, 1)
                sc.add(2 * NT)

                @blk.vector
                def _(v):
                    for t in range(NT):
                        csl = c_all[:, t * E:(t + 1) * E]
                        v.wait_ge(sc.h, b_sc + 2 * t + 1)
                        v.reduce_max(stc("negmax"), lg[:, :], axis=AX.X,
                                     negate=True).then_inc(ve.h, 1)
                        v.wait_ge(sc.h, b_sc + 2 * t + 2)
                        v.reduce_max(stc("m1"), ex[:, :], axis=AX.X)
                        v.drain()
                        v.tensor_single_scalar(selb[:, :], ex[:, :], stc("m1"), op=ALU.is_ge)
                        v.drain()
                        v.scalar_tensor_tensor(pmb[:, :], selb[:, :], -1.0e9, ex[:, :],
                                               op0=ALU.mult, op1=ALU.add)
                        v.drain()
                        v.reduce_max(stc("m2"), pmb[:, :], axis=AX.X)
                        v.drain()
                        v.tensor_single_scalar(selb[:, :], ex[:, :], stc("m2"), op=ALU.is_ge)
                        v.tensor_tensor(stc("den"), stc("m1"), stc("m2"), op=ALU.add)
                        v.drain()
                        v.scalar_tensor_tensor(stc("den"), zz[:, :], 1.0e-20, stc("den"),
                                               op0=ALU.mult, op1=ALU.add)
                        v.drain()
                        v.reciprocal(stc("rinv"), stc("den"))
                        v.tensor_mul(csl, ex[:, :], selb[:, :])
                        v.drain()
                        v.tensor_scalar(csl, csl, stc("rinv"), 0.5,
                                        op0=ALU.mult, op1=ALU.mult).then_inc(ve.h, 1)
                        v.drain()
                        v.tensor_single_scalar(selr[:, :], csl, 0.0, op=ALU.is_gt)
                        v.drain()
                        v.tensor_copy(selbf[:, :], selr[:, :]).then_inc(ve.h, 1)
                        v.wait_ge(pe.h, b_pe + 2 * t + 2)
                        v.tensor_copy(rkb[:, :], rp2[:, 0:E])
                        v.drain()
                        v.tensor_scalar_add(slot[:, :], rkb[:, :], float(t * CT))
                        v.tensor_single_scalar(selv[:, :], rkb[:, :], float(CT), op=ALU.is_lt)
                        v.drain()
                        v.tensor_mul(selv[:, :], selv[:, :], selr[:, :])
                        v.drain()
                        v.tensor_mul(t8a[:, :], slot[:, :], selv[:, :])
                        v.tensor_scalar(t8b[:, :], selv[:, :], -1.0e5, 1.0e5,
                                        op0=ALU.mult, op1=ALU.add)
                        v.drain()
                        v.tensor_add(t8a[:, :], t8a[:, :], t8b[:, :])
                        v.drain()
                        v.tensor_add(t8a[:, :], t8a[:, :], eo_sb[:, :])
                        v.drain()
                        v.tensor_copy(sslot_i[:, t * E:(t + 1) * E], t8a[:, :]).then_inc(ve.h, 1)
                        v.reduce_max(c1b[:, t:t + 1], csl, axis=AX.X)
                        v.reduce_sum(t1a[:, :], csl, axis=AX.X)
                        v.drain()
                        v.tensor_sub(c2b[:, t:t + 1], t1a[:, :], c1b[:, t:t + 1])
                        v.tensor_single_scalar(t8b[:, :], csl, c1b[:, t:t + 1], op=ALU.is_ge)
                        v.drain()
                        v.tensor_mul(t8b[:, :], t8b[:, :], selv[:, :])
                        v.tensor_add(t8a[:, :], slot[:, :], eo_sb[:, :])
                        v.drain()
                        v.tensor_mul(selb[:, :], t8b[:, :], t8a[:, :])
                        v.drain()
                        v.reduce_sum(t1a[:, :], selb[:, :], axis=AX.X)
                        v.reduce_sum(t1b[:, :], t8b[:, :], axis=AX.X)
                        v.drain()
                        v.tensor_scalar(t1b[:, :], t1b[:, :], float(-ZROW), float(ZROW),
                                        op0=ALU.mult, op1=ALU.add)
                        v.drain()
                        v.tensor_add(t1a[:, :], t1a[:, :], t1b[:, :])
                        v.drain()
                        v.tensor_copy(ptr1_i[:, t:t + 1], t1a[:, :])
                        v.tensor_sub(t8b[:, :], selv[:, :], t8b[:, :])
                        v.drain()
                        v.tensor_mul(selb[:, :], t8b[:, :], t8a[:, :])
                        v.drain()
                        v.reduce_sum(t1a[:, :], selb[:, :], axis=AX.X)
                        v.reduce_sum(t1b[:, :], t8b[:, :], axis=AX.X)
                        v.drain()
                        v.tensor_scalar(t1b[:, :], t1b[:, :], float(-ZROW), float(ZROW),
                                        op0=ALU.mult, op1=ALU.add)
                        v.drain()
                        v.tensor_add(t1a[:, :], t1a[:, :], t1b[:, :])
                        v.drain()
                        v.tensor_copy(ptr2_i[:, t:t + 1], t1a[:, :]).then_inc(ve.h, 1)
                ve.add(5 * NT)

        assert pe.n == b_pe + 2 * NT and sc.n == b_sc + 2 * NT and ve.n == b_ve + 5 * NT

    hb_last = [None]

    def emit_ffn(ln_T, tag):
        KT = DT
        mid_k = KT // 2
        mid_i = max(1, IT // 2)
        gu_waves = [(0, TC)] if TC <= 2 else [(0, TC // 2), (TC // 2, TC)]
        tg_sizes = [min(8, NT - 8 * tgi) for tgi in range(NTG)]

        def dn_waves(ntt):
            return [(0, ntt)] if ntt <= 4 else [(0, 4), (4, ntt)]

        experts = []
        s1 = (IT + 1) // 2 if IT >= 4 else IT
        for half in range(2):
            for jlo, jhi in [(0, s1), (s1, IT)]:
                if jhi > jlo:
                    experts.append(dict(kind="sh", e=None, d5=wsd5, d5e=half,
                                        wex=E + half, ioff=jlo, it=jhi - jlo))
        NX = len(experts)
        ITX = max(ex["it"] for ex in experts)
        NW = len(gu_waves)

        b_pe, b_sc, b_ve, b_dW = pe.base(), sc.base(), ve.base(), dW.base()

        pe_i, sc_i, ve_i, dw_i = {}, {}, {}, {}
        npe = nsc = nve = ndw = 0
        for x in range(NX):
            xit = experts[x]["it"]
            for i in range(xit):
                for q in range(2):
                    ndw += 16; dw_i[(x, "gu", i, q)] = ndw
            for dc in range(DC):
                for hlf in range(2):
                    ndw += 16; dw_i[(x, "dn", dc, hlf)] = ndw
            for i in range(xit):
                for w in range(NW):
                    npe += 1; pe_i[(x, "gu", i, w)] = npe
            for dc in range(DC):
                for tgi in range(NTG):
                    ntt = tg_sizes[tgi]
                    c = dc * NTG + tgi
                    dws = dn_waves(ntt)
                    for wi in range(len(dws)):
                        if wi == len(dws) - 1:
                            npe += 1; pe_i[(x, "dnmid", c)] = npe
                        npe += 1; pe_i[(x, "dn", c, wi)] = npe
            for i in range(xit):
                for w in range(NW):
                    nsc += 1; sc_i[(x, i, w)] = nsc
            for i in range(xit):
                for w in range(NW):
                    nve += 1; ve_i[(x, "h", i, w)] = nve
            for dc in range(DC):
                for tgi in range(NTG):
                    ntt = tg_sizes[tgi]
                    c = dc * NTG + tgi
                    for wi in range(len(dn_waves(ntt))):
                        nve += 1; ve_i[(x, "x", c, wi)] = nve

        def PF(k):
            return b_pe + pe_i[k]

        def SF(k):
            return b_sc + sc_i[k]

        def VF(k):
            return b_ve + ve_i[k]

        def DWF(k):
            return b_dW + dw_i[k]

        NCH = NTG * DC

        with ExitStack() as fctx:
            wgb = [sbuf("wg%d_%s" % (p, tag), [128, KT * 128], BF16, fctx) for p in range(2)]
            wub = [sbuf("wu%d_%s" % (p, tag), [128, KT * 128], BF16, fctx) for p in range(2)]
            wdb = sbuf("wd_" + tag, [128, ITX * DCW], BF16, fctx)
            sgb = sbuf("sg_" + tag, [128, TC * TCW], BF16, fctx)
            hb = sbuf("h_" + tag, [128, ITX * T], BF16, fctx)
            hb_last[0] = hb
            ps = [psum("fp%d_%s" % (i, tag), [128, 512], F32, fctx) for i in range(8)]

            with nc.Block() as blk:
                @blk.gpsimd
                def _(gp):
                    gp.wait_ge(dW.h, zfill_dW)
                    b_dX = dX.base()
                    breg = gp.alloc_register()
                    gp.reg_mov(breg, E * CAP - 1)
                    tgrp = [(x * NT // NX, (x + 1) * NT // NX) for x in range(NX)]
                    for x in range(NX):
                        xit = experts[x]["it"]
                        for i in range(xit):
                            par = (experts[x]["ioff"] + i) % 2
                            for q in range(2):
                                dst = (wgb if q == 0 else wub)[par]
                                if i >= 1:
                                    gp.wait_ge(pe.h, PF((x, "gu", i - 1, 0)))
                                elif x > 0:
                                    gp.wait_ge(pe.h, PF((x - 1, "dnmid", NCH - 1)))
                                wgidx = experts[x]["wex"] * IT + experts[x]["ioff"] + i
                                gp.dma_start(
                                    bass.AP(dst, 0, [[KT * 128, 128], [1, KT * 128]]),
                                    bass.AP(wg5 if q == 0 else wu5,
                                            (wgidx * 128) * (KT * 128),
                                            [[KT * 128, 128], [1, KT * 128]]),
                                ).then_inc(dW.h, 16)
                        ex = experts[x]
                        xmid = max(1, xit // 2)
                        for dc in range(DC):
                            for hlf in range(2):
                                ilo = hlf * xmid
                                ihi = xmid if hlf == 0 else xit - xmid
                                if dc > 0:
                                    pc = dc * NTG - 1
                                    gp.wait_ge(pe.h, PF((x, "dnmid", pc)) if hlf == 0
                                               else PF((x, "dn", pc, len(dn_waves(tg_sizes[NTG - 1])) - 1)))
                                else:
                                    gp.wait_ge(pe.h, PF((x, "gu", xit - 1, NW - 1)))
                                gp.dma_start(
                                    bass.AP(wdb, ilo * DCW, [[ITX * DCW, 128], [1, ihi * DCW]]),
                                    bass.AP(ex["d5"],
                                            ((ex["d5e"] * DC + dc) * 128) * (IT * DCW)
                                            + (ex["ioff"] + ilo) * DCW,
                                            [[IT * DCW, 128], [1, ihi * DCW]]),
                                ).then_inc(dW.h, 16)
                        for t in range(*tgrp[x]):
                            gp.wait_ge(dX.h, b_dX + t * (E + 1) * 16)
                            gp.dma_start(
                                rstg[:, :],
                                bass.AP(ln_hbm, t * 128 * D, [[D, 128], [1, D]]),
                            ).then_inc(dX.h, 16)
                            gp.wait_ge(dX.h, b_dX + t * (E + 1) * 16 + 16)
                            for e in range(E):
                                gp.indirect_dma_start(
                                    bass.AP(cbuf, 0, [[D, E * CAP], [1, D]]),
                                    IndirectOffsetOnAxis(
                                        ap=sslot_i[:, t * E + e: t * E + e + 1], axis=0),
                                    rstg[:, :],
                                    None,
                                    bounds_check=breg,
                                    oob_is_err=False,
                                ).then_inc(dX.h, 16)
                    gp.free_register(breg)
                    dX.add(NT * (E + 1) * 16)
                    dW.add(ndw)

                @blk.tensor
                def _(te):
                    for x in range(NX):
                        xit = experts[x]["it"]
                        for i in range(xit):
                            par = (experts[x]["ioff"] + i) % 2
                            for w, (tclo, tchi) in enumerate(gu_waves):
                                if i > 0:
                                    te.wait_ge(sc.h, SF((x, i - 1, w)))
                                    te.wait_ge(ve.h, VF((x, "h", i - 1, w)))
                                elif x > 0:
                                    te.wait_ge(sc.h, SF((x - 1, experts[x - 1]["it"] - 1, w)))
                                    lc = NCH - 1
                                    lw = len(dn_waves(tg_sizes[lc // DC])) - 1
                                    te.wait_ge(ve.h, VF((x - 1, "x", lc, lw)))
                                if w == 0:
                                    te.wait_ge(dW.h, DWF((x, "gu", i, 1)))
                                last = None
                                for k in range(KT):
                                    for tc in range(tclo, tchi):
                                        te.matmul(ps[tc][:, 0:TCW],
                                                  wgb[par][:, k * 128:(k + 1) * 128],
                                                  bass.AP(ln_T, k * T + tc * TCW, [[NT * D, 128], [1, TCW]]),
                                                  start=(k == 0), stop=(k == KT - 1))
                                    for tc in range(tclo, tchi):
                                        last = te.matmul(ps[TC + tc][:, 0:TCW],
                                                         wub[par][:, k * 128:(k + 1) * 128],
                                                         bass.AP(ln_T, k * T + tc * TCW, [[NT * D, 128], [1, TCW]]),
                                                         start=(k == 0), stop=(k == KT - 1))
                                last.then_inc(pe.h, 1)
                        for dc in range(DC):
                            for tgi in range(NTG):
                                ntt = tg_sizes[tgi]
                                c = dc * NTG + tgi
                                dws = dn_waves(ntt)
                                for wi, (ttlo, tthi) in enumerate(dws):
                                    if c == 0 and wi == 0:
                                        te.wait_ge(sc.h, SF((x, xit - 1, NW - 1)))
                                        te.wait_ge(ve.h, VF((x, "h", xit - 1, NW - 1)))
                                    if c > 0:
                                        pdws = dn_waves(tg_sizes[(c - 1) % NTG])
                                        te.wait_ge(ve.h, VF((x, "x", c - 1, min(wi, len(pdws) - 1))))
                                    if tgi == 0 and wi == 0:
                                        te.wait_ge(dW.h, DWF((x, "dn", dc, 1)))
                                    last = None
                                    xmid = max(1, xit // 2)
                                    merge_mid = (wi == len(dws) - 1) and (xmid == xit)
                                    for i in range(xit):
                                        for tt in range(ttlo, tthi):
                                            tok = (tgi * 8 + tt) * 128
                                            last = te.matmul(
                                                ps[tt][:, 0:DCW],
                                                bass.AP(hb, i * T + tok, [[ITX * T, 128], [1, 128]]),
                                                wdb[:, i * DCW:(i + 1) * DCW],
                                                start=(i == 0), stop=(i == xit - 1))
                                        if (wi == len(dws) - 1 and i == xmid - 1
                                                and tt == tthi - 1 and not merge_mid):
                                            last.then_inc(pe.h, 1)
                                    last.then_inc(pe.h, 2 if merge_mid else 1)
                    pe.add(npe)

                @blk.scalar
                def _(s):
                    preload_set(s, "silu_and_others")
                    for x in range(NX):
                        xit = experts[x]["it"]
                        for i in range(xit):
                            for w, (tclo, tchi) in enumerate(gu_waves):
                                s.wait_ge(pe.h, PF((x, "gu", i, w)))
                                if i > 0:
                                    s.wait_ge(ve.h, VF((x, "h", i - 1, w)))
                                elif x > 0:
                                    s.wait_ge(ve.h, VF((x - 1, "h", experts[x - 1]["it"] - 1, w)))
                                lasts = None
                                for tc in range(tclo, tchi):
                                    lasts = s.activation(sgb[:, tc * TCW:(tc + 1) * TCW],
                                                         ps[tc][:, 0:TCW], AF.Silu)
                                lasts.then_inc(sc.h, 1)
                    sc.add(nsc)

                @blk.vector
                def _(v):
                    for x in range(NX):
                        ex = experts[x]
                        for i in range(ex["it"]):
                            for w, (tclo, tchi) in enumerate(gu_waves):
                                v.wait_ge(sc.h, SF((x, i, w)))
                                lastm = None
                                for tc in range(tclo, tchi):
                                    lastm = v.tensor_tensor(
                                        hb[:, i * T + tc * TCW: i * T + (tc + 1) * TCW],
                                        sgb[:, tc * TCW:(tc + 1) * TCW],
                                        ps[TC + tc][:, 0:TCW], op=ALU.mult)
                                lastm.then_inc(ve.h, 1)
                        for dc in range(DC):
                            for tgi in range(NTG):
                                ntt = tg_sizes[tgi]
                                c = dc * NTG + tgi
                                dws = dn_waves(ntt)
                                for wi, (ttlo, tthi) in enumerate(dws):
                                    v.wait_ge(pe.h, PF((x, "dn", c, wi)))
                                    lastx = None
                                    for tt in range(ttlo, tthi):
                                        ti = tgi * 8 + tt
                                        if ex["kind"] == "exp":
                                            cap = c_all[:, ti * E + ex["e"]: ti * E + ex["e"] + 1]
                                        else:
                                            cap = 0.5
                                        lastx = v.scalar_tensor_tensor(
                                            x_res[:, ti * D + dc * DCW: ti * D + (dc + 1) * DCW],
                                            ps[tt][:, 0:DCW], cap,
                                            x_res[:, ti * D + dc * DCW: ti * D + (dc + 1) * DCW],
                                            op0=ALU.mult, op1=ALU.add)
                                    if c == NCH - 1 and wi == len(dws) - 1:
                                        v.drain().then_inc(ve.h, 1)
                                    else:
                                        lastx.then_inc(ve.h, 1)
                    ve.add(nve)
        assert pe.n == b_pe + npe and sc.n == b_sc + nsc
        assert ve.n == b_ve + nve and dW.n == b_dW + ndw

    def emit_sparse(tag):
        KT = DT
        mid_k = KT // 2
        mid_i = max(1, IT // 2)
        cws = chunk_widths(CAP, 512)
        TCT = len(cws)
        assert TCT <= 2
        coffs = [sum(cws[:j]) for j in range(TCT)]
        NSLT = CAP // 128
        assert NSLT <= 8

        def dn_waves(n):
            if n <= 3:
                return [(0, n)]
            if n <= 6:
                return [(0, 3), (3, n)]
            return [(0, 3), (3, 6), (6, n)]

        DWS = dn_waves(NSLT)
        NDW = len(DWS)
        wv = {}
        for wi, (lo, hi) in enumerate(DWS):
            for tt in range(lo, hi):
                wv[tt] = wi
        NX = E
        nbt = (DT + 7) // 8

        b_pe, b_sc, b_ve = pe.base(), sc.base(), ve.base()
        b_dW, b_dS, b_dY = dW.base(), dS.base(), dY.base()
        dx_target = dX.n

        pe_i, sc_i, ve_i, dw_i = {}, {}, {}, {}
        npe = nsc = nve = ndw = 0
        for x in range(NX):
            for i in range(IT):
                for q in range(2):
                    ndw += 16; dw_i[(x, "gu", i, q)] = ndw
            for dc in range(DC):
                for hlf in range(2):
                    ndw += 16; dw_i[(x, "dn", dc, hlf)] = ndw
            for ct in range(NSLT):
                for b in range(nbt):
                    npe += 1; pe_i[(x, "tp", ct, b)] = npe
            for i in range(IT):
                npe += 1; pe_i[(x, "gu", i)] = npe
            for dc in range(DC):
                for wi in range(NDW):
                    if wi == NDW - 1:
                        npe += 1; pe_i[(x, "dnmid", dc)] = npe
                    npe += 1; pe_i[(x, "dn", dc, wi)] = npe
            for i in range(IT):
                nsc += 1; sc_i[(x, i)] = nsc
            for ct in range(NSLT):
                for b in range(nbt):
                    nve += 1; ve_i[(x, "cf", ct, b)] = nve
            for i in range(IT):
                nve += 1; ve_i[(x, "h", i)] = nve
            for dc in range(DC):
                for wi in range(NDW):
                    nve += 1; ve_i[(x, "y", dc, wi)] = nve

        def PF(k):
            return b_pe + pe_i[k]

        def SF(k):
            return b_sc + sc_i[k]

        def VF(k):
            return b_ve + ve_i[k]

        def DWF(k):
            return b_dW + dw_i[k]

        def DSL(x, ct):
            return b_dS + (x * NSLT + ct + 1) * 16

        with ExitStack() as fctx:
            wgb = [sbuf("swg%d_%s" % (p, tag), [128, KT * 128], BF16, fctx) for p in range(2)]
            wub = [sbuf("swu%d_%s" % (p, tag), [128, KT * 128], BF16, fctx) for p in range(2)]
            wdb = sbuf("swd_" + tag, [128, IT * DCW], BF16, fctx)
            sgb = sbuf("ssg_" + tag, [128, CAP], BF16, fctx)
            hb = sbuf("sh_" + tag, [128, IT * CAP], BF16, fctx)
            stage = [sbuf("sst%d_%s" % (p, tag), [128, D], BF16, fctx) for p in range(2)]
            cfT = sbuf("scf_" + tag, [128, DT * CAP], BF16, fctx)
            y_sb = sbuf("sy_" + tag, [128, NSLT * D], BF16, fctx)
            ps = [psum("sp%d_%s" % (i, tag), [128, 512], F32, fctx) for i in range(6)]
            psb = [psum("sb%d_%s" % (i, tag), [128, 1024], BF16, fctx) for i in range(2)]

            with nc.Block() as blk:
                @blk.sync
                def _(s):
                    s.wait_ge(dX.h, dx_target)

                    def emit_loads(x):
                        for ct in range(NSLT):
                            g_ct = x * NSLT + ct
                            if g_ct >= 1:
                                p = g_ct - 1
                                s.wait_ge(pe.h, PF((p // NSLT, "tp", p % NSLT, 0)))
                            s.dma_start(
                                stage[g_ct % 2][:, :],
                                bass.AP(cbuf, (x * CAP + ct * 128) * D, [[D, 128], [1, D]]),
                            ).then_inc(dS.h, 16)

                    def emit_stores(x):
                        for stt in range(NSLT):
                            s.wait_ge(ve.h, VF((x, "y", DC - 1, wv[stt])))
                            s.dma_start(
                                bass.AP(ybuf, (x * CAP + stt * 128) * D, [[D, 128], [1, D]]),
                                y_sb[:, stt * D:(stt + 1) * D]).then_inc(dY.h, 16)

                    emit_loads(0)
                    for x in range(NX):
                        if x + 1 < NX:
                            emit_loads(x + 1)
                        emit_stores(x)
                    s.wait_ge(dS.h, b_dS + NX * NSLT * 16)
                    s.wait_ge(dY.h, b_dY + NX * NSLT * 16)
                dS.add(NX * NSLT * 16)
                dY.add(NX * NSLT * 16)

                @blk.gpsimd
                def _(gp):
                    for x in range(NX):
                        for i in range(IT):
                            par = (x * IT + i) % 2
                            for q in range(2):
                                dst = (wgb if q == 0 else wub)[par]
                                if i >= 1:
                                    gp.wait_ge(pe.h, PF((x, "gu", i - 1)))
                                elif x > 0:
                                    gp.wait_ge(pe.h, PF((x - 1, "dnmid", DC - 1)))
                                wgidx = x * IT + i
                                gp.dma_start(
                                    bass.AP(dst, 0, [[KT * 128, 128], [1, KT * 128]]),
                                    bass.AP(wg5 if q == 0 else wu5,
                                            (wgidx * 128) * (KT * 128),
                                            [[KT * 128, 128], [1, KT * 128]]),
                                ).then_inc(dW.h, 16)
                        for dc in range(DC):
                            for hlf in range(2):
                                ilo = hlf * mid_i
                                ihi = mid_i if hlf == 0 else IT - mid_i
                                if dc > 0:
                                    gp.wait_ge(pe.h, PF((x, "dnmid", dc - 1)) if hlf == 0
                                               else PF((x, "dn", dc - 1, NDW - 1)))
                                else:
                                    gp.wait_ge(pe.h, PF((x, "gu", IT - 1)))
                                gp.dma_start(
                                    bass.AP(wdb, ilo * DCW, [[IT * DCW, 128], [1, ihi * DCW]]),
                                    bass.AP(wed5,
                                            ((x * DC + dc) * 128) * (IT * DCW) + ilo * DCW,
                                            [[IT * DCW, 128], [1, ihi * DCW]]),
                                ).then_inc(dW.h, 16)
                    dW.add(ndw)

                @blk.tensor
                def _(te):
                    for x in range(NX):
                        for ct in range(NSLT):
                            g_ct = x * NSLT + ct
                            te.wait_ge(dS.h, DSL(x, ct))
                            for b in range(nbt):
                                lo, hi = b * 8, min(DT, b * 8 + 8)
                                gb = g_ct * nbt + b
                                if gb >= 2:
                                    pgb = gb - 2
                                    pxx, rem = divmod(pgb, NSLT * nbt)
                                    pct, pb = divmod(rem, nbt)
                                    te.wait_ge(ve.h, VF((pxx, "cf", pct, pb)))
                                last = None
                                for d in range(lo, hi):
                                    last = te.transpose(
                                        psb[gb % 2][:, (d - lo) * 128:(d - lo + 1) * 128],
                                        stage[g_ct % 2][:, d * 128:(d + 1) * 128],
                                        id_sb[:, :])
                                last.then_inc(pe.h, 1)
                        for i in range(IT):
                            par = (x * IT + i) % 2
                            if i > 0:
                                te.wait_ge(sc.h, SF((x, i - 1)))
                                te.wait_ge(ve.h, VF((x, "h", i - 1)))
                            elif x > 0:
                                te.wait_ge(ve.h, VF((x - 1, "y", DC - 1, NDW - 1)))
                            if i == 0:
                                te.wait_ge(ve.h, VF((x, "cf", NSLT - 1, nbt - 1)))
                            te.wait_ge(dW.h, DWF((x, "gu", i, 1)))
                            last = None
                            for k in range(KT):
                                for tc in range(TCT):
                                    te.matmul(ps[tc][:, 0:cws[tc]],
                                              wgb[par][:, k * 128:(k + 1) * 128],
                                              bass.AP(cfT, k * CAP + coffs[tc],
                                                      [[DT * CAP, 128], [1, cws[tc]]]),
                                              start=(k == 0), stop=(k == KT - 1))
                                for tc in range(TCT):
                                    last = te.matmul(ps[2 + tc][:, 0:cws[tc]],
                                                     wub[par][:, k * 128:(k + 1) * 128],
                                                     bass.AP(cfT, k * CAP + coffs[tc],
                                                             [[DT * CAP, 128], [1, cws[tc]]]),
                                                     start=(k == 0), stop=(k == KT - 1))
                            last.then_inc(pe.h, 1)
                        for dc in range(DC):
                            for wi, (ttlo, tthi) in enumerate(DWS):
                                if dc == 0 and wi == 0:
                                    te.wait_ge(sc.h, SF((x, IT - 1)))
                                    te.wait_ge(ve.h, VF((x, "h", IT - 1)))
                                if dc > 0:
                                    if wi == 0:
                                        te.wait_ge(ve.h, VF((x, "y", dc - 1, NDW - 1)))
                                    elif wi == 1:
                                        te.wait_ge(ve.h, VF((x, "y", dc - 1, min(1, NDW - 1))))
                                if wi >= 2:
                                    te.wait_ge(ve.h, VF((x, "y", dc, wi - 2)))
                                if wi == 0:
                                    te.wait_ge(dW.h, DWF((x, "dn", dc, 1)))
                                last = None
                                merge_mid = (wi == NDW - 1) and (mid_i == IT)
                                for i in range(IT):
                                    for tt in range(ttlo, tthi):
                                        last = te.matmul(
                                            ps[tt % 6][:, 0:DCW],
                                            bass.AP(hb, i * CAP + tt * 128,
                                                    [[IT * CAP, 128], [1, 128]]),
                                            wdb[:, i * DCW:(i + 1) * DCW],
                                            start=(i == 0), stop=(i == IT - 1))
                                    if wi == NDW - 1 and i == mid_i - 1 and not merge_mid:
                                        last.then_inc(pe.h, 1)
                                last.then_inc(pe.h, 2 if merge_mid else 1)
                    pe.add(npe)

                @blk.scalar
                def _(s):
                    for x in range(NX):
                        for i in range(IT):
                            s.wait_ge(pe.h, PF((x, "gu", i)))
                            if i > 0:
                                s.wait_ge(ve.h, VF((x, "h", i - 1)))
                            elif x > 0:
                                s.wait_ge(ve.h, VF((x - 1, "h", IT - 1)))
                            lasts = None
                            for tc in range(TCT):
                                lasts = s.activation(sgb[:, coffs[tc]:coffs[tc] + cws[tc]],
                                                     ps[tc][:, 0:cws[tc]], AF.Silu)
                            lasts.then_inc(sc.h, 1)
                    sc.add(nsc)

                @blk.vector
                def _(v):
                    for x in range(NX):
                        for ct in range(NSLT):
                            g_ct = x * NSLT + ct
                            for b in range(nbt):
                                lo, hi = b * 8, min(DT, b * 8 + 8)
                                gb = g_ct * nbt + b
                                v.wait_ge(pe.h, PF((x, "tp", ct, b)))
                                v.tensor_copy(
                                    bass.AP(cfT, lo * CAP + ct * 128,
                                            [[DT * CAP, 128], [CAP, hi - lo], [1, 128]]),
                                    psb[gb % 2][:, 0:(hi - lo) * 128]).then_inc(ve.h, 1)
                        for i in range(IT):
                            v.wait_ge(sc.h, SF((x, i)))
                            lastm = None
                            for tc in range(TCT):
                                lastm = v.tensor_tensor(
                                    hb[:, i * CAP + coffs[tc]: i * CAP + coffs[tc] + cws[tc]],
                                    sgb[:, coffs[tc]:coffs[tc] + cws[tc]],
                                    ps[2 + tc][:, 0:cws[tc]], op=ALU.mult)
                            lastm.then_inc(ve.h, 1)
                        for dc in range(DC):
                            for wi, (ttlo, tthi) in enumerate(DWS):
                                v.wait_ge(pe.h, PF((x, "dn", dc, wi)))
                                if x > 0 and dc == 0 and wi == 0:
                                    v.wait_ge(dY.h, b_dY + x * NSLT * 16)
                                lasty = None
                                for tt in range(ttlo, tthi):
                                    lasty = v.tensor_copy(
                                        y_sb[:, tt * D + dc * DCW: tt * D + (dc + 1) * DCW],
                                        ps[tt % 6][:, 0:DCW])
                                lasty.then_inc(ve.h, 1)
                    ve.add(nve)
        assert pe.n == b_pe + npe and sc.n == b_sc + nsc and ve.n == b_ve + nve
        assert dW.n == b_dW + ndw

    def emit_recombine(tag):
        b_ve, b_dX = ve.base(), dX.base()
        with ExitStack() as rctx:
            r1 = [sbuf("r1%d_%s" % (p, tag), [128, D], BF16, rctx) for p in range(2)]
            r2 = [sbuf("r2%d_%s" % (p, tag), [128, D], BF16, rctx) for p in range(2)]
            with nc.Block() as blk:
                @blk.gpsimd
                def _(gp):
                    for t in range(NT):
                        if t >= 1:
                            gp.wait_ge(dX.h, b_dX + 32 * t)
                        if t >= 2:
                            gp.wait_ge(ve.h, b_ve + (t - 2) + 1)
                        gp.indirect_dma_start(
                            r1[t % 2][:, :], None,
                            ybuf[:, :],
                            IndirectOffsetOnAxis(ap=ptr1_i[:, t:t + 1], axis=0),
                        ).then_inc(dX.h, 16)
                        gp.indirect_dma_start(
                            r2[t % 2][:, :], None,
                            ybuf[:, :],
                            IndirectOffsetOnAxis(ap=ptr2_i[:, t:t + 1], axis=0),
                        ).then_inc(dX.h, 16)
                    dX.add(32 * NT)

                @blk.vector
                def _(v):
                    for t in range(NT):
                        v.wait_ge(dX.h, b_dX + 32 * (t + 1))
                        v.scalar_tensor_tensor(
                            x_res[:, t * D:(t + 1) * D], r1[t % 2][:, :],
                            c1b[:, t:t + 1], x_res[:, t * D:(t + 1) * D],
                            op0=ALU.mult, op1=ALU.add)
                        v.drain()
                        v.scalar_tensor_tensor(
                            x_res[:, t * D:(t + 1) * D], r2[t % 2][:, :],
                            c2b[:, t:t + 1], x_res[:, t * D:(t + 1) * D],
                            op0=ALU.mult, op1=ALU.add)
                        v.drain().then_inc(ve.h, 1)
                    ve.add(NT)

    def emit_qkv(ln_T, tag):
        b_pe, b_ve, b_dW, b_dS = pe.base(), ve.base(), dW.base(), dS.base()
        nvb = (NT + 7) // 8
        PE_H = 3 + nvb
        VE_H = 3 + 1
        W = [wq, wk, wv]
        BIAS = [bqs_sb, bk_sb, bv_sb]
        SCALE = [RSQ, 1.0, 1.0]

        def pe_j(h, j):
            return b_pe + h * PE_H + j + 1

        def pe_vt(h, b):
            return b_pe + h * PE_H + 3 + b + 1

        def ve_j(h, j):
            return b_ve + VE_H * h + j + 1

        def ve_vt(h):
            return b_ve + VE_H * h + 4

        def dw_j(h, j):
            return b_dW + (3 * h + j + 1) * 16

        with ExitStack() as qctx:
            wjb = [sbuf("wj%d_%s" % (j, tag), [128, 2 * DT * 128], BF16, qctx) for j in range(3)]
            js_sb = [sbuf("js%d_%s" % (j, tag), [128, T], BF16, qctx) for j in range(3)]
            vts = sbuf("vts_" + tag, [128, NT * 128], BF16, qctx)
            psq = [psum("qp%d_%s" % (i, tag), [128, 512], F32, qctx) for i in range(4)]
            psb = [psum("qb%d_%s" % (i, tag), [128, 1024], BF16, qctx) for i in range(2)]

            with nc.Block() as blk:
                @blk.gpsimd
                def _(gp):
                    for h in range(H):
                        for j in range(3):
                            buf = wjb[j]
                            par = h % 2
                            if h >= 1:
                                gp.wait_ge(pe.h, pe_j(h - 1, 0))
                            gp.dma_start(
                                bass.AP(buf, par * DT * 128, [[2 * DT * 128, 128], [1, DT * 128]]),
                                bass.AP(W[j], h * 128 * D, [[D, 128], [1, D]]),
                            ).then_inc(dW.h, 16)
                    dW.add(3 * H * 16)

                @blk.tensor
                def _(te):
                    for h in range(H):
                        te.wait_ge(dW.h, b_dW + 3 * (h + 1) * 16)
                        for j in range(3):
                            par = h % 2
                            if j > 0:
                                te.wait_ge(ve.h, ve_j(h, j - 1))
                            elif h > 0:
                                te.wait_ge(ve.h, ve_j(h - 1, 2))
                            last = None
                            for k in range(DT):
                                for tc in range(TC):
                                    last = te.matmul(
                                        psq[tc][:, 0:TCW],
                                        wjb[j][:, (par * DT + k) * 128:(par * DT + k + 1) * 128],
                                        bass.AP(ln_T, k * T + tc * TCW, [[NT * D, 128], [1, TCW]]),
                                        start=(k == 0), stop=(k == DT - 1))
                            last.then_inc(pe.h, 1)
                        te.wait_ge(ve.h, ve_j(h, 2))
                        if h > 0:
                            te.wait_ge(ve.h, ve_vt(h - 1))
                        for b in range(nvb):
                            lo, hi = b * 8, min(NT, b * 8 + 8)
                            last = None
                            for tti in range(lo, hi):
                                last = te.transpose(psb[b % 2][:, (tti - lo) * 128:(tti - lo + 1) * 128],
                                                    js_sb[2][:, tti * 128:(tti + 1) * 128],
                                                    id_sb[:, :])
                            last.then_inc(pe.h, 1)
                    pe.add(H * PE_H)

                @blk.vector
                def _(v):
                    for h in range(H):
                        if h > 0:
                            v.wait_ge(dS.h, b_dS + 3 * h * 16)
                        for j in range(3):
                            v.wait_ge(pe.h, pe_j(h, j))
                            lastq = None
                            for tc in range(TC):
                                if SCALE[j] == 1.0:
                                    lastq = v.tensor_scalar_add(js_sb[j][:, tc * TCW:(tc + 1) * TCW],
                                                                psq[tc][:, 0:TCW], BIAS[j][:, h:h + 1])
                                else:
                                    lastq = v.tensor_scalar(js_sb[j][:, tc * TCW:(tc + 1) * TCW],
                                                            psq[tc][:, 0:TCW], SCALE[j],
                                                            BIAS[j][:, h:h + 1],
                                                            op0=ALU.mult, op1=ALU.add)
                            lastq.then_inc(ve.h, 1)
                        lastv = None
                        for b in range(nvb):
                            lo, hi = b * 8, min(NT, b * 8 + 8)
                            v.wait_ge(pe.h, pe_vt(h, b))
                            lastv = v.tensor_copy(vts[:, lo * 128:lo * 128 + (hi - lo) * 128],
                                                  psb[b % 2][:, 0:(hi - lo) * 128])
                        lastv.then_inc(ve.h, 1)
                    ve.add(H * VE_H)

                @blk.sync
                def _(s):
                    for h in range(H):
                        for j in range(3):
                            s.wait_ge(ve.h, ve_j(h, j))
                            if j == 0:
                                s.dma_start(bass.AP(q_hbm, h * 128 * T, [[T, 128], [1, T]]),
                                            js_sb[0][:, :]).then_inc(dS.h, 16)
                            elif j == 1:
                                s.dma_start(bass.AP(kv_send[h // HG], (h % HG) * 128 * T,
                                                    [[T, 128], [1, T]]),
                                            js_sb[1][:, :]).then_inc(dS.h, 16)
                            else:
                                s.wait_ge(ve.h, ve_vt(h))
                                s.dma_start(
                                    bass.AP(kv_send[h // HG], KSZ + (h % HG) * T * 128,
                                            [[128, 128], [128 * 128, NT], [1, 128]]),
                                    bass.AP(vts, 0, [[NT * 128, 128], [128, NT], [1, 128]]),
                                ).then_inc(dS.h, 16)
                    s.wait_ge(dS.h, dS.n + 3 * H * 16)
                    dS.add(3 * H * 16)
        assert pe.n == b_pe + H * PE_H and ve.n == b_ve + H * VE_H

        with nc.Block() as blk:
            @blk.gpsimd
            def _(gp):
                for i in range(NCC):
                    gp.collective_compute(
                        "AllGather", ALU.bypass,
                        ins=[kv_send[i].ap().opt()],
                        outs=[kv_gath[i].ap().opt()],
                        replica_groups=g.get("RG") or [[0, 1], [2, 3], [4, 5], [6, 7]],
                    ).then_inc(ccS.h)
                    ccS.add(1)
                    if i < NCC - 1:
                        gp.wait_ge(ccS.h, ccS.n)

    def emit_attn(tag):
        SEQ_TOT = 2 * T
        assert NT % 2 == 0
        b_pe, b_sc, b_ve, b_dS = pe.base(), sc.base(), ve.base(), dS.base()
        cc_base = ccS.n - NCC
        pos = []
        for i in range(NT):
            ext = 256 * (i + 1)
            cw = chunk_widths(ext, 512)
            ntr = ext // 128
            nb = (ntr + 7) // 8
            pos.append((ext, cw, ntr, nb))

        def prev3(i):
            return NT - 1 - ((NT - 1 - i) % 3)

        pe_i, sc_i, ve_i = {}, {}, {}
        npe = nsc = nve = 0
        for h in range(H):
            for r in range(NT + 2):
                if r < NT:
                    for ci in range(len(pos[r][1])):
                        npe += 1; pe_i[(h, "s", r, ci)] = npe
                j = r - 2
                if 0 <= j < NT:
                    for b in range(pos[j][3]):
                        npe += 1; pe_i[(h, "t", j, b)] = npe
                    npe += 1; pe_i[(h, "av", j)] = npe
            for i in range(NT):
                for ci in range(len(pos[i][1])):
                    nsc += 1; sc_i[(h, "ev", i, ci)] = nsc
                if i >= 2:
                    nsc += 1; sc_i[(h, "z", i - 2)] = nsc
                nsc += 1; sc_i[(h, "exp", i)] = nsc
            nsc += 1; sc_i[(h, "z", NT - 2)] = nsc
            nsc += 1; sc_i[(h, "z", NT - 1)] = nsc
            for i in range(NT):
                if i >= 2:
                    for b in range(pos[i - 2][3]):
                        nve += 1; ve_i[(h, "te", i - 2, b)] = nve
                nve += 1; ve_i[(h, "nm", i)] = nve
                nve += 1; ve_i[(h, "sca", i)] = nve
            for j in (NT - 2, NT - 1):
                for b in range(pos[j][3]):
                    nve += 1; ve_i[(h, "te", j, b)] = nve

        def P(k):
            return b_pe + pe_i[k]

        def S(k):
            return b_sc + sc_i[k]

        def V(k):
            return b_ve + ve_i[k]

        def DSH(h):
            return b_dS + (h + 1) * 6 * 16

        with ExitStack() as actx:
            k_sb = sbuf("k_" + tag, [128, SEQ_TOT], BF16, actx)
            vts = sbuf("vt_" + tag, [128, SEQ_TOT], BF16, actx)
            q_sb = sbuf("q_" + tag, [128, T], BF16, actx)
            rows = [sbuf("rows%d_%s" % (i, tag), [128, SEQ_TOT], F32, actx) for i in range(2)]
            att = [sbuf("att%d_%s" % (i, tag), [128, SEQ_TOT], BF16, actx) for i in range(3)]
            attT = sbuf("attT_" + tag, [128, SEQ_TOT], BF16, actx)
            zall = sbuf("zall_" + tag, [128, T], BF16, actx)
            pss = [psum("ap%d_%s" % (i, tag), [128, 512], F32, actx) for i in range(4)]
            psz = psum("az_" + tag, [128, 512], F32, actx)
            psb = [psum("ab%d_%s" % (i, tag), [128, 1024], BF16, actx) for i in range(2)]
            NMC = [7, 8]
            DENC = [10, 12]
            RIC = [11, 13]

            def stp(cols, i):
                c = cols[i % 2]
                return st[:, c:c + 1]

            def emit_te_evac(v, h, j):
                for b in range(pos[j][3]):
                    lo, hi = b * 8, min(pos[j][2], b * 8 + 8)
                    v.wait_ge(pe.h, P((h, "t", j, b)))
                    v.tensor_copy(attT[:, lo * 128:lo * 128 + (hi - lo) * 128],
                                  psb[b % 2][:, 0:(hi - lo) * 128]).then_inc(ve.h, 1)

            with nc.Block() as blk:
                @blk.sync
                def _(s):
                    for h in range(H):
                        if h == 0:
                            s.wait_ge(ccS.h, cc_base + 1)
                        elif h == HG:
                            s.wait_ge(ccS.h, cc_base + NCC)
                        if h > 0:
                            s.wait_ge(sc.h, S((h - 1, "z", NT - 1)))
                        gb = kv_gath[h // HG]
                        hh = h % HG
                        for r in range(2):
                            s.dma_start(
                                bass.AP(k_sb, r * 128, [[SEQ_TOT, 128], [256, NT], [1, 128]]),
                                bass.AP(gb, r * 2 * KSZ + hh * 128 * T,
                                        [[T, 128], [128, NT], [1, 128]]),
                            ).then_inc(dS.h, 16)
                            s.dma_start(
                                bass.AP(vts, r * 128, [[SEQ_TOT, 128], [256, NT], [1, 128]]),
                                bass.AP(gb, r * 2 * KSZ + KSZ + hh * T * 128,
                                        [[128, 128], [128 * 128, NT], [1, 128]]),
                            ).then_inc(dS.h, 16)
                        s.dma_start(q_sb[:, :],
                                    bass.AP(q_hbm, h * 128 * T, [[T, 128], [1, T]])
                                    ).then_inc(dS.h, 16)
                        s.wait_ge(sc.h, S((h, "z", NT - 1)))
                        s.dma_start(bass.AP(z_hbm, h * 128 * T, [[T, 128], [1, T]]),
                                    zall[:, :]).then_inc(dS.h, 16)
                        s.wait_ge(dS.h, DSH(h))
                    dS.add(H * 6 * 16)

                @blk.tensor
                def _(te):
                    for h in range(H):
                        te.wait_ge(dS.h, DSH(h) - 16)
                        for r in range(NT + 2):
                            if r < NT:
                                ext, cw, ntr, nb = pos[r]
                                base_bank = (r % 2) * 2
                                off = 0
                                for ci, w in enumerate(cw):
                                    if ci >= 2:
                                        te.wait_ge(sc.h, S((h, "ev", r, ci - 2)))
                                    elif r >= 2:
                                        te.wait_ge(sc.h, S((h, "ev", r - 2, len(pos[r - 2][1]) - 1)))
                                    elif h > 0:
                                        te.wait_ge(sc.h, S((h - 1, "ev", NT - 2 + r, len(pos[NT - 2 + r][1]) - 1)))
                                    te.matmul(pss[base_bank + ci % 2][:, 0:w],
                                              q_sb[:, r * 128:(r + 1) * 128],
                                              k_sb[:, off:off + w],
                                              start=True, stop=True).then_inc(pe.h, 1)
                                    off += w
                            j = r - 2
                            if 0 <= j < NT:
                                ext, cw, ntr, nb = pos[j]
                                te.wait_ge(ve.h, V((h, "sca", j)))
                                for b in range(nb):
                                    lo, hi = b * 8, min(ntr, b * 8 + 8)
                                    if b >= 2:
                                        te.wait_ge(ve.h, V((h, "te", j, b - 2)))
                                    elif j > 0:
                                        te.wait_ge(ve.h, V((h, "te", j - 1, pos[j - 1][3] - 1)))
                                    elif h > 0:
                                        te.wait_ge(ve.h, V((h - 1, "te", NT - 1, pos[NT - 1][3] - 1)))
                                    last = None
                                    for kt in range(lo, hi):
                                        last = te.transpose(
                                            psb[b % 2][:, (kt - lo) * 128:(kt - lo + 1) * 128],
                                            att[j % 3][:, kt * 128:(kt + 1) * 128], id_sb[:, :])
                                    last.then_inc(pe.h, 1)
                                te.wait_ge(ve.h, V((h, "te", j, nb - 1)))
                                if j > 0:
                                    te.wait_ge(sc.h, S((h, "z", j - 1)))
                                elif h > 0:
                                    te.wait_ge(sc.h, S((h - 1, "z", NT - 1)))
                                last = None
                                for kt in range(ntr):
                                    last = te.matmul(psz[:, 0:128],
                                                     vts[:, kt * 128:(kt + 1) * 128],
                                                     attT[:, kt * 128:(kt + 1) * 128],
                                                     start=(kt == 0), stop=(kt == ntr - 1))
                                last.then_inc(pe.h, 1)
                    pe.add(npe)

                @blk.scalar
                def _(s):
                    preload_set(s, "natural_log_exp_and_others")
                    for h in range(H):
                        for i in range(NT):
                            ext, cw, ntr, nb = pos[i]
                            base_bank = (i % 2) * 2
                            off = 0
                            for ci, w in enumerate(cw):
                                s.wait_ge(pe.h, P((h, "s", i, ci)))
                                if ci == 0 and i >= 2:
                                    s.wait_ge(ve.h, V((h, "nm", i - 2)))
                                s.activation(rows[i % 2][:, off:off + w],
                                             pss[base_bank + ci % 2][:, 0:w],
                                             AF.Copy).then_inc(sc.h, 1)
                                off += w
                            if i >= 2:
                                s.wait_ge(pe.h, P((h, "av", i - 2)))
                                s.activation(zall[:, (i - 2) * 128:(i - 1) * 128],
                                             psz[:, 0:128], AF.Copy).then_inc(sc.h, 1)
                            elif i == 0 and h > 0:
                                s.wait_ge(dS.h, DSH(h - 1))
                            s.wait_ge(ve.h, V((h, "nm", i)))
                            if i >= 3:
                                s.wait_ge(pe.h, P((h, "t", i - 3, pos[i - 3][3] - 1)))
                            elif h > 0:
                                pj = prev3(i)
                                s.wait_ge(pe.h, P((h - 1, "t", pj, pos[pj][3] - 1)))
                            s.drain()
                            s.activation(att[i % 3][:, 0:ext], rows[i % 2][:, 0:ext], AF.Exp,
                                         bias=stp(NMC, i),
                                         accum_out=stp(DENC, i)).then_inc(sc.h, 1)
                        for j in (NT - 2, NT - 1):
                            s.wait_ge(pe.h, P((h, "av", j)))
                            s.activation(zall[:, j * 128:(j + 1) * 128],
                                         psz[:, 0:128], AF.Copy).then_inc(sc.h, 1)
                    sc.add(nsc)

                @blk.vector
                def _(v):
                    for h in range(H):
                        for i in range(NT):
                            ext, cw, ntr, nb = pos[i]
                            if i >= 2:
                                emit_te_evac(v, h, i - 2)
                            v.wait_ge(sc.h, S((h, "ev", i, len(cw) - 1)))
                            v.tensor_add(rows[i % 2][:, ext - 256:ext],
                                         rows[i % 2][:, ext - 256:ext], mask_sb[:, :])
                            v.drain()
                            v.reduce_max(stp(NMC, i), rows[i % 2][:, 0:ext], axis=AX.X,
                                         negate=True).then_inc(ve.h, 1)
                            v.wait_ge(sc.h, S((h, "exp", i)))
                            v.reciprocal(stp(RIC, i), stp(DENC, i))
                            v.drain()
                            v.tensor_scalar_mul(att[i % 3][:, 0:ext], att[i % 3][:, 0:ext],
                                                stp(RIC, i)).then_inc(ve.h, 1)
                        for j in (NT - 2, NT - 1):
                            emit_te_evac(v, h, j)
                    ve.add(nve)
        assert pe.n == b_pe + npe and sc.n == b_sc + nsc and ve.n == b_ve + nve

    def emit_oproj(tag):
        b_pe, b_sc, b_ve, b_dS, b_dW = pe.base(), sc.base(), ve.base(), dS.base(), dW.base()
        NG8 = (NT + 7) // 8
        with ExitStack() as octx:
            wob = sbuf("wo_" + tag, [128, H * D], BF16, octx)
            zt8 = sbuf("zt_" + tag, [128, H * 8 * 128], BF16, octx)
            bob = sbuf("bo_" + tag, [128, D], F32, octx)
            tmp = [sbuf("otmp%d_%s" % (i, tag), [128, 512], F32, octx) for i in range(2)]
            pso = [psum("op%d_%s" % (i, tag), [128, 512], F32, octx) for i in range(min(4, NT * DC))]
            nps = len(pso)

            with nc.Block() as blk:
                @blk.gpsimd
                def _(gp):
                    gp.dma_start(
                        bass.AP(wob, 0, [[H * D, 128], [D, H], [1, D]]),
                        bass.AP(wo, 0, [[D, 128], [128 * D, H], [1, D]]),
                    ).then_inc(dW.h, 16)
                    dW.add(16)

                @blk.sync
                def _(s):
                    s.dma_start(bob[:, :], bass.AP(bo, 0, [[0, 128], [1, D]])).then_inc(dS.h, 16)
                    s.wait_ge(dS.h, b_dS + 16)
                    for gp8 in range(NG8):
                        ntt = min(8, NT - gp8 * 8)
                        if gp8 > 0:
                            s.wait_ge(pe.h, b_pe + gp8 * 8 * DC)
                        s.dma_start(
                            bass.AP(zt8, 0, [[H * 8 * 128, 128], [8 * 128, H], [1, ntt * 128]]),
                            bass.AP(z_hbm, gp8 * 8 * 128, [[T, 128], [128 * T, H], [1, ntt * 128]]),
                        ).then_inc(dS.h, 16)
                        s.wait_ge(dS.h, b_dS + (gp8 + 2) * 16)
                    dS.add((NG8 + 1) * 16)

                @blk.tensor
                def _(te):
                    te.wait_ge(dW.h, b_dW + 16)
                    for gp8 in range(NG8):
                        ntt = min(8, NT - gp8 * 8)
                        te.wait_ge(dS.h, b_dS + (gp8 + 2) * 16)
                        for tt in range(ntt):
                            t = gp8 * 8 + tt
                            for dc in range(DC):
                                if t * DC + dc >= nps:
                                    te.wait_ge(ve.h, b_ve + t * DC + dc - nps + 1)
                                last = None
                                for h in range(H):
                                    last = te.matmul(pso[(t * DC + dc) % nps][:, 0:DCW],
                                                     zt8[:, h * 8 * 128 + tt * 128: h * 8 * 128 + (tt + 1) * 128],
                                                     wob[:, h * D + dc * DCW: h * D + (dc + 1) * DCW],
                                                     start=(h == 0), stop=(h == H - 1))
                                last.then_inc(pe.h, 1)
                    pe.add(NT * DC)

                @blk.vector
                def _(v):
                    v.wait_ge(dS.h, b_dS + 16)
                    for t in range(NT):
                        for dc in range(DC):
                            gi = t * DC + dc
                            v.wait_ge(pe.h, b_pe + gi + 1)
                            v.tensor_add(tmp[gi % 2][:, 0:DCW], pso[gi % nps][:, 0:DCW],
                                         bob[:, dc * DCW:(dc + 1) * DCW])
                            v.drain()
                            v.tensor_add(x_res[:, t * D + dc * DCW: t * D + (dc + 1) * DCW],
                                         x_res[:, t * D + dc * DCW: t * D + (dc + 1) * DCW],
                                         tmp[gi % 2][:, 0:DCW]).then_inc(ve.h, 1)
                    ve.add(NT * DC)

    def emit_out():
        b_sc, b_dS = sc.base(), dS.base()
        with ExitStack() as octx:
            of = [sbuf("of%d" % i, [128, D], F32, octx) for i in range(2)]
            with nc.Block() as blk:
                @blk.scalar
                def _(s):
                    for t in range(NT):
                        if t >= 2:
                            s.wait_ge(dS.h, b_dS + (t - 1) * 16)
                        s.activation(of[t % 2][:, :], x_res[:, t * D:(t + 1) * D],
                                     AF.Copy).then_inc(sc.h, 1)
                    sc.add(NT)

                @blk.sync
                def _(s):
                    for t in range(NT):
                        s.wait_ge(sc.h, b_sc + t + 1)
                        s.dma_start(bass.AP(y_out, t * 128 * D, [[D, 128], [1, D]]),
                                    of[t % 2][:, :]).then_inc(dS.h, 16)
                        s.wait_ge(dS.h, b_dS + (t + 1) * 16)
                    dS.add(NT * 16)

    def emit_dump(dst):
        b_sc, b_dS = sc.base(), dS.base()
        with ExitStack() as octx:
            of = [sbuf("dof%d_%s" % (i, dst.name), [128, D], F32, octx) for i in range(2)]
            with nc.Block() as blk:
                @blk.scalar
                def _(s):
                    for t in range(NT):
                        if t >= 2:
                            s.wait_ge(dS.h, b_dS + (t - 1) * 16)
                        s.activation(of[t % 2][:, :], x_res[:, t * D:(t + 1) * D],
                                     AF.Copy).then_inc(sc.h, 1)
                    sc.add(NT)

                @blk.sync
                def _(s):
                    for t in range(NT):
                        s.wait_ge(sc.h, b_sc + t + 1)
                        s.dma_start(bass.AP(dst, t * 128 * D, [[D, 128], [1, D]]),
                                    of[t % 2][:, :]).then_inc(dS.h, 16)
                        s.wait_ge(dS.h, b_dS + (t + 1) * 16)
                    dS.add(NT * 16)

    STOP = g.get("STOP")
    with ExitStack() as inv1:
        ln_T = sbuf("lnT1", [128, NT * D], BF16, inv1)
        emit_ln(ln_T, "l1", store_rows=True)
        if DBG:
            emit_sbuf_dump(dbg_lnT, ln_T, NT * D)
            emit_sbuf_dump(dbg_st, st, 32)
        if STOP == "ln1":
            return nc
        emit_routing(ln_T, "r1")
        if g.get("DBGSL"):
            emit_sbuf_dump(dbg_sl, sslot_i, NT * E)
            emit_sbuf_dump(dbg_p1, ptr1_i, NT)
            emit_sbuf_dump(dbg_p2, ptr2_i, NT)
            emit_sbuf_dump(dbg_cc, c_all, NT * E)
        if DBG:
            emit_sbuf_dump(dbg_c, c_all, NT * E)
        if STOP == "route1":
            return nc
        emit_ffn(ln_T, "f1")
        if DBG:
            emit_sbuf_dump(dbg_h, hb_last[0], IT * T)
    emit_sparse("s1")
    emit_recombine("g1")
    if DBG >= 1:
        emit_dump(dbg_outs[0])
    if STOP == "ffn1":
        return nc
    with ExitStack() as att:
        ln_T = sbuf("lnT2", [128, NT * D], BF16, att)
        emit_ln(ln_T, "l2")
        emit_qkv(ln_T, "qk")
    emit_attn("at")
    emit_oproj("op")
    if DBG >= 2:
        emit_dump(dbg_outs[1])
    with ExitStack() as inv2:
        ln_T = sbuf("lnT3", [128, NT * D], BF16, inv2)
        emit_ln(ln_T, "l3", store_rows=True)
        emit_routing(ln_T, "r3")
        emit_ffn(ln_T, "f3")
    emit_sparse("s3")
    emit_recombine("g3")
    emit_out()
    return nc


def make_inputs(core, full, g):
    D, T, I, E, H = g["D"], g["T"], g["I"], g["E"], g["H"]
    NT = T // 128
    b, r = core // 2, core % 2
    xb = np.ascontiguousarray(
        full["L0_attn_x"][b].reshape(2 * NT, 128, D)[r::2])
    mask = np.where(np.arange(256)[None, :] <= 128 * r + np.arange(128)[:, None],
                    np.float32(0), np.float32(-1e9 if False else -100000.0)).astype(np.float32)
    E_, I_, H_ = g["E"], g["I"], g["H"]
    DT_, IT_ = D // 128, I_ // 128

    def gu5(expert_mat, shared_mat):
        egT = expert_mat.transpose(0, 2, 1)
        e5 = egT.reshape(E_, DT_, 128, IT_, 128).transpose(0, 3, 4, 1, 2)
        e5 = egT.reshape(E_, DT_, 128, IT_, 128).transpose(0, 3, 2, 1, 4)
        e5 = e5.reshape(E_, IT_, 128, DT_ * 128)
        shT = shared_mat.T
        s5 = shT.reshape(DT_, 128, 2, IT_, 128).transpose(2, 3, 1, 0, 4)
        s5 = s5.reshape(2, IT_, 128, DT_ * 128)
        return np.ascontiguousarray(np.concatenate([e5, s5], axis=0))

    def d5(wT):
        G = wT.shape[0]
        DC_ = max(1, D // 512)
        DCW_ = min(D, 512)
        w = wT.reshape(G, IT_, 128, DC_, DCW_).transpose(0, 3, 2, 1, 4)
        return np.ascontiguousarray(w.reshape(G, DC_, 128, IT_ * DCW_))

    def qkv4(w):
        return np.ascontiguousarray(
            w.reshape(H_, DT_, 128, 128).transpose(0, 2, 1, 3).reshape(H_, 128, D))

    im = dict(
        x_in=xb,
        wq=qkv4(full["L0_attn_W_Q"]), wk=qkv4(full["L0_attn_W_K"]),
        wv=qkv4(full["L0_attn_W_V"]),
        wo=full["L0_attn_W_O"],
        bqT=np.ascontiguousarray(full["L0_attn_b_Q"].T),
        bkT=np.ascontiguousarray(full["L0_attn_b_K"].T),
        bvT=np.ascontiguousarray(full["L0_attn_b_V"].T),
        bo=full["L0_attn_b_O"],
        gate_wT=np.ascontiguousarray(full["L0_ffn_gate_w"].T),
        wg5=gu5(full["L0_ffn_expert_gate"], full["L0_ffn_shared_gate"]),
        wu5=gu5(full["L0_ffn_expert_up"], full["L0_ffn_shared_up"]),
        wed5=d5(full["L0_ffn_expert_down"].transpose(0, 2, 1).reshape(E_, I_, D)),
        wsd5=d5(full["L0_ffn_shared_down"].T.reshape(2, I_, D)),
        idm=np.eye(128, dtype=ml_dtypes.bfloat16),
        ltm=np.triu(np.ones((128, 128), np.float32), 1).astype(ml_dtypes.bfloat16),
        eoff=np.ascontiguousarray(np.broadcast_to(
            (np.arange(E_) * np.float32(g.get("CAP", T // 2))).astype(np.float32),
            (128, E_))),
        amask=mask,
    )
    return im


def kernel(**inputs):
    g = FULL_GEO
    D, T = g["D"], g["T"]
    NT = T // 128
    nc = build_nc(g)
    shared = None
    in_maps = []
    for c in range(8):
        im = make_inputs(c, inputs, g)
        in_maps.append(im)
    res = run_bass_kernel_spmd(nc, in_maps, core_ids=list(range(8)))
    out = np.zeros((4, 4096, D), np.float32)
    for c in range(8):
        b, r = c // 2, c % 2
        y = res.results[c]["y_out"]
        out[b].reshape(2 * NT, 128, D)[r::2] = y
    return out


# version 12
# speedup vs baseline: 1.3741x; 1.0926x over previous
import numpy as np
import ml_dtypes
from contextlib import ExitStack

import concourse.bass as bass
import concourse.mybir as mybir
from concourse.bass_core_rust import IndirectOffsetOnAxis
from concourse.bass_utils import run_bass_kernel_spmd

F32 = mybir.dt.float32
BF16 = mybir.dt.bfloat16
AF = mybir.ActivationFunctionType
ALU = mybir.AluOpType
AX = mybir.AxisListType

RSQ = float(1.0 / np.sqrt(np.float32(128.0)))
LN_EPS = 1e-5

FULL_GEO = dict(D=2048, T=2048, I=1408, E=8, H=16)


class Sem:
    def __init__(self, ctx, nc, name):
        self.h = ctx.enter_context(nc.semaphore(name))
        self.n = 0

    def base(self):
        return self.n

    def add(self, amt=1):
        self.n += amt
        return self.n


def build_nc(g):
    D, T, I, E, H = g["D"], g["T"], g["I"], g["E"], g["H"]
    ISH = 2 * I
    DT = D // 128
    NT = T // 128
    IT = I // 128
    TC = max(1, T // 512)
    TCW = min(T, 512)
    NEXP = E + 2
    CAP = g.get("CAP", T // 2)
    assert CAP % 128 == 0 and CAP % NT == 0
    CT = CAP // NT
    CH = min(512, CAP)
    NHF = (CAP + CH - 1) // CH
    NST = CH // 128
    ZROW = E * CAP
    NTG = (NT + 7) // 8
    DC = max(1, D // 512)
    DCW = min(D, 512)
    assert D % 128 == 0 and T % 128 == 0 and I % 128 == 0 and T % TCW == 0 and D % DCW == 0

    nc = bass.Bass("TRN2", target_bir_lowering=False)

    x_in = nc.dram_tensor("x_in", [NT, 128, D], F32, kind="ExternalInput")
    wq = nc.dram_tensor("wq", [H, 128, D], F32, kind="ExternalInput")
    wk = nc.dram_tensor("wk", [H, 128, D], F32, kind="ExternalInput")
    wv = nc.dram_tensor("wv", [H, 128, D], F32, kind="ExternalInput")
    wo = nc.dram_tensor("wo", [H, 128, D], F32, kind="ExternalInput")
    bqT = nc.dram_tensor("bqT", [128, H], F32, kind="ExternalInput")
    bkT = nc.dram_tensor("bkT", [128, H], F32, kind="ExternalInput")
    bvT = nc.dram_tensor("bvT", [128, H], F32, kind="ExternalInput")
    bo = nc.dram_tensor("bo", [D], F32, kind="ExternalInput")
    gate_wT = nc.dram_tensor("gate_wT", [D, E], F32, kind="ExternalInput")
    wg5 = nc.dram_tensor("wg5", [E + 2, IT, 128, DT * 128], F32, kind="ExternalInput")
    wu5 = nc.dram_tensor("wu5", [E + 2, IT, 128, DT * 128], F32, kind="ExternalInput")
    wed5 = nc.dram_tensor("wed5", [E, DC, 128, IT * DCW], F32, kind="ExternalInput")
    wsd5 = nc.dram_tensor("wsd5", [2, DC, 128, IT * DCW], F32, kind="ExternalInput")
    idm = nc.dram_tensor("idm", [128, 128], BF16, kind="ExternalInput")
    ltm = nc.dram_tensor("ltm", [128, 128], BF16, kind="ExternalInput")
    eoff = nc.dram_tensor("eoff", [128, E], F32, kind="ExternalInput")
    amask = nc.dram_tensor("amask", [128, 256], F32, kind="ExternalInput")
    y_out = nc.dram_tensor("y_out", [NT, 128, D], F32, kind="ExternalOutput")
    DBG = g.get("DBG", 0)
    dbg_outs = [nc.dram_tensor("dbg%d" % i, [NT, 128, D], F32, kind="ExternalOutput")
                for i in range(DBG)]
    if g.get("DBGSL"):
        dbg_sl = nc.dram_tensor("dbg_sl", [128, NT * E], mybir.dt.int32, kind="ExternalOutput")
        dbg_p1 = nc.dram_tensor("dbg_p1", [128, NT], mybir.dt.int32, kind="ExternalOutput")
        dbg_p2 = nc.dram_tensor("dbg_p2", [128, NT], mybir.dt.int32, kind="ExternalOutput")
        dbg_cc = nc.dram_tensor("dbg_cc", [128, NT * E], F32, kind="ExternalOutput")
    if DBG:
        dbg_lnT = nc.dram_tensor("dbg_lnT", [128, NT * D], BF16, kind="ExternalOutput")
        dbg_c = nc.dram_tensor("dbg_c", [128, NT * E], F32, kind="ExternalOutput")
        dbg_h = nc.dram_tensor("dbg_h", [128, IT * T], BF16, kind="ExternalOutput")
        dbg_st = nc.dram_tensor("dbg_st", [128, 32], F32, kind="ExternalOutput")

    def emit_sbuf_dump(dst, src_sb, cols):
        with nc.Block() as blk:
            @blk.sync
            def _(s):
                s.dma_start(bass.AP(dst, 0, [[cols, 128], [1, cols]]),
                            src_sb[:, 0:cols]).then_inc(dS.h, 16)
                s.wait_ge(dS.h, dS.add(16))

    HG = max(1, H // 2)
    NCC = (H + HG - 1) // HG
    KSZ = HG * 128 * T
    kv_send = [nc.dram_tensor("kv_send%d" % i, [2 * KSZ // 512, 512], BF16)
               for i in range(NCC)]
    kv_gath = [nc.dram_tensor("kv_gath%d" % i, [4 * KSZ // 512, 512], BF16)
               for i in range(NCC)]
    q_hbm = nc.dram_tensor("q_hbm", [H, 128, T], BF16)
    z_hbm = nc.dram_tensor("z_hbm", [H, 128, T], BF16)
    ln_hbm = nc.dram_tensor("ln_hbm", [T, D], BF16)
    cbuf = nc.dram_tensor("cbuf", [E, CAP, D], BF16)
    ybuf = nc.dram_tensor("ybuf", [E * CAP + 128, D], BF16)

    ctx = ExitStack()
    nc._kernel_ctx = ctx

    def sbuf(name, shape, dt, scope=None):
        return (scope or ctx).enter_context(nc.sbuf_tensor(name, shape, dt))

    def psum(name, shape, dt, scope=None):
        return (scope or ctx).enter_context(nc.psum_tensor(name, shape, dt))

    x_res = sbuf("x_res", [128, NT * D], BF16)
    id_sb = sbuf("id_sb", [128, 128], BF16)
    st = sbuf("st", [128, 32], F32)
    c_all = sbuf("c_all", [128, NT * E], F32)
    bq_sb = sbuf("bq_sb", [128, H], F32)
    bk_sb = sbuf("bk_sb", [128, H], F32)
    bv_sb = sbuf("bv_sb", [128, H], F32)
    bqs_sb = sbuf("bqs_sb", [128, H], F32)
    gw_sb = sbuf("gw_sb", [128, DT * E], BF16)
    mask_sb = sbuf("mask_sb", [128, 256], F32)
    lt_sb = sbuf("lt_sb", [128, 128], BF16)
    eo_sb = sbuf("eo_sb", [128, E], F32)
    sslot_i = sbuf("sslot_i", [128, NT * E], mybir.dt.int32)
    ptr1_i = sbuf("ptr1_i", [128, NT], mybir.dt.int32)
    ptr2_i = sbuf("ptr2_i", [128, NT], mybir.dt.int32)
    c1b = sbuf("c1b", [128, NT], F32)
    c2b = sbuf("c2b", [128, NT], F32)
    rstg = sbuf("rstg", [128, D], BF16)

    sems = ExitStack()
    ctx.enter_context(sems)
    dW = Sem(ctx, nc, "dW")
    dS = Sem(ctx, nc, "dS")
    pe = Sem(ctx, nc, "pe")
    sc = Sem(ctx, nc, "sc")
    ve = Sem(ctx, nc, "ve")
    ccS = Sem(ctx, nc, "ccS")
    dX = Sem(ctx, nc, "dX")
    dY = Sem(ctx, nc, "dY")

    COL = dict(sum=0, sq=1, mean=2, e2=3, var=4, rstd=5, mr=6,
               negmax=7, m1=8, m2=9, den=10, rinv=11)

    def stc(name):
        c = COL[name]
        return st[:, c:c + 1]

    from concourse.hw_specs import get_activation_tables
    _tabs = list(get_activation_tables(nc.m.arch).keys())

    def preload_set(s, name):
        s.add_instruction(mybir.InstLoadActFuncSet(
            name=nc.get_next_instruction_name(),
            act_func_set_id=_tabs.index(name), ins=[], outs=[]))

    def chunk_widths(total, w):
        out = []
        o = 0
        while o < total:
            out.append(min(w, total - o))
            o += w
        return out

    with nc.Block() as blk:
        @blk.sync
        def _(s):
            s.dma_start(id_sb[:, :], idm[:, :]).then_inc(dS.h, 16); dS.add(16)
            s.dma_start(mask_sb[:, :], amask[:, :]).then_inc(dS.h, 16); dS.add(16)
            s.dma_start(bq_sb[:, :], bqT[:, :]).then_inc(dS.h, 16); dS.add(16)
            s.dma_start(bk_sb[:, :], bkT[:, :]).then_inc(dS.h, 16); dS.add(16)
            s.dma_start(bv_sb[:, :], bvT[:, :]).then_inc(dS.h, 16); dS.add(16)
            s.dma_start(lt_sb[:, :], ltm[:, :]).then_inc(dS.h, 16); dS.add(16)
            s.dma_start(eo_sb[:, :], eoff[:, :]).then_inc(dS.h, 16); dS.add(16)
            s.wait_ge(dS.h, dS.n)

        @blk.gpsimd
        def _(gp):
            gp.dma_start(
                bass.AP(x_res, 0, [[NT * D, 128], [D, NT], [1, D]]),
                bass.AP(x_in, 0, [[D, 128], [128 * D, NT], [1, D]]),
            ).then_inc(dW.h, 16); dW.add(16)
            gp.dma_start(
                bass.AP(gw_sb, 0, [[DT * E, 128], [E, DT], [1, E]]),
                bass.AP(gate_wT, 0, [[E, 128], [128 * E, DT], [1, E]]),
            ).then_inc(dW.h, 16); dW.add(16)
            gp.wait_ge(dW.h, dW.n)
            gp.wait_ge(ve.h, ve.add(1))
            for j in range(E * CAP // 128):
                gp.dma_start(bass.AP(cbuf, j * 128 * D, [[D, 128], [1, D]]),
                             rstg[:, :]).then_inc(dW.h, 16); dW.add(16)
            gp.dma_start(bass.AP(ybuf, ZROW * D, [[D, 128], [1, D]]),
                         rstg[:, :]).then_inc(dW.h, 16); dW.add(16)

        @blk.vector
        def _(v):
            v.memset(rstg[:, :], 0.0)
            v.drain().then_inc(ve.h, 1)
            v.wait_ge(dS.h, dS.n)
            v.tensor_scalar_mul(bqs_sb[:, :], bq_sb[:, :], RSQ)

    zfill_dW = dW.n

    def emit_ln(ln_T, tag, store_rows=False):
        nbat = (DT + 7) // 8
        b_sc, b_ve, b_pe = sc.base(), ve.base(), pe.base()
        b_dS = dS.base()
        with ExitStack() as lctx:
            dump = sbuf("lndump_" + tag, [128, D], BF16, lctx)
            lntok = sbuf("lntok_" + tag, [128, D], BF16, lctx)
            psb = [psum("lnp%d_%s" % (i, tag), [128, 1024], BF16, lctx) for i in range(2)]
            with nc.Block() as blk:
                @blk.scalar
                def _(s):
                    preload_set(s, "natural_log_exp_and_others")
                    for t in range(NT):
                        if t > 0:
                            s.wait_ge(ve.h, b_ve + 3 * (t - 1) + 1)
                        s.activation(dump[:, :], x_res[:, t * D:(t + 1) * D],
                                     AF.Square, accum_out=stc("sq")).then_inc(sc.h, 1)
                        s.wait_ge(ve.h, b_ve + 3 * t + 1)
                        s.activation(stc("rstd"), stc("var"), AF.Ln)
                        s.drain()
                        s.activation(stc("rstd"), stc("rstd"), AF.Exp,
                                     scale=-0.5).then_inc(sc.h, 1)
                sc.add(2 * NT)

                @blk.vector
                def _(v):
                    for t in range(NT):
                        v.reduce_sum(stc("sum"), x_res[:, t * D:(t + 1) * D], axis=AX.X)
                        v.wait_ge(sc.h, b_sc + 2 * t + 1)
                        v.drain()
                        v.tensor_scalar_mul(stc("mean"), stc("sum"), 1.0 / D)
                        v.tensor_scalar_mul(stc("e2"), stc("sq"), 1.0 / D)
                        v.drain()
                        v.tensor_mul(stc("var"), stc("mean"), stc("mean"))
                        v.drain()
                        v.tensor_sub(stc("var"), stc("e2"), stc("var"))
                        v.drain()
                        v.tensor_scalar_add(stc("var"), stc("var"),
                                            LN_EPS).then_inc(ve.h, 1)
                        v.wait_ge(sc.h, b_sc + 2 * t + 2)
                        v.tensor_mul(stc("mr"), stc("mean"), stc("rstd"))
                        v.drain()
                        if t > 0:
                            v.wait_ge(pe.h, b_pe + nbat * t)
                        if store_rows and t > 0:
                            v.wait_ge(dS.h, b_dS + t * 16)
                        v.tensor_scalar(lntok[:, :], x_res[:, t * D:(t + 1) * D],
                                        stc("rstd"), stc("mr"),
                                        op0=ALU.mult, op1=ALU.subtract).then_inc(ve.h, 1)
                        lastc = None
                        for b in range(nbat):
                            lo, hi = b * 8, min(DT, b * 8 + 8)
                            v.wait_ge(pe.h, b_pe + nbat * t + b + 1)
                            lastc = v.tensor_copy(
                                bass.AP(ln_T, lo * T + t * 128,
                                        [[NT * D, 128], [T, hi - lo], [1, 128]]),
                                psb[b % 2][:, 0:(hi - lo) * 128])
                        lastc.then_inc(ve.h, 1)
                ve.add(3 * NT)

                @blk.tensor
                def _(te):
                    for t in range(NT):
                        te.wait_ge(ve.h, b_ve + 3 * t + 2)
                        if t > 0:
                            te.wait_ge(ve.h, b_ve + 3 * (t - 1) + 3)
                        for b in range(nbat):
                            lo, hi = b * 8, min(DT, b * 8 + 8)
                            last = None
                            for d in range(lo, hi):
                                last = te.transpose(
                                    psb[b % 2][:, (d - lo) * 128:(d - lo + 1) * 128],
                                    lntok[:, d * 128:(d + 1) * 128], id_sb[:, :])
                            last.then_inc(pe.h, 1)
                pe.add(nbat * NT)

                if store_rows:
                    @blk.sync
                    def _(s):
                        for t in range(NT):
                            s.wait_ge(ve.h, b_ve + 3 * t + 2)
                            s.dma_start(bass.AP(ln_hbm, t * 128 * D, [[D, 128], [1, D]]),
                                        lntok[:, :]).then_inc(dS.h, 16)
                        s.wait_ge(dS.h, b_dS + NT * 16)
                    dS.add(NT * 16)
        assert sc.n == b_sc + 2 * NT and ve.n == b_ve + 3 * NT and pe.n == b_pe + nbat * NT

    def emit_routing(ln_T, tag):
        b_pe, b_sc, b_ve, b_dW = pe.base(), sc.base(), ve.base(), dW.base()
        with ExitStack() as rctx:
            rp = psum("rp_" + tag, [128, 512], F32, rctx)
            rp2 = psum("rp2_" + tag, [128, 512], F32, rctx)
            lg = sbuf("lg_" + tag, [128, E], F32, rctx)
            ex = sbuf("ex_" + tag, [128, E], F32, rctx)
            zz = sbuf("zz_" + tag, [128, 1], F32, rctx)
            selb = sbuf("sel_" + tag, [128, E], F32, rctx)
            pmb = sbuf("pm_" + tag, [128, E], F32, rctx)
            selr = sbuf("selr_" + tag, [128, E], F32, rctx)
            selbf = sbuf("selbf_" + tag, [128, E], BF16, rctx)
            rkb = sbuf("rk_" + tag, [128, E], F32, rctx)
            slot = sbuf("slot_" + tag, [128, E], F32, rctx)
            selv = sbuf("selv_" + tag, [128, E], F32, rctx)
            t8a = sbuf("t8a_" + tag, [128, E], F32, rctx)
            t8b = sbuf("t8b_" + tag, [128, E], F32, rctx)
            t1a = sbuf("t1a_" + tag, [128, 1], F32, rctx)
            t1b = sbuf("t1b_" + tag, [128, 1], F32, rctx)
            with nc.Block() as blk:
                @blk.tensor
                def _(te):
                    for t in range(NT):
                        if t > 0:
                            te.wait_ge(sc.h, b_sc + 2 * (t - 1) + 1)
                        last = None
                        for d in range(DT):
                            last = te.matmul(rp[:, 0:E],
                                             bass.AP(ln_T, d * T + t * 128, [[NT * D, 128], [1, 128]]),
                                             gw_sb[:, d * E:(d + 1) * E],
                                             start=(d == 0), stop=(d == DT - 1))
                        last.then_inc(pe.h, 1)
                        te.wait_ge(ve.h, b_ve + 5 * t + 3)
                        te.matmul(rp2[:, 0:E], lt_sb[:, :], selbf[:, :],
                                  start=True, stop=True).then_inc(pe.h, 1)
                pe.add(2 * NT)

                @blk.scalar
                def _(s):
                    for t in range(NT):
                        s.wait_ge(pe.h, b_pe + 2 * t + 1)
                        if t > 0:
                            s.wait_ge(ve.h, b_ve + 5 * (t - 1) + 2)
                        s.activation(lg[:, :], rp[:, 0:E], AF.Copy).then_inc(sc.h, 1)
                        s.wait_ge(ve.h, b_ve + 5 * t + 1)
                        s.activation(ex[:, :], lg[:, :], AF.Exp, bias=stc("negmax"),
                                     accum_out=zz[:, :]).then_inc(sc.h, 1)
                sc.add(2 * NT)

                @blk.vector
                def _(v):
                    for t in range(NT):
                        csl = c_all[:, t * E:(t + 1) * E]
                        v.wait_ge(sc.h, b_sc + 2 * t + 1)
                        v.reduce_max(stc("negmax"), lg[:, :], axis=AX.X,
                                     negate=True).then_inc(ve.h, 1)
                        v.wait_ge(sc.h, b_sc + 2 * t + 2)
                        v.reduce_max(stc("m1"), ex[:, :], axis=AX.X)
                        v.drain()
                        v.tensor_single_scalar(selb[:, :], ex[:, :], stc("m1"), op=ALU.is_ge)
                        v.drain()
                        v.scalar_tensor_tensor(pmb[:, :], selb[:, :], -1.0e9, ex[:, :],
                                               op0=ALU.mult, op1=ALU.add)
                        v.drain()
                        v.reduce_max(stc("m2"), pmb[:, :], axis=AX.X)
                        v.drain()
                        v.tensor_single_scalar(selb[:, :], ex[:, :], stc("m2"), op=ALU.is_ge)
                        v.tensor_tensor(stc("den"), stc("m1"), stc("m2"), op=ALU.add)
                        v.drain()
                        v.scalar_tensor_tensor(stc("den"), zz[:, :], 1.0e-20, stc("den"),
                                               op0=ALU.mult, op1=ALU.add)
                        v.drain()
                        v.reciprocal(stc("rinv"), stc("den"))
                        v.tensor_mul(csl, ex[:, :], selb[:, :])
                        v.drain()
                        v.tensor_scalar(csl, csl, stc("rinv"), 0.5,
                                        op0=ALU.mult, op1=ALU.mult).then_inc(ve.h, 1)
                        v.drain()
                        v.tensor_single_scalar(selr[:, :], csl, 0.0, op=ALU.is_gt)
                        v.drain()
                        v.tensor_copy(selbf[:, :], selr[:, :]).then_inc(ve.h, 1)
                        v.wait_ge(pe.h, b_pe + 2 * t + 2)
                        v.tensor_copy(rkb[:, :], rp2[:, 0:E])
                        v.drain()
                        v.tensor_scalar_add(slot[:, :], rkb[:, :], float(t * CT))
                        v.tensor_single_scalar(selv[:, :], rkb[:, :], float(CT), op=ALU.is_lt)
                        v.drain()
                        v.tensor_mul(selv[:, :], selv[:, :], selr[:, :])
                        v.drain()
                        v.tensor_mul(t8a[:, :], slot[:, :], selv[:, :])
                        v.tensor_scalar(t8b[:, :], selv[:, :], -1.0e5, 1.0e5,
                                        op0=ALU.mult, op1=ALU.add)
                        v.drain()
                        v.tensor_add(t8a[:, :], t8a[:, :], t8b[:, :])
                        v.drain()
                        v.tensor_add(t8a[:, :], t8a[:, :], eo_sb[:, :])
                        v.drain()
                        v.tensor_copy(sslot_i[:, t * E:(t + 1) * E], t8a[:, :]).then_inc(ve.h, 1)
                        v.reduce_max(c1b[:, t:t + 1], csl, axis=AX.X)
                        v.reduce_sum(t1a[:, :], csl, axis=AX.X)
                        v.drain()
                        v.tensor_sub(c2b[:, t:t + 1], t1a[:, :], c1b[:, t:t + 1])
                        v.tensor_single_scalar(t8b[:, :], csl, c1b[:, t:t + 1], op=ALU.is_ge)
                        v.drain()
                        v.tensor_mul(t8b[:, :], t8b[:, :], selv[:, :])
                        v.tensor_add(t8a[:, :], slot[:, :], eo_sb[:, :])
                        v.drain()
                        v.tensor_mul(selb[:, :], t8b[:, :], t8a[:, :])
                        v.drain()
                        v.reduce_sum(t1a[:, :], selb[:, :], axis=AX.X)
                        v.reduce_sum(t1b[:, :], t8b[:, :], axis=AX.X)
                        v.drain()
                        v.tensor_scalar(t1b[:, :], t1b[:, :], float(-ZROW), float(ZROW),
                                        op0=ALU.mult, op1=ALU.add)
                        v.drain()
                        v.tensor_add(t1a[:, :], t1a[:, :], t1b[:, :])
                        v.drain()
                        v.tensor_copy(ptr1_i[:, t:t + 1], t1a[:, :])
                        v.tensor_sub(t8b[:, :], selv[:, :], t8b[:, :])
                        v.drain()
                        v.tensor_mul(selb[:, :], t8b[:, :], t8a[:, :])
                        v.drain()
                        v.reduce_sum(t1a[:, :], selb[:, :], axis=AX.X)
                        v.reduce_sum(t1b[:, :], t8b[:, :], axis=AX.X)
                        v.drain()
                        v.tensor_scalar(t1b[:, :], t1b[:, :], float(-ZROW), float(ZROW),
                                        op0=ALU.mult, op1=ALU.add)
                        v.drain()
                        v.tensor_add(t1a[:, :], t1a[:, :], t1b[:, :])
                        v.drain()
                        v.tensor_copy(ptr2_i[:, t:t + 1], t1a[:, :]).then_inc(ve.h, 1)
                ve.add(5 * NT)

        assert pe.n == b_pe + 2 * NT and sc.n == b_sc + 2 * NT and ve.n == b_ve + 5 * NT

    hb_last = [None]

    def emit_ffn(ln_T, tag):
        KT = DT
        mid_k = KT // 2
        mid_i = max(1, IT // 2)
        gu_waves = [(0, TC)] if TC <= 2 else [(0, TC // 2), (TC // 2, TC)]
        tg_sizes = [min(8, NT - 8 * tgi) for tgi in range(NTG)]

        def dn_waves(ntt):
            return [(0, ntt)] if ntt <= 4 else [(0, 4), (4, ntt)]

        experts = []
        s1 = (IT + 1) // 2 if IT >= 4 else IT
        for half in range(2):
            for jlo, jhi in [(0, s1), (s1, IT)]:
                if jhi > jlo:
                    experts.append(dict(kind="sh", e=None, d5=wsd5, d5e=half,
                                        wex=E + half, ioff=jlo, it=jhi - jlo))
        NX = len(experts)
        ITX = max(ex["it"] for ex in experts)
        NW = len(gu_waves)

        b_pe, b_sc, b_ve, b_dW = pe.base(), sc.base(), ve.base(), dW.base()

        pe_i, sc_i, ve_i, dw_i = {}, {}, {}, {}
        npe = nsc = nve = ndw = 0
        for x in range(NX):
            xit = experts[x]["it"]
            for i in range(xit):
                for q in range(2):
                    ndw += 16; dw_i[(x, "gu", i, q)] = ndw
            for dc in range(DC):
                for hlf in range(2):
                    ndw += 16; dw_i[(x, "dn", dc, hlf)] = ndw
            for i in range(xit):
                for w in range(NW):
                    npe += 1; pe_i[(x, "gu", i, w)] = npe
            for dc in range(DC):
                for tgi in range(NTG):
                    ntt = tg_sizes[tgi]
                    c = dc * NTG + tgi
                    dws = dn_waves(ntt)
                    for wi in range(len(dws)):
                        if wi == len(dws) - 1:
                            npe += 1; pe_i[(x, "dnmid", c)] = npe
                        npe += 1; pe_i[(x, "dn", c, wi)] = npe
            for i in range(xit):
                for w in range(NW):
                    nsc += 1; sc_i[(x, i, w)] = nsc
            for i in range(xit):
                for w in range(NW):
                    nve += 1; ve_i[(x, "h", i, w)] = nve
            for dc in range(DC):
                for tgi in range(NTG):
                    ntt = tg_sizes[tgi]
                    c = dc * NTG + tgi
                    for wi in range(len(dn_waves(ntt))):
                        nve += 1; ve_i[(x, "x", c, wi)] = nve

        def PF(k):
            return b_pe + pe_i[k]

        def SF(k):
            return b_sc + sc_i[k]

        def VF(k):
            return b_ve + ve_i[k]

        def DWF(k):
            return b_dW + dw_i[k]

        NCH = NTG * DC

        with ExitStack() as fctx:
            wgb = [sbuf("wg%d_%s" % (p, tag), [128, KT * 128], BF16, fctx) for p in range(2)]
            wub = [sbuf("wu%d_%s" % (p, tag), [128, KT * 128], BF16, fctx) for p in range(2)]
            wdb = sbuf("wd_" + tag, [128, ITX * DCW], BF16, fctx)
            sgb = sbuf("sg_" + tag, [128, TC * TCW], BF16, fctx)
            hb = sbuf("h_" + tag, [128, ITX * T], BF16, fctx)
            hb_last[0] = hb
            ps = [psum("fp%d_%s" % (i, tag), [128, 512], F32, fctx) for i in range(8)]

            with nc.Block() as blk:
                @blk.gpsimd
                def _(gp):
                    gp.wait_ge(dW.h, zfill_dW)
                    b_dX = dX.base()
                    breg = gp.alloc_register()
                    gp.reg_mov(breg, E * CAP - 1)
                    tgrp = [(x * NT // NX, (x + 1) * NT // NX) for x in range(NX)]
                    for x in range(NX):
                        xit = experts[x]["it"]
                        for i in range(xit):
                            par = (experts[x]["ioff"] + i) % 2
                            for q in range(2):
                                dst = (wgb if q == 0 else wub)[par]
                                if i >= 1:
                                    gp.wait_ge(pe.h, PF((x, "gu", i - 1, 0)))
                                elif x > 0:
                                    gp.wait_ge(pe.h, PF((x - 1, "dnmid", NCH - 1)))
                                wgidx = experts[x]["wex"] * IT + experts[x]["ioff"] + i
                                gp.dma_start(
                                    bass.AP(dst, 0, [[KT * 128, 128], [1, KT * 128]]),
                                    bass.AP(wg5 if q == 0 else wu5,
                                            (wgidx * 128) * (KT * 128),
                                            [[KT * 128, 128], [1, KT * 128]]),
                                ).then_inc(dW.h, 16)
                        ex = experts[x]
                        xmid = max(1, xit // 2)
                        for dc in range(DC):
                            for hlf in range(2):
                                ilo = hlf * xmid
                                ihi = xmid if hlf == 0 else xit - xmid
                                if dc > 0:
                                    pc = dc * NTG - 1
                                    gp.wait_ge(pe.h, PF((x, "dnmid", pc)) if hlf == 0
                                               else PF((x, "dn", pc, len(dn_waves(tg_sizes[NTG - 1])) - 1)))
                                else:
                                    gp.wait_ge(pe.h, PF((x, "gu", xit - 1, NW - 1)))
                                gp.dma_start(
                                    bass.AP(wdb, ilo * DCW, [[ITX * DCW, 128], [1, ihi * DCW]]),
                                    bass.AP(ex["d5"],
                                            ((ex["d5e"] * DC + dc) * 128) * (IT * DCW)
                                            + (ex["ioff"] + ilo) * DCW,
                                            [[IT * DCW, 128], [1, ihi * DCW]]),
                                ).then_inc(dW.h, 16)
                        for t in range(*tgrp[x]):
                            gp.wait_ge(dX.h, b_dX + t * (E + 1) * 16)
                            gp.dma_start(
                                rstg[:, :],
                                bass.AP(ln_hbm, t * 128 * D, [[D, 128], [1, D]]),
                            ).then_inc(dX.h, 16)
                            gp.wait_ge(dX.h, b_dX + t * (E + 1) * 16 + 16)
                            for e in range(E):
                                gp.indirect_dma_start(
                                    bass.AP(cbuf, 0, [[D, E * CAP], [1, D]]),
                                    IndirectOffsetOnAxis(
                                        ap=sslot_i[:, t * E + e: t * E + e + 1], axis=0),
                                    rstg[:, :],
                                    None,
                                    bounds_check=breg,
                                    oob_is_err=False,
                                ).then_inc(dX.h, 16)
                    gp.free_register(breg)
                    dX.add(NT * (E + 1) * 16)
                    dW.add(ndw)

                @blk.tensor
                def _(te):
                    for x in range(NX):
                        xit = experts[x]["it"]
                        for i in range(xit):
                            par = (experts[x]["ioff"] + i) % 2
                            for w, (tclo, tchi) in enumerate(gu_waves):
                                if i > 0:
                                    te.wait_ge(sc.h, SF((x, i - 1, w)))
                                    te.wait_ge(ve.h, VF((x, "h", i - 1, w)))
                                elif x > 0:
                                    te.wait_ge(sc.h, SF((x - 1, experts[x - 1]["it"] - 1, w)))
                                    lc = NCH - 1
                                    lw = len(dn_waves(tg_sizes[lc // DC])) - 1
                                    te.wait_ge(ve.h, VF((x - 1, "x", lc, lw)))
                                if w == 0:
                                    te.wait_ge(dW.h, DWF((x, "gu", i, 1)))
                                last = None
                                for k in range(KT):
                                    for tc in range(tclo, tchi):
                                        te.matmul(ps[tc][:, 0:TCW],
                                                  wgb[par][:, k * 128:(k + 1) * 128],
                                                  bass.AP(ln_T, k * T + tc * TCW, [[NT * D, 128], [1, TCW]]),
                                                  start=(k == 0), stop=(k == KT - 1))
                                    for tc in range(tclo, tchi):
                                        last = te.matmul(ps[TC + tc][:, 0:TCW],
                                                         wub[par][:, k * 128:(k + 1) * 128],
                                                         bass.AP(ln_T, k * T + tc * TCW, [[NT * D, 128], [1, TCW]]),
                                                         start=(k == 0), stop=(k == KT - 1))
                                last.then_inc(pe.h, 1)
                        for dc in range(DC):
                            for tgi in range(NTG):
                                ntt = tg_sizes[tgi]
                                c = dc * NTG + tgi
                                dws = dn_waves(ntt)
                                for wi, (ttlo, tthi) in enumerate(dws):
                                    if c == 0 and wi == 0:
                                        te.wait_ge(sc.h, SF((x, xit - 1, NW - 1)))
                                        te.wait_ge(ve.h, VF((x, "h", xit - 1, NW - 1)))
                                    if c > 0:
                                        pdws = dn_waves(tg_sizes[(c - 1) % NTG])
                                        te.wait_ge(ve.h, VF((x, "x", c - 1, min(wi, len(pdws) - 1))))
                                    if tgi == 0 and wi == 0:
                                        te.wait_ge(dW.h, DWF((x, "dn", dc, 1)))
                                    last = None
                                    xmid = max(1, xit // 2)
                                    merge_mid = (wi == len(dws) - 1) and (xmid == xit)
                                    for i in range(xit):
                                        for tt in range(ttlo, tthi):
                                            tok = (tgi * 8 + tt) * 128
                                            last = te.matmul(
                                                ps[tt][:, 0:DCW],
                                                bass.AP(hb, i * T + tok, [[ITX * T, 128], [1, 128]]),
                                                wdb[:, i * DCW:(i + 1) * DCW],
                                                start=(i == 0), stop=(i == xit - 1))
                                        if (wi == len(dws) - 1 and i == xmid - 1
                                                and tt == tthi - 1 and not merge_mid):
                                            last.then_inc(pe.h, 1)
                                    last.then_inc(pe.h, 2 if merge_mid else 1)
                    pe.add(npe)

                @blk.scalar
                def _(s):
                    preload_set(s, "silu_and_others")
                    for x in range(NX):
                        xit = experts[x]["it"]
                        for i in range(xit):
                            for w, (tclo, tchi) in enumerate(gu_waves):
                                s.wait_ge(pe.h, PF((x, "gu", i, w)))
                                if i > 0:
                                    s.wait_ge(ve.h, VF((x, "h", i - 1, w)))
                                elif x > 0:
                                    s.wait_ge(ve.h, VF((x - 1, "h", experts[x - 1]["it"] - 1, w)))
                                lasts = None
                                for tc in range(tclo, tchi):
                                    lasts = s.activation(sgb[:, tc * TCW:(tc + 1) * TCW],
                                                         ps[tc][:, 0:TCW], AF.Silu)
                                lasts.then_inc(sc.h, 1)
                    sc.add(nsc)

                @blk.vector
                def _(v):
                    for x in range(NX):
                        ex = experts[x]
                        for i in range(ex["it"]):
                            for w, (tclo, tchi) in enumerate(gu_waves):
                                v.wait_ge(sc.h, SF((x, i, w)))
                                lastm = None
                                for tc in range(tclo, tchi):
                                    lastm = v.tensor_tensor(
                                        hb[:, i * T + tc * TCW: i * T + (tc + 1) * TCW],
                                        sgb[:, tc * TCW:(tc + 1) * TCW],
                                        ps[TC + tc][:, 0:TCW], op=ALU.mult)
                                lastm.then_inc(ve.h, 1)
                        for dc in range(DC):
                            for tgi in range(NTG):
                                ntt = tg_sizes[tgi]
                                c = dc * NTG + tgi
                                dws = dn_waves(ntt)
                                for wi, (ttlo, tthi) in enumerate(dws):
                                    v.wait_ge(pe.h, PF((x, "dn", c, wi)))
                                    lastx = None
                                    for tt in range(ttlo, tthi):
                                        ti = tgi * 8 + tt
                                        if ex["kind"] == "exp":
                                            cap = c_all[:, ti * E + ex["e"]: ti * E + ex["e"] + 1]
                                        else:
                                            cap = 0.5
                                        lastx = v.scalar_tensor_tensor(
                                            x_res[:, ti * D + dc * DCW: ti * D + (dc + 1) * DCW],
                                            ps[tt][:, 0:DCW], cap,
                                            x_res[:, ti * D + dc * DCW: ti * D + (dc + 1) * DCW],
                                            op0=ALU.mult, op1=ALU.add)
                                    if c == NCH - 1 and wi == len(dws) - 1:
                                        v.drain().then_inc(ve.h, 1)
                                    else:
                                        lastx.then_inc(ve.h, 1)
                    ve.add(nve)
        assert pe.n == b_pe + npe and sc.n == b_sc + nsc
        assert ve.n == b_ve + nve and dW.n == b_dW + ndw

    def emit_sparse(tag):
        KT = DT
        mid_k = KT // 2
        mid_i = max(1, IT // 2)
        cws = chunk_widths(CAP, 512)
        TCT = len(cws)
        assert TCT <= 2
        coffs = [sum(cws[:j]) for j in range(TCT)]
        gu_waves = [(w, w + 1) for w in range(TCT)]
        NW = TCT
        NSLT = CAP // 128
        assert NSLT <= 8

        def dn_waves(n):
            if n <= 3:
                return [(0, n)]
            if n <= 6:
                return [(0, 3), (3, n)]
            return [(0, 3), (3, 6), (6, n)]

        DWS = dn_waves(NSLT)
        NDW = len(DWS)
        wv = {}
        for wi, (lo, hi) in enumerate(DWS):
            for tt in range(lo, hi):
                wv[tt] = wi
        NX = E
        nbt = (DT + 7) // 8

        b_pe, b_sc, b_ve = pe.base(), sc.base(), ve.base()
        b_dW, b_dS, b_dY = dW.base(), dS.base(), dY.base()
        dx_target = dX.n

        pe_i, sc_i, ve_i, dw_i = {}, {}, {}, {}
        npe = nsc = nve = ndw = 0
        for x in range(NX):
            for i in range(IT):
                for q in range(2):
                    ndw += 16; dw_i[(x, "gu", i, q)] = ndw
            for dc in range(DC):
                for hlf in range(2):
                    ndw += 16; dw_i[(x, "dn", dc, hlf)] = ndw
            for ct in range(NSLT):
                for b in range(nbt):
                    npe += 1; pe_i[(x, "tp", ct, b)] = npe
            for i in range(IT):
                for w in range(NW):
                    npe += 1; pe_i[(x, "gu", i, w)] = npe
            for dc in range(DC):
                for wi in range(NDW):
                    if wi == NDW - 1:
                        npe += 1; pe_i[(x, "dnmid", dc)] = npe
                    npe += 1; pe_i[(x, "dn", dc, wi)] = npe
            for i in range(IT):
                for w in range(NW):
                    nsc += 1; sc_i[(x, i, w)] = nsc
            for ct in range(NSLT):
                for b in range(nbt):
                    nve += 1; ve_i[(x, "cf", ct, b)] = nve
            for i in range(IT):
                for w in range(NW):
                    nve += 1; ve_i[(x, "h", i, w)] = nve
            for dc in range(DC):
                for wi in range(NDW):
                    nve += 1; ve_i[(x, "y", dc, wi)] = nve

        def PF(k):
            return b_pe + pe_i[k]

        def SF(k):
            return b_sc + sc_i[k]

        def VF(k):
            return b_ve + ve_i[k]

        def DWF(k):
            return b_dW + dw_i[k]

        def DSL(x, ct):
            return b_dS + (x * NSLT + ct + 1) * 16

        with ExitStack() as fctx:
            wgb = [sbuf("swg%d_%s" % (p, tag), [128, KT * 128], BF16, fctx) for p in range(2)]
            wub = [sbuf("swu%d_%s" % (p, tag), [128, KT * 128], BF16, fctx) for p in range(2)]
            wdb = sbuf("swd_" + tag, [128, IT * DCW], BF16, fctx)
            sgb = sbuf("ssg_" + tag, [128, CAP], BF16, fctx)
            hb = sbuf("sh_" + tag, [128, IT * CAP], BF16, fctx)
            stage = [sbuf("sst%d_%s" % (p, tag), [128, D], BF16, fctx) for p in range(2)]
            cfT = sbuf("scf_" + tag, [128, DT * CAP], BF16, fctx)
            y_sb = sbuf("sy_" + tag, [128, NSLT * D], BF16, fctx)
            ps = [psum("sp%d_%s" % (i, tag), [128, 512], F32, fctx) for i in range(6)]
            psb = [psum("sb%d_%s" % (i, tag), [128, 1024], BF16, fctx) for i in range(2)]

            with nc.Block() as blk:
                @blk.sync
                def _(s):
                    s.wait_ge(dX.h, dx_target)

                    def emit_loads(x):
                        for ct in range(NSLT):
                            g_ct = x * NSLT + ct
                            if g_ct >= 1:
                                p = g_ct - 1
                                s.wait_ge(pe.h, PF((p // NSLT, "tp", p % NSLT, 0)))
                            s.dma_start(
                                stage[g_ct % 2][:, :],
                                bass.AP(cbuf, (x * CAP + ct * 128) * D, [[D, 128], [1, D]]),
                            ).then_inc(dS.h, 16)

                    def emit_stores(x):
                        for stt in range(NSLT):
                            s.wait_ge(ve.h, VF((x, "y", DC - 1, wv[stt])))
                            s.dma_start(
                                bass.AP(ybuf, (x * CAP + stt * 128) * D, [[D, 128], [1, D]]),
                                y_sb[:, stt * D:(stt + 1) * D]).then_inc(dY.h, 16)

                    emit_loads(0)
                    for x in range(NX):
                        if x + 1 < NX:
                            emit_loads(x + 1)
                        emit_stores(x)
                    s.wait_ge(dS.h, b_dS + NX * NSLT * 16)
                    s.wait_ge(dY.h, b_dY + NX * NSLT * 16)
                dS.add(NX * NSLT * 16)
                dY.add(NX * NSLT * 16)

                @blk.gpsimd
                def _(gp):
                    for x in range(NX):
                        for i in range(IT):
                            par = (x * IT + i) % 2
                            for q in range(2):
                                dst = (wgb if q == 0 else wub)[par]
                                if i >= 1:
                                    gp.wait_ge(pe.h, PF((x, "gu", i - 1, 0)))
                                elif x > 0:
                                    gp.wait_ge(pe.h, PF((x - 1, "dnmid", DC - 1)))
                                wgidx = x * IT + i
                                gp.dma_start(
                                    bass.AP(dst, 0, [[KT * 128, 128], [1, KT * 128]]),
                                    bass.AP(wg5 if q == 0 else wu5,
                                            (wgidx * 128) * (KT * 128),
                                            [[KT * 128, 128], [1, KT * 128]]),
                                ).then_inc(dW.h, 16)
                        for dc in range(DC):
                            for hlf in range(2):
                                ilo = hlf * mid_i
                                ihi = mid_i if hlf == 0 else IT - mid_i
                                if dc > 0:
                                    gp.wait_ge(pe.h, PF((x, "dnmid", dc - 1)) if hlf == 0
                                               else PF((x, "dn", dc - 1, NDW - 1)))
                                else:
                                    gp.wait_ge(pe.h, PF((x, "gu", IT - 1, NW - 1)))
                                gp.dma_start(
                                    bass.AP(wdb, ilo * DCW, [[IT * DCW, 128], [1, ihi * DCW]]),
                                    bass.AP(wed5,
                                            ((x * DC + dc) * 128) * (IT * DCW) + ilo * DCW,
                                            [[IT * DCW, 128], [1, ihi * DCW]]),
                                ).then_inc(dW.h, 16)
                    dW.add(ndw)

                @blk.tensor
                def _(te):
                    for x in range(NX):
                        for ct in range(NSLT):
                            g_ct = x * NSLT + ct
                            te.wait_ge(dS.h, DSL(x, ct))
                            for b in range(nbt):
                                lo, hi = b * 8, min(DT, b * 8 + 8)
                                gb = g_ct * nbt + b
                                if gb >= 2:
                                    pgb = gb - 2
                                    pxx, rem = divmod(pgb, NSLT * nbt)
                                    pct, pb = divmod(rem, nbt)
                                    te.wait_ge(ve.h, VF((pxx, "cf", pct, pb)))
                                last = None
                                for d in range(lo, hi):
                                    last = te.transpose(
                                        psb[gb % 2][:, (d - lo) * 128:(d - lo + 1) * 128],
                                        stage[g_ct % 2][:, d * 128:(d + 1) * 128],
                                        id_sb[:, :])
                                last.then_inc(pe.h, 1)
                        for i in range(IT):
                            par = (x * IT + i) % 2
                            for w, (tclo, tchi) in enumerate(gu_waves):
                                if i > 0:
                                    te.wait_ge(sc.h, SF((x, i - 1, w)))
                                    te.wait_ge(ve.h, VF((x, "h", i - 1, w)))
                                elif x > 0 and w == 0:
                                    te.wait_ge(ve.h, VF((x - 1, "y", DC - 1, NDW - 1)))
                                if i == 0 and w == 0:
                                    te.wait_ge(ve.h, VF((x, "cf", NSLT - 1, nbt - 1)))
                                if w == 0:
                                    te.wait_ge(dW.h, DWF((x, "gu", i, 1)))
                                last = None
                                for k in range(KT):
                                    for tc in range(tclo, tchi):
                                        te.matmul(ps[tc][:, 0:cws[tc]],
                                                  wgb[par][:, k * 128:(k + 1) * 128],
                                                  bass.AP(cfT, k * CAP + coffs[tc],
                                                          [[DT * CAP, 128], [1, cws[tc]]]),
                                                  start=(k == 0), stop=(k == KT - 1))
                                    for tc in range(tclo, tchi):
                                        last = te.matmul(ps[2 + tc][:, 0:cws[tc]],
                                                         wub[par][:, k * 128:(k + 1) * 128],
                                                         bass.AP(cfT, k * CAP + coffs[tc],
                                                                 [[DT * CAP, 128], [1, cws[tc]]]),
                                                         start=(k == 0), stop=(k == KT - 1))
                                last.then_inc(pe.h, 1)
                        for dc in range(DC):
                            for wi, (ttlo, tthi) in enumerate(DWS):
                                if dc == 0 and wi == 0:
                                    te.wait_ge(sc.h, SF((x, IT - 1, NW - 1)))
                                    te.wait_ge(ve.h, VF((x, "h", IT - 1, NW - 1)))
                                if dc > 0:
                                    if wi == 0:
                                        te.wait_ge(ve.h, VF((x, "y", dc - 1, NDW - 1)))
                                    elif wi == 1:
                                        te.wait_ge(ve.h, VF((x, "y", dc - 1, min(1, NDW - 1))))
                                if wi >= 2:
                                    te.wait_ge(ve.h, VF((x, "y", dc, wi - 2)))
                                if wi == 0:
                                    te.wait_ge(dW.h, DWF((x, "dn", dc, 1)))
                                last = None
                                merge_mid = (wi == NDW - 1) and (mid_i == IT)
                                for i in range(IT):
                                    for tt in range(ttlo, tthi):
                                        last = te.matmul(
                                            ps[tt % 6][:, 0:DCW],
                                            bass.AP(hb, i * CAP + tt * 128,
                                                    [[IT * CAP, 128], [1, 128]]),
                                            wdb[:, i * DCW:(i + 1) * DCW],
                                            start=(i == 0), stop=(i == IT - 1))
                                    if wi == NDW - 1 and i == mid_i - 1 and not merge_mid:
                                        last.then_inc(pe.h, 1)
                                last.then_inc(pe.h, 2 if merge_mid else 1)
                    pe.add(npe)

                @blk.scalar
                def _(s):
                    for x in range(NX):
                        for i in range(IT):
                            for w, (tclo, tchi) in enumerate(gu_waves):
                                s.wait_ge(pe.h, PF((x, "gu", i, w)))
                                if i > 0:
                                    s.wait_ge(ve.h, VF((x, "h", i - 1, w)))
                                elif x > 0:
                                    s.wait_ge(ve.h, VF((x - 1, "h", IT - 1, w)))
                                lasts = None
                                for tc in range(tclo, tchi):
                                    lasts = s.activation(sgb[:, coffs[tc]:coffs[tc] + cws[tc]],
                                                         ps[tc][:, 0:cws[tc]], AF.Silu)
                                lasts.then_inc(sc.h, 1)
                    sc.add(nsc)

                @blk.vector
                def _(v):
                    for x in range(NX):
                        for ct in range(NSLT):
                            g_ct = x * NSLT + ct
                            for b in range(nbt):
                                lo, hi = b * 8, min(DT, b * 8 + 8)
                                gb = g_ct * nbt + b
                                v.wait_ge(pe.h, PF((x, "tp", ct, b)))
                                v.tensor_copy(
                                    bass.AP(cfT, lo * CAP + ct * 128,
                                            [[DT * CAP, 128], [CAP, hi - lo], [1, 128]]),
                                    psb[gb % 2][:, 0:(hi - lo) * 128]).then_inc(ve.h, 1)
                        for i in range(IT):
                            for w, (tclo, tchi) in enumerate(gu_waves):
                                v.wait_ge(sc.h, SF((x, i, w)))
                                lastm = None
                                for tc in range(tclo, tchi):
                                    lastm = v.tensor_tensor(
                                        hb[:, i * CAP + coffs[tc]: i * CAP + coffs[tc] + cws[tc]],
                                        sgb[:, coffs[tc]:coffs[tc] + cws[tc]],
                                        ps[2 + tc][:, 0:cws[tc]], op=ALU.mult)
                                lastm.then_inc(ve.h, 1)
                        for dc in range(DC):
                            for wi, (ttlo, tthi) in enumerate(DWS):
                                v.wait_ge(pe.h, PF((x, "dn", dc, wi)))
                                if x > 0 and dc == 0 and wi == 0:
                                    v.wait_ge(dY.h, b_dY + x * NSLT * 16)
                                lasty = None
                                for tt in range(ttlo, tthi):
                                    lasty = v.tensor_copy(
                                        y_sb[:, tt * D + dc * DCW: tt * D + (dc + 1) * DCW],
                                        ps[tt % 6][:, 0:DCW])
                                lasty.then_inc(ve.h, 1)
                    ve.add(nve)
        assert pe.n == b_pe + npe and sc.n == b_sc + nsc and ve.n == b_ve + nve
        assert dW.n == b_dW + ndw

    def emit_recombine(tag):
        b_ve, b_dX = ve.base(), dX.base()
        with ExitStack() as rctx:
            r1 = [sbuf("r1%d_%s" % (p, tag), [128, D], BF16, rctx) for p in range(2)]
            r2 = [sbuf("r2%d_%s" % (p, tag), [128, D], BF16, rctx) for p in range(2)]
            with nc.Block() as blk:
                @blk.gpsimd
                def _(gp):
                    for t in range(NT):
                        if t >= 1:
                            gp.wait_ge(dX.h, b_dX + 32 * t)
                        if t >= 2:
                            gp.wait_ge(ve.h, b_ve + (t - 2) + 1)
                        gp.indirect_dma_start(
                            r1[t % 2][:, :], None,
                            ybuf[:, :],
                            IndirectOffsetOnAxis(ap=ptr1_i[:, t:t + 1], axis=0),
                        ).then_inc(dX.h, 16)
                        gp.indirect_dma_start(
                            r2[t % 2][:, :], None,
                            ybuf[:, :],
                            IndirectOffsetOnAxis(ap=ptr2_i[:, t:t + 1], axis=0),
                        ).then_inc(dX.h, 16)
                    dX.add(32 * NT)

                @blk.vector
                def _(v):
                    for t in range(NT):
                        v.wait_ge(dX.h, b_dX + 32 * (t + 1))
                        v.scalar_tensor_tensor(
                            x_res[:, t * D:(t + 1) * D], r1[t % 2][:, :],
                            c1b[:, t:t + 1], x_res[:, t * D:(t + 1) * D],
                            op0=ALU.mult, op1=ALU.add)
                        v.drain()
                        v.scalar_tensor_tensor(
                            x_res[:, t * D:(t + 1) * D], r2[t % 2][:, :],
                            c2b[:, t:t + 1], x_res[:, t * D:(t + 1) * D],
                            op0=ALU.mult, op1=ALU.add)
                        v.drain().then_inc(ve.h, 1)
                    ve.add(NT)

    def emit_qkv(ln_T, tag):
        b_pe, b_ve, b_dW, b_dS = pe.base(), ve.base(), dW.base(), dS.base()
        nvb = (NT + 7) // 8
        PE_H = 3 + nvb
        VE_H = 3 + 1
        W = [wq, wk, wv]
        BIAS = [bqs_sb, bk_sb, bv_sb]
        SCALE = [RSQ, 1.0, 1.0]

        def pe_j(h, j):
            return b_pe + h * PE_H + j + 1

        def pe_vt(h, b):
            return b_pe + h * PE_H + 3 + b + 1

        def ve_j(h, j):
            return b_ve + VE_H * h + j + 1

        def ve_vt(h):
            return b_ve + VE_H * h + 4

        def dw_j(h, j):
            return b_dW + (3 * h + j + 1) * 16

        with ExitStack() as qctx:
            wjb = [sbuf("wj%d_%s" % (j, tag), [128, 2 * DT * 128], BF16, qctx) for j in range(3)]
            js_sb = [sbuf("js%d_%s" % (j, tag), [128, T], BF16, qctx) for j in range(3)]
            vts = sbuf("vts_" + tag, [128, NT * 128], BF16, qctx)
            psq = [psum("qp%d_%s" % (i, tag), [128, 512], F32, qctx) for i in range(4)]
            psb = [psum("qb%d_%s" % (i, tag), [128, 1024], BF16, qctx) for i in range(2)]

            with nc.Block() as blk:
                @blk.gpsimd
                def _(gp):
                    for h in range(H):
                        for j in range(3):
                            buf = wjb[j]
                            par = h % 2
                            if h >= 1:
                                gp.wait_ge(pe.h, pe_j(h - 1, 0))
                            gp.dma_start(
                                bass.AP(buf, par * DT * 128, [[2 * DT * 128, 128], [1, DT * 128]]),
                                bass.AP(W[j], h * 128 * D, [[D, 128], [1, D]]),
                            ).then_inc(dW.h, 16)
                    dW.add(3 * H * 16)

                @blk.tensor
                def _(te):
                    for h in range(H):
                        te.wait_ge(dW.h, b_dW + 3 * (h + 1) * 16)
                        for j in range(3):
                            par = h % 2
                            if j > 0:
                                te.wait_ge(ve.h, ve_j(h, j - 1))
                            elif h > 0:
                                te.wait_ge(ve.h, ve_j(h - 1, 2))
                            last = None
                            for k in range(DT):
                                for tc in range(TC):
                                    last = te.matmul(
                                        psq[tc][:, 0:TCW],
                                        wjb[j][:, (par * DT + k) * 128:(par * DT + k + 1) * 128],
                                        bass.AP(ln_T, k * T + tc * TCW, [[NT * D, 128], [1, TCW]]),
                                        start=(k == 0), stop=(k == DT - 1))
                            last.then_inc(pe.h, 1)
                        te.wait_ge(ve.h, ve_j(h, 2))
                        if h > 0:
                            te.wait_ge(ve.h, ve_vt(h - 1))
                        for b in range(nvb):
                            lo, hi = b * 8, min(NT, b * 8 + 8)
                            last = None
                            for tti in range(lo, hi):
                                last = te.transpose(psb[b % 2][:, (tti - lo) * 128:(tti - lo + 1) * 128],
                                                    js_sb[2][:, tti * 128:(tti + 1) * 128],
                                                    id_sb[:, :])
                            last.then_inc(pe.h, 1)
                    pe.add(H * PE_H)

                @blk.vector
                def _(v):
                    for h in range(H):
                        if h > 0:
                            v.wait_ge(dS.h, b_dS + 3 * h * 16)
                        for j in range(3):
                            v.wait_ge(pe.h, pe_j(h, j))
                            lastq = None
                            for tc in range(TC):
                                if SCALE[j] == 1.0:
                                    lastq = v.tensor_scalar_add(js_sb[j][:, tc * TCW:(tc + 1) * TCW],
                                                                psq[tc][:, 0:TCW], BIAS[j][:, h:h + 1])
                                else:
                                    lastq = v.tensor_scalar(js_sb[j][:, tc * TCW:(tc + 1) * TCW],
                                                            psq[tc][:, 0:TCW], SCALE[j],
                                                            BIAS[j][:, h:h + 1],
                                                            op0=ALU.mult, op1=ALU.add)
                            lastq.then_inc(ve.h, 1)
                        lastv = None
                        for b in range(nvb):
                            lo, hi = b * 8, min(NT, b * 8 + 8)
                            v.wait_ge(pe.h, pe_vt(h, b))
                            lastv = v.tensor_copy(vts[:, lo * 128:lo * 128 + (hi - lo) * 128],
                                                  psb[b % 2][:, 0:(hi - lo) * 128])
                        lastv.then_inc(ve.h, 1)
                    ve.add(H * VE_H)

                @blk.sync
                def _(s):
                    for h in range(H):
                        for j in range(3):
                            s.wait_ge(ve.h, ve_j(h, j))
                            if j == 0:
                                s.dma_start(bass.AP(q_hbm, h * 128 * T, [[T, 128], [1, T]]),
                                            js_sb[0][:, :]).then_inc(dS.h, 16)
                            elif j == 1:
                                s.dma_start(bass.AP(kv_send[h // HG], (h % HG) * 128 * T,
                                                    [[T, 128], [1, T]]),
                                            js_sb[1][:, :]).then_inc(dS.h, 16)
                            else:
                                s.wait_ge(ve.h, ve_vt(h))
                                s.dma_start(
                                    bass.AP(kv_send[h // HG], KSZ + (h % HG) * T * 128,
                                            [[128, 128], [128 * 128, NT], [1, 128]]),
                                    bass.AP(vts, 0, [[NT * 128, 128], [128, NT], [1, 128]]),
                                ).then_inc(dS.h, 16)
                    s.wait_ge(dS.h, dS.n + 3 * H * 16)
                    dS.add(3 * H * 16)
        assert pe.n == b_pe + H * PE_H and ve.n == b_ve + H * VE_H

        with nc.Block() as blk:
            @blk.gpsimd
            def _(gp):
                for i in range(NCC):
                    gp.collective_compute(
                        "AllGather", ALU.bypass,
                        ins=[kv_send[i].ap().opt()],
                        outs=[kv_gath[i].ap().opt()],
                        replica_groups=g.get("RG") or [[0, 1], [2, 3], [4, 5], [6, 7]],
                    ).then_inc(ccS.h)
                    ccS.add(1)
                    if i < NCC - 1:
                        gp.wait_ge(ccS.h, ccS.n)

    def emit_attn(tag):
        SEQ_TOT = 2 * T
        assert NT % 2 == 0
        b_pe, b_sc, b_ve, b_dS = pe.base(), sc.base(), ve.base(), dS.base()
        cc_base = ccS.n - NCC
        pos = []
        for i in range(NT):
            ext = 256 * (i + 1)
            cw = chunk_widths(ext, 512)
            ntr = ext // 128
            nb = (ntr + 7) // 8
            pos.append((ext, cw, ntr, nb))

        def prev3(i):
            return NT - 1 - ((NT - 1 - i) % 3)

        pe_i, sc_i, ve_i = {}, {}, {}
        npe = nsc = nve = 0
        for h in range(H):
            for r in range(NT + 2):
                if r < NT:
                    for ci in range(len(pos[r][1])):
                        npe += 1; pe_i[(h, "s", r, ci)] = npe
                j = r - 2
                if 0 <= j < NT:
                    for b in range(pos[j][3]):
                        npe += 1; pe_i[(h, "t", j, b)] = npe
                    npe += 1; pe_i[(h, "av", j)] = npe
            for i in range(NT):
                for ci in range(len(pos[i][1])):
                    nsc += 1; sc_i[(h, "ev", i, ci)] = nsc
                if i >= 2:
                    nsc += 1; sc_i[(h, "z", i - 2)] = nsc
                nsc += 1; sc_i[(h, "exp", i)] = nsc
            nsc += 1; sc_i[(h, "z", NT - 2)] = nsc
            nsc += 1; sc_i[(h, "z", NT - 1)] = nsc
            for i in range(NT):
                if i >= 2:
                    for b in range(pos[i - 2][3]):
                        nve += 1; ve_i[(h, "te", i - 2, b)] = nve
                nve += 1; ve_i[(h, "nm", i)] = nve
                nve += 1; ve_i[(h, "sca", i)] = nve
            for j in (NT - 2, NT - 1):
                for b in range(pos[j][3]):
                    nve += 1; ve_i[(h, "te", j, b)] = nve

        def P(k):
            return b_pe + pe_i[k]

        def S(k):
            return b_sc + sc_i[k]

        def V(k):
            return b_ve + ve_i[k]

        def DSH(h):
            return b_dS + (h + 1) * 6 * 16

        with ExitStack() as actx:
            k_sb = sbuf("k_" + tag, [128, SEQ_TOT], BF16, actx)
            vts = sbuf("vt_" + tag, [128, SEQ_TOT], BF16, actx)
            q_sb = sbuf("q_" + tag, [128, T], BF16, actx)
            rows = [sbuf("rows%d_%s" % (i, tag), [128, SEQ_TOT], F32, actx) for i in range(2)]
            att = [sbuf("att%d_%s" % (i, tag), [128, SEQ_TOT], BF16, actx) for i in range(3)]
            attT = sbuf("attT_" + tag, [128, SEQ_TOT], BF16, actx)
            zall = sbuf("zall_" + tag, [128, T], BF16, actx)
            pss = [psum("ap%d_%s" % (i, tag), [128, 512], F32, actx) for i in range(4)]
            psz = psum("az_" + tag, [128, 512], F32, actx)
            psb = [psum("ab%d_%s" % (i, tag), [128, 1024], BF16, actx) for i in range(2)]
            NMC = [7, 8]
            DENC = [10, 12]
            RIC = [11, 13]

            def stp(cols, i):
                c = cols[i % 2]
                return st[:, c:c + 1]

            def emit_te_evac(v, h, j):
                for b in range(pos[j][3]):
                    lo, hi = b * 8, min(pos[j][2], b * 8 + 8)
                    v.wait_ge(pe.h, P((h, "t", j, b)))
                    v.tensor_copy(attT[:, lo * 128:lo * 128 + (hi - lo) * 128],
                                  psb[b % 2][:, 0:(hi - lo) * 128]).then_inc(ve.h, 1)

            with nc.Block() as blk:
                @blk.sync
                def _(s):
                    for h in range(H):
                        if h == 0:
                            s.wait_ge(ccS.h, cc_base + 1)
                        elif h == HG:
                            s.wait_ge(ccS.h, cc_base + NCC)
                        if h > 0:
                            s.wait_ge(sc.h, S((h - 1, "z", NT - 1)))
                        gb = kv_gath[h // HG]
                        hh = h % HG
                        for r in range(2):
                            s.dma_start(
                                bass.AP(k_sb, r * 128, [[SEQ_TOT, 128], [256, NT], [1, 128]]),
                                bass.AP(gb, r * 2 * KSZ + hh * 128 * T,
                                        [[T, 128], [128, NT], [1, 128]]),
                            ).then_inc(dS.h, 16)
                            s.dma_start(
                                bass.AP(vts, r * 128, [[SEQ_TOT, 128], [256, NT], [1, 128]]),
                                bass.AP(gb, r * 2 * KSZ + KSZ + hh * T * 128,
                                        [[128, 128], [128 * 128, NT], [1, 128]]),
                            ).then_inc(dS.h, 16)
                        s.dma_start(q_sb[:, :],
                                    bass.AP(q_hbm, h * 128 * T, [[T, 128], [1, T]])
                                    ).then_inc(dS.h, 16)
                        s.wait_ge(sc.h, S((h, "z", NT - 1)))
                        s.dma_start(bass.AP(z_hbm, h * 128 * T, [[T, 128], [1, T]]),
                                    zall[:, :]).then_inc(dS.h, 16)
                        s.wait_ge(dS.h, DSH(h))
                    dS.add(H * 6 * 16)

                @blk.tensor
                def _(te):
                    for h in range(H):
                        te.wait_ge(dS.h, DSH(h) - 16)
                        for r in range(NT + 2):
                            if r < NT:
                                ext, cw, ntr, nb = pos[r]
                                base_bank = (r % 2) * 2
                                off = 0
                                for ci, w in enumerate(cw):
                                    if ci >= 2:
                                        te.wait_ge(sc.h, S((h, "ev", r, ci - 2)))
                                    elif r >= 2:
                                        te.wait_ge(sc.h, S((h, "ev", r - 2, len(pos[r - 2][1]) - 1)))
                                    elif h > 0:
                                        te.wait_ge(sc.h, S((h - 1, "ev", NT - 2 + r, len(pos[NT - 2 + r][1]) - 1)))
                                    te.matmul(pss[base_bank + ci % 2][:, 0:w],
                                              q_sb[:, r * 128:(r + 1) * 128],
                                              k_sb[:, off:off + w],
                                              start=True, stop=True).then_inc(pe.h, 1)
                                    off += w
                            j = r - 2
                            if 0 <= j < NT:
                                ext, cw, ntr, nb = pos[j]
                                te.wait_ge(ve.h, V((h, "sca", j)))
                                for b in range(nb):
                                    lo, hi = b * 8, min(ntr, b * 8 + 8)
                                    if b >= 2:
                                        te.wait_ge(ve.h, V((h, "te", j, b - 2)))
                                    elif j > 0:
                                        te.wait_ge(ve.h, V((h, "te", j - 1, pos[j - 1][3] - 1)))
                                    elif h > 0:
                                        te.wait_ge(ve.h, V((h - 1, "te", NT - 1, pos[NT - 1][3] - 1)))
                                    last = None
                                    for kt in range(lo, hi):
                                        last = te.transpose(
                                            psb[b % 2][:, (kt - lo) * 128:(kt - lo + 1) * 128],
                                            att[j % 3][:, kt * 128:(kt + 1) * 128], id_sb[:, :])
                                    last.then_inc(pe.h, 1)
                                te.wait_ge(ve.h, V((h, "te", j, nb - 1)))
                                if j > 0:
                                    te.wait_ge(sc.h, S((h, "z", j - 1)))
                                elif h > 0:
                                    te.wait_ge(sc.h, S((h - 1, "z", NT - 1)))
                                last = None
                                for kt in range(ntr):
                                    last = te.matmul(psz[:, 0:128],
                                                     vts[:, kt * 128:(kt + 1) * 128],
                                                     attT[:, kt * 128:(kt + 1) * 128],
                                                     start=(kt == 0), stop=(kt == ntr - 1))
                                last.then_inc(pe.h, 1)
                    pe.add(npe)

                @blk.scalar
                def _(s):
                    preload_set(s, "natural_log_exp_and_others")
                    for h in range(H):
                        for i in range(NT):
                            ext, cw, ntr, nb = pos[i]
                            base_bank = (i % 2) * 2
                            off = 0
                            for ci, w in enumerate(cw):
                                s.wait_ge(pe.h, P((h, "s", i, ci)))
                                if ci == 0 and i >= 2:
                                    s.wait_ge(ve.h, V((h, "nm", i - 2)))
                                s.activation(rows[i % 2][:, off:off + w],
                                             pss[base_bank + ci % 2][:, 0:w],
                                             AF.Copy).then_inc(sc.h, 1)
                                off += w
                            if i >= 2:
                                s.wait_ge(pe.h, P((h, "av", i - 2)))
                                s.activation(zall[:, (i - 2) * 128:(i - 1) * 128],
                                             psz[:, 0:128], AF.Copy).then_inc(sc.h, 1)
                            elif i == 0 and h > 0:
                                s.wait_ge(dS.h, DSH(h - 1))
                            s.wait_ge(ve.h, V((h, "nm", i)))
                            if i >= 3:
                                s.wait_ge(pe.h, P((h, "t", i - 3, pos[i - 3][3] - 1)))
                            elif h > 0:
                                pj = prev3(i)
                                s.wait_ge(pe.h, P((h - 1, "t", pj, pos[pj][3] - 1)))
                            s.drain()
                            s.activation(att[i % 3][:, 0:ext], rows[i % 2][:, 0:ext], AF.Exp,
                                         bias=stp(NMC, i),
                                         accum_out=stp(DENC, i)).then_inc(sc.h, 1)
                        for j in (NT - 2, NT - 1):
                            s.wait_ge(pe.h, P((h, "av", j)))
                            s.activation(zall[:, j * 128:(j + 1) * 128],
                                         psz[:, 0:128], AF.Copy).then_inc(sc.h, 1)
                    sc.add(nsc)

                @blk.vector
                def _(v):
                    for h in range(H):
                        for i in range(NT):
                            ext, cw, ntr, nb = pos[i]
                            if i >= 2:
                                emit_te_evac(v, h, i - 2)
                            v.wait_ge(sc.h, S((h, "ev", i, len(cw) - 1)))
                            v.tensor_add(rows[i % 2][:, ext - 256:ext],
                                         rows[i % 2][:, ext - 256:ext], mask_sb[:, :])
                            v.drain()
                            v.reduce_max(stp(NMC, i), rows[i % 2][:, 0:ext], axis=AX.X,
                                         negate=True).then_inc(ve.h, 1)
                            v.wait_ge(sc.h, S((h, "exp", i)))
                            v.reciprocal(stp(RIC, i), stp(DENC, i))
                            v.drain()
                            v.tensor_scalar_mul(att[i % 3][:, 0:ext], att[i % 3][:, 0:ext],
                                                stp(RIC, i)).then_inc(ve.h, 1)
                        for j in (NT - 2, NT - 1):
                            emit_te_evac(v, h, j)
                    ve.add(nve)
        assert pe.n == b_pe + npe and sc.n == b_sc + nsc and ve.n == b_ve + nve

    def emit_oproj(tag):
        b_pe, b_sc, b_ve, b_dS, b_dW = pe.base(), sc.base(), ve.base(), dS.base(), dW.base()
        NG8 = (NT + 7) // 8
        with ExitStack() as octx:
            wob = sbuf("wo_" + tag, [128, H * D], BF16, octx)
            zt8 = sbuf("zt_" + tag, [128, H * 8 * 128], BF16, octx)
            bob = sbuf("bo_" + tag, [128, D], F32, octx)
            tmp = [sbuf("otmp%d_%s" % (i, tag), [128, 512], F32, octx) for i in range(2)]
            pso = [psum("op%d_%s" % (i, tag), [128, 512], F32, octx) for i in range(min(4, NT * DC))]
            nps = len(pso)

            with nc.Block() as blk:
                @blk.gpsimd
                def _(gp):
                    gp.dma_start(
                        bass.AP(wob, 0, [[H * D, 128], [D, H], [1, D]]),
                        bass.AP(wo, 0, [[D, 128], [128 * D, H], [1, D]]),
                    ).then_inc(dW.h, 16)
                    dW.add(16)

                @blk.sync
                def _(s):
                    s.dma_start(bob[:, :], bass.AP(bo, 0, [[0, 128], [1, D]])).then_inc(dS.h, 16)
                    s.wait_ge(dS.h, b_dS + 16)
                    for gp8 in range(NG8):
                        ntt = min(8, NT - gp8 * 8)
                        if gp8 > 0:
                            s.wait_ge(pe.h, b_pe + gp8 * 8 * DC)
                        s.dma_start(
                            bass.AP(zt8, 0, [[H * 8 * 128, 128], [8 * 128, H], [1, ntt * 128]]),
                            bass.AP(z_hbm, gp8 * 8 * 128, [[T, 128], [128 * T, H], [1, ntt * 128]]),
                        ).then_inc(dS.h, 16)
                        s.wait_ge(dS.h, b_dS + (gp8 + 2) * 16)
                    dS.add((NG8 + 1) * 16)

                @blk.tensor
                def _(te):
                    te.wait_ge(dW.h, b_dW + 16)
                    for gp8 in range(NG8):
                        ntt = min(8, NT - gp8 * 8)
                        te.wait_ge(dS.h, b_dS + (gp8 + 2) * 16)
                        for tt in range(ntt):
                            t = gp8 * 8 + tt
                            for dc in range(DC):
                                if t * DC + dc >= nps:
                                    te.wait_ge(ve.h, b_ve + t * DC + dc - nps + 1)
                                last = None
                                for h in range(H):
                                    last = te.matmul(pso[(t * DC + dc) % nps][:, 0:DCW],
                                                     zt8[:, h * 8 * 128 + tt * 128: h * 8 * 128 + (tt + 1) * 128],
                                                     wob[:, h * D + dc * DCW: h * D + (dc + 1) * DCW],
                                                     start=(h == 0), stop=(h == H - 1))
                                last.then_inc(pe.h, 1)
                    pe.add(NT * DC)

                @blk.vector
                def _(v):
                    v.wait_ge(dS.h, b_dS + 16)
                    for t in range(NT):
                        for dc in range(DC):
                            gi = t * DC + dc
                            v.wait_ge(pe.h, b_pe + gi + 1)
                            v.tensor_add(tmp[gi % 2][:, 0:DCW], pso[gi % nps][:, 0:DCW],
                                         bob[:, dc * DCW:(dc + 1) * DCW])
                            v.drain()
                            v.tensor_add(x_res[:, t * D + dc * DCW: t * D + (dc + 1) * DCW],
                                         x_res[:, t * D + dc * DCW: t * D + (dc + 1) * DCW],
                                         tmp[gi % 2][:, 0:DCW]).then_inc(ve.h, 1)
                    ve.add(NT * DC)

    def emit_out():
        b_sc, b_dS = sc.base(), dS.base()
        with ExitStack() as octx:
            of = [sbuf("of%d" % i, [128, D], F32, octx) for i in range(2)]
            with nc.Block() as blk:
                @blk.scalar
                def _(s):
                    for t in range(NT):
                        if t >= 2:
                            s.wait_ge(dS.h, b_dS + (t - 1) * 16)
                        s.activation(of[t % 2][:, :], x_res[:, t * D:(t + 1) * D],
                                     AF.Copy).then_inc(sc.h, 1)
                    sc.add(NT)

                @blk.sync
                def _(s):
                    for t in range(NT):
                        s.wait_ge(sc.h, b_sc + t + 1)
                        s.dma_start(bass.AP(y_out, t * 128 * D, [[D, 128], [1, D]]),
                                    of[t % 2][:, :]).then_inc(dS.h, 16)
                        s.wait_ge(dS.h, b_dS + (t + 1) * 16)
                    dS.add(NT * 16)

    def emit_dump(dst):
        b_sc, b_dS = sc.base(), dS.base()
        with ExitStack() as octx:
            of = [sbuf("dof%d_%s" % (i, dst.name), [128, D], F32, octx) for i in range(2)]
            with nc.Block() as blk:
                @blk.scalar
                def _(s):
                    for t in range(NT):
                        if t >= 2:
                            s.wait_ge(dS.h, b_dS + (t - 1) * 16)
                        s.activation(of[t % 2][:, :], x_res[:, t * D:(t + 1) * D],
                                     AF.Copy).then_inc(sc.h, 1)
                    sc.add(NT)

                @blk.sync
                def _(s):
                    for t in range(NT):
                        s.wait_ge(sc.h, b_sc + t + 1)
                        s.dma_start(bass.AP(dst, t * 128 * D, [[D, 128], [1, D]]),
                                    of[t % 2][:, :]).then_inc(dS.h, 16)
                        s.wait_ge(dS.h, b_dS + (t + 1) * 16)
                    dS.add(NT * 16)

    STOP = g.get("STOP")
    with ExitStack() as inv1:
        ln_T = sbuf("lnT1", [128, NT * D], BF16, inv1)
        emit_ln(ln_T, "l1", store_rows=True)
        if DBG:
            emit_sbuf_dump(dbg_lnT, ln_T, NT * D)
            emit_sbuf_dump(dbg_st, st, 32)
        if STOP == "ln1":
            return nc
        emit_routing(ln_T, "r1")
        if g.get("DBGSL"):
            emit_sbuf_dump(dbg_sl, sslot_i, NT * E)
            emit_sbuf_dump(dbg_p1, ptr1_i, NT)
            emit_sbuf_dump(dbg_p2, ptr2_i, NT)
            emit_sbuf_dump(dbg_cc, c_all, NT * E)
        if DBG:
            emit_sbuf_dump(dbg_c, c_all, NT * E)
        if STOP == "route1":
            return nc
        emit_ffn(ln_T, "f1")
        if DBG:
            emit_sbuf_dump(dbg_h, hb_last[0], IT * T)
    emit_sparse("s1")
    emit_recombine("g1")
    if DBG >= 1:
        emit_dump(dbg_outs[0])
    if STOP == "ffn1":
        return nc
    with ExitStack() as att:
        ln_T = sbuf("lnT2", [128, NT * D], BF16, att)
        emit_ln(ln_T, "l2")
        emit_qkv(ln_T, "qk")
    emit_attn("at")
    emit_oproj("op")
    if DBG >= 2:
        emit_dump(dbg_outs[1])
    with ExitStack() as inv2:
        ln_T = sbuf("lnT3", [128, NT * D], BF16, inv2)
        emit_ln(ln_T, "l3", store_rows=True)
        emit_routing(ln_T, "r3")
        emit_ffn(ln_T, "f3")
    emit_sparse("s3")
    emit_recombine("g3")
    emit_out()
    return nc


def make_inputs(core, full, g):
    D, T, I, E, H = g["D"], g["T"], g["I"], g["E"], g["H"]
    NT = T // 128
    b, r = core // 2, core % 2
    xb = np.ascontiguousarray(
        full["L0_attn_x"][b].reshape(2 * NT, 128, D)[r::2])
    mask = np.where(np.arange(256)[None, :] <= 128 * r + np.arange(128)[:, None],
                    np.float32(0), np.float32(-1e9 if False else -100000.0)).astype(np.float32)
    E_, I_, H_ = g["E"], g["I"], g["H"]
    DT_, IT_ = D // 128, I_ // 128

    def gu5(expert_mat, shared_mat):
        egT = expert_mat.transpose(0, 2, 1)
        e5 = egT.reshape(E_, DT_, 128, IT_, 128).transpose(0, 3, 4, 1, 2)
        e5 = egT.reshape(E_, DT_, 128, IT_, 128).transpose(0, 3, 2, 1, 4)
        e5 = e5.reshape(E_, IT_, 128, DT_ * 128)
        shT = shared_mat.T
        s5 = shT.reshape(DT_, 128, 2, IT_, 128).transpose(2, 3, 1, 0, 4)
        s5 = s5.reshape(2, IT_, 128, DT_ * 128)
        return np.ascontiguousarray(np.concatenate([e5, s5], axis=0))

    def d5(wT):
        G = wT.shape[0]
        DC_ = max(1, D // 512)
        DCW_ = min(D, 512)
        w = wT.reshape(G, IT_, 128, DC_, DCW_).transpose(0, 3, 2, 1, 4)
        return np.ascontiguousarray(w.reshape(G, DC_, 128, IT_ * DCW_))

    def qkv4(w):
        return np.ascontiguousarray(
            w.reshape(H_, DT_, 128, 128).transpose(0, 2, 1, 3).reshape(H_, 128, D))

    im = dict(
        x_in=xb,
        wq=qkv4(full["L0_attn_W_Q"]), wk=qkv4(full["L0_attn_W_K"]),
        wv=qkv4(full["L0_attn_W_V"]),
        wo=full["L0_attn_W_O"],
        bqT=np.ascontiguousarray(full["L0_attn_b_Q"].T),
        bkT=np.ascontiguousarray(full["L0_attn_b_K"].T),
        bvT=np.ascontiguousarray(full["L0_attn_b_V"].T),
        bo=full["L0_attn_b_O"],
        gate_wT=np.ascontiguousarray(full["L0_ffn_gate_w"].T),
        wg5=gu5(full["L0_ffn_expert_gate"], full["L0_ffn_shared_gate"]),
        wu5=gu5(full["L0_ffn_expert_up"], full["L0_ffn_shared_up"]),
        wed5=d5(full["L0_ffn_expert_down"].transpose(0, 2, 1).reshape(E_, I_, D)),
        wsd5=d5(full["L0_ffn_shared_down"].T.reshape(2, I_, D)),
        idm=np.eye(128, dtype=ml_dtypes.bfloat16),
        ltm=np.triu(np.ones((128, 128), np.float32), 1).astype(ml_dtypes.bfloat16),
        eoff=np.ascontiguousarray(np.broadcast_to(
            (np.arange(E_) * np.float32(g.get("CAP", T // 2))).astype(np.float32),
            (128, E_))),
        amask=mask,
    )
    return im


def kernel(**inputs):
    g = FULL_GEO
    D, T = g["D"], g["T"]
    NT = T // 128
    nc = build_nc(g)
    shared = None
    in_maps = []
    for c in range(8):
        im = make_inputs(c, inputs, g)
        in_maps.append(im)
    res = run_bass_kernel_spmd(nc, in_maps, core_ids=list(range(8)))
    out = np.zeros((4, 4096, D), np.float32)
    for c in range(8):
        b, r = c // 2, c % 2
        y = res.results[c]["y_out"]
        out[b].reshape(2 * NT, 128, D)[r::2] = y
    return out
